# version 21
# baseline (speedup 1.0000x reference)
import itertools
import numpy as np
import ml_dtypes
from contextlib import ExitStack

import concourse.bass as bass
import concourse.mybir as mybir
from concourse.bass_utils import run_bass_kernel_spmd

F32 = mybir.dt.float32
BF16 = mybir.dt.bfloat16
F16 = mybir.dt.float16
AF = mybir.ActivationFunctionType
ALU = mybir.AluOpType
AX = mybir.AxisListType
BFNP = ml_dtypes.bfloat16

T = 2048
D = 1024
NT = T // 128
NC = D // 128
NH = 16
HD = 64
NHP = 8
WM = 2 * T - 1
LNEPS = 1e-5

ENGINES = ("sync", "scalar", "vector", "tensor", "gpsimd")


class Prog:

    def __init__(self, nc, ctx):
        self.nc, self.ctx = nc, ctx
        self.pre = ""
        self.actx = ctx
        self.sem = {e: ctx.enter_context(nc.semaphore("S_" + e)) for e in ENGINES}
        self.cnt = {e: 0 for e in ENGINES}
        self.dsem = {}
        self.dcnt = {}
        self.writer = {}
        self.readers = {}
        self.waited = {e: {} for e in ENGINES}
        self.recs = {e: [] for e in ENGINES}

    def chan(self, name):
        if name not in self.dsem:
            self.dsem[name] = self.ctx.enter_context(self.nc.semaphore("D_" + name))
            self.dcnt[name] = 0
        return name

    def _need(self, eng, ins, outs):
        need = []
        for k in ins:
            w = self.writer.get(k)
            if w:
                need.append(w)
        for k in outs:
            w = self.writer.get(k)
            if w:
                need.append(w)
            need += self.readers.get(k, ())
        best = {}
        for typ, a, c in need:
            if best.get((typ, a), -1) < c:
                best[(typ, a)] = c
        waits = []
        wt = self.waited[eng]
        for (typ, a), c in sorted(best.items()):
            if typ == "d":
                c = self.dcnt[a]
            if wt.get((typ, a), 0) >= c:
                continue
            wt[(typ, a)] = c
            waits.append((typ, a, c))
        return waits

    def _book(self, me, ins, outs):
        for k in outs:
            self.writer[k] = me
            self.readers[k] = []
        for k in ins:
            self.readers.setdefault(k, []).append(me)

    def op(self, eng, fn, ins=(), outs=()):
        waits = self._need(eng, ins, outs)
        self.cnt[eng] += 1
        self._book(("e", eng, self.cnt[eng]), ins, outs)
        self.recs[eng].append((waits, fn, None))

    def dma(self, eng, ch, fn, ins=(), outs=()):
        ch = self.chan(ch)
        waits = self._need(eng, ins, outs)
        self.dcnt[ch] += 1
        self._book(("d", ch, self.dcnt[ch]), ins, outs)
        self.recs[eng].append((waits, fn, ch))

    def barrier(self):
        snap_e = dict(self.cnt)
        snap_d = dict(self.dcnt)
        for e in ENGINES:
            waits = []
            wt = self.waited[e]
            for e2 in ENGINES:
                c = snap_e[e2]
                if e2 != e and c > 0 and wt.get(("e", e2), 0) < c:
                    wt[("e", e2)] = c
                    waits.append(("e", e2, c))
            for ch, c in snap_d.items():
                if c > 0 and wt.get(("d", ch), 0) < c:
                    wt[("d", ch)] = c
                    waits.append(("d", ch, c))
            if waits:
                self.recs[e].append((waits, None, None))
        self.writer = {}
        self.readers = {}

    def emit(self):
        block = self.ctx.enter_context(self.nc.Block())
        for eng in ENGINES:
            recs = self.recs[eng]

            def mk(recs_, eng_):
                def body(e):
                    for waits, fn, ch in recs_:
                        for typ, a, c in waits:
                            if typ == "e":
                                e.wait_ge(self.sem[a], c)
                            else:
                                e.wait_ge(self.dsem[a], 16 * c)
                        if fn is None:
                            continue
                        inst = fn(e)
                        if ch is None:
                            inst.then_inc(self.sem[eng_], 1)
                        else:
                            inst.then_inc(self.dsem[ch], 16)
                return body

            getattr(block, eng)(mk(recs, eng))

    def sb(self, name, shape, dtype):
        return self.actx.enter_context(
            self.nc.sbuf_tensor(self.pre + name, shape, dtype))

    def ps(self, name, shape, dtype=F32):
        return self.actx.enter_context(
            self.nc.psum_tensor(self.pre + name, shape, dtype))



def ln_stats_and_u(p, xt_ap, xkey, sq_scr, sqkey, stats, stkey, col, u_out, ukey,
                   extra_u=None, rstd_via="act"):
    ncol = stats.shape[1] // 4
    s1 = stats[:, col:col + 1]
    rstd = stats[:, ncol + col:ncol + col + 1]
    negmr = stats[:, 2 * ncol + col:2 * ncol + col + 1]
    ssq = stats[:, 3 * ncol + col:3 * ncol + col + 1]
    kst = stkey + str(col) + "_"
    p.op("vector", lambda e: e.reduce_sum(s1, xt_ap, AX.X), ins=[xkey], outs=[kst + "s1"])
    p.op("vector", lambda e: e.tensor_scalar_mul(s1, s1, -1.0 / D),
         ins=[kst + "s1"], outs=[kst + "s1"])
    p.op("scalar", lambda e: e.activation(out=sq_scr, in_=xt_ap, func=AF.Square,
                                          bias=s1, accum_out=ssq),
         ins=[xkey, kst + "s1"], outs=[sqkey, kst + "ssq"])
    if rstd_via == "act":
        p.op("scalar", lambda e: e.activation(out=rstd, in_=ssq, func=AF.Ln,
                                              scale=np.float32(1.0 / D),
                                              bias=np.float32(LNEPS)),
             ins=[kst + "ssq"], outs=[kst + "rstd"])
        p.op("scalar", lambda e: e.activation(out=rstd, in_=rstd, func=AF.Exp,
                                              scale=np.float32(-0.5)),
             ins=[kst + "rstd"], outs=[kst + "rstd"])
    else:
        p.op("vector", lambda e: e.tensor_scalar(rstd, ssq, 1.0 / D, LNEPS,
                                                 ALU.mult, ALU.add),
             ins=[kst + "ssq"], outs=[kst + "rstd"])
        p.op("gpsimd", lambda e: e.tensor_tensor(rstd, rstd, p.mh[:], ALU.pow),
             ins=[kst + "rstd", "mh"], outs=[kst + "rstd"])
    p.op("vector", lambda e: e.tensor_mul(negmr, s1, rstd),
         ins=[kst + "s1", kst + "rstd"], outs=[kst + "negmr"])
    p.op("scalar", lambda e: e.activation(out=u_out, in_=xt_ap, func=AF.Identity,
                                          scale=rstd, bias=negmr),
         ins=[xkey, kst + "rstd", kst + "negmr"], outs=[ukey])
    if extra_u is not None:
        ap2, key2 = extra_u
        p.op("scalar", lambda e: e.activation(out=ap2, in_=xt_ap, func=AF.Identity,
                                              scale=rstd, bias=negmr),
             ins=[xkey, kst + "rstd", kst + "negmr"], outs=[key2])


def transpose_tile_to_fm(p, u_src_fn, ukey, t, fm_sb, fmkey, ptr, eye, tcols=T):
    for c in range(NC):
        pk = "ptr" + str(c % 2)
        p.op("tensor",
             lambda e, c=c: e.transpose(ptr[c % 2][:], u_src_fn(c), eye[:]),
             ins=[ukey, "eye"], outs=[pk])
        p.op("vector",
             lambda e, c=c: e.tensor_copy(
                 fm_sb[:, c * tcols + t * 128: c * tcols + (t + 1) * 128],
                 ptr[c % 2][:]),
             ins=[pk], outs=[f"{fmkey}:{c}:{t}"])


def fm_keys(fmkey, ncs=NC, nts=NT):
    return [f"{fmkey}:{c}:{t}" for c in range(ncs) for t in range(nts)]



def emit_quant_fp4(p, x_ap, xkey, out_ap, outkey, scr, kb, nblk, blk=16):
    if isinstance(xkey, str):
        xkey = [xkey]
    W = nblk * blk
    bm, y, u, a, b = scr["bm"], scr["y"], scr["u"], scr["a"], scr["b"]
    xg = x_ap.rearrange("p (n i) -> p n i", i=blk)
    p.op("vector", lambda e: e.tensor_reduce(bm[:, 0:nblk].unsqueeze(2), xg, AX.X,
                                             ALU.max, apply_absolute_value=True),
         ins=xkey, outs=[kb + "bm"])
    p.op("vector", lambda e: e.tensor_scalar(bm[:, nblk:2 * nblk], bm[:, 0:nblk],
                                             1.0 / 6.0, 1e-12, ALU.mult, ALU.max),
         ins=[kb + "bm"], outs=[kb + "sc"])
    p.op("vector", lambda e: e.reciprocal(bm[:, 2 * nblk:3 * nblk],
                                          bm[:, nblk:2 * nblk]),
         ins=[kb + "sc"], outs=[kb + "rs"])
    rb = bm[:, 2 * nblk:3 * nblk].unsqueeze(2).broadcast_to([128, nblk, blk])
    scb = bm[:, nblk:2 * nblk].unsqueeze(2).broadcast_to([128, nblk, blk])
    yg = y[:, 0:W].rearrange("p (n i) -> p n i", i=blk)
    p.op("vector", lambda e: e.tensor_tensor(yg, xg, rb, ALU.mult),
         ins=xkey + [kb + "rs"], outs=[kb + "y"])
    p.op("vector", lambda e: e.scalar_tensor_tensor(
        out=u[:, 0:W], in0=y[:, 0:W], scalar=-1.0, in1=y[:, 0:W],
        op0=ALU.mult, op1=ALU.max),
         ins=[kb + "y"], outs=[kb + "u"])
    p.op("vector", lambda e: e.tensor_single_scalar(a[:, 0:W], u[:, 0:W], 0.25,
                                                    ALU.is_gt),
         ins=[kb + "u"], outs=[kb + "a"])
    for th in (0.75, 1.25, 1.75):
        p.op("vector", lambda e, th=th: e.scalar_tensor_tensor(
            out=a[:, 0:W], in0=u[:, 0:W], scalar=th, in1=a[:, 0:W],
            op0=ALU.is_gt, op1=ALU.add), ins=[kb + "u", kb + "a"], outs=[kb + "a"])
    p.op("vector", lambda e: e.tensor_single_scalar(b[:, 0:W], u[:, 0:W], 2.5,
                                                    ALU.is_gt),
         ins=[kb + "u"], outs=[kb + "b"])
    p.op("vector", lambda e: e.scalar_tensor_tensor(
        out=b[:, 0:W], in0=u[:, 0:W], scalar=3.5, in1=b[:, 0:W],
        op0=ALU.is_gt, op1=ALU.add), ins=[kb + "u", kb + "b"], outs=[kb + "b"])
    p.op("vector", lambda e: e.tensor_single_scalar(u[:, 0:W], u[:, 0:W], 5.0,
                                                    ALU.is_gt),
         ins=[kb + "u"], outs=[kb + "cc"])
    p.op("vector", lambda e: e.scalar_tensor_tensor(
        out=b[:, 0:W], in0=u[:, 0:W], scalar=2.0, in1=b[:, 0:W],
        op0=ALU.mult, op1=ALU.add), ins=[kb + "cc", kb + "b"], outs=[kb + "b"])
    p.op("vector", lambda e: e.scalar_tensor_tensor(
        out=b[:, 0:W], in0=a[:, 0:W], scalar=0.5, in1=b[:, 0:W],
        op0=ALU.mult, op1=ALU.add), ins=[kb + "a", kb + "b"], outs=[kb + "b"])
    bg = b[:, 0:W].rearrange("p (n i) -> p n i", i=blk)
    p.op("vector", lambda e: e.tensor_tensor(bg, bg, scb, ALU.mult),
         ins=[kb + "b", kb + "sc"], outs=[kb + "b"])
    p.op("vector", lambda e: e.tensor_scalar(u[:, 0:W], y[:, 0:W], 0.0, 2.0,
                                             ALU.is_gt, ALU.mult),
         ins=[kb + "y"], outs=[kb + "ssel"])
    p.op("vector", lambda e: e.tensor_mul(a[:, 0:W], b[:, 0:W], u[:, 0:W]),
         ins=[kb + "b", kb + "ssel"], outs=[kb + "d2"])
    p.op("vector", lambda e: e.tensor_sub(out_ap, a[:, 0:W], b[:, 0:W]),
         ins=[kb + "d2", kb + "b"], outs=[outkey])


def emit_quant_fp6(p, x_ap, xkey, out_ap, outkey, scr, kb, nblk, blk=32):
    if isinstance(xkey, str):
        xkey = [xkey]
    W = nblk * blk
    bm, y, u, a, b, s_t = scr["bm"], scr["y"], scr["u"], scr["a"], scr["b"], scr["s"]
    xg = x_ap.rearrange("p (n i) -> p n i", i=blk)
    p.op("vector", lambda e: e.tensor_reduce(bm[:, 0:nblk].unsqueeze(2), xg, AX.X,
                                             ALU.max, apply_absolute_value=True),
         ins=xkey, outs=[kb + "bm"])
    p.op("vector", lambda e: e.tensor_scalar(bm[:, nblk:2 * nblk], bm[:, 0:nblk],
                                             1.0 / 7.5, 1e-30, ALU.mult, ALU.max),
         ins=[kb + "bm"], outs=[kb + "sc"])
    p.op("vector", lambda e: e.reciprocal(bm[:, 2 * nblk:3 * nblk],
                                          bm[:, nblk:2 * nblk]),
         ins=[kb + "sc"], outs=[kb + "rs"])
    rb = bm[:, 2 * nblk:3 * nblk].unsqueeze(2).broadcast_to([128, nblk, blk])
    scb = bm[:, nblk:2 * nblk].unsqueeze(2).broadcast_to([128, nblk, blk])
    yg = y[:, 0:W].rearrange("p (n i) -> p n i", i=blk)
    p.op("vector", lambda e: e.tensor_tensor(yg, xg, rb, ALU.mult),
         ins=xkey + [kb + "rs"], outs=[kb + "y"])
    p.op("vector", lambda e: e.scalar_tensor_tensor(
        out=u[:, 0:W], in0=y[:, 0:W], scalar=-1.0, in1=y[:, 0:W],
        op0=ALU.mult, op1=ALU.max),
         ins=[kb + "y"], outs=[kb + "u"])
    p.op("vector", lambda e: e.tensor_single_scalar(a[:, 0:W], u[:, 0:W], 2.0,
                                                    ALU.is_gt),
         ins=[kb + "u"], outs=[kb + "c1"])
    p.op("vector", lambda e: e.tensor_single_scalar(b[:, 0:W], u[:, 0:W], 3.875,
                                                    ALU.is_gt),
         ins=[kb + "u"], outs=[kb + "c2"])
    p.op("vector", lambda e: e.tensor_scalar(s_t[:, 0:W], a[:, 0:W], 0.125, 0.125,
                                             ALU.mult, ALU.add),
         ins=[kb + "c1"], outs=[kb + "s"])
    p.op("vector", lambda e: e.scalar_tensor_tensor(
        out=s_t[:, 0:W], in0=b[:, 0:W], scalar=0.25, in1=s_t[:, 0:W],
        op0=ALU.mult, op1=ALU.add), ins=[kb + "c2", kb + "s"], outs=[kb + "s"])
    p.op("vector", lambda e: e.tensor_scalar(a[:, 0:W], a[:, 0:W], -4.0, 8.0,
                                             ALU.mult, ALU.add),
         ins=[kb + "c1"], outs=[kb + "t"])
    p.op("vector", lambda e: e.scalar_tensor_tensor(
        out=a[:, 0:W], in0=b[:, 0:W], scalar=-2.0, in1=a[:, 0:W],
        op0=ALU.mult, op1=ALU.add), ins=[kb + "c2", kb + "t"], outs=[kb + "t"])
    p.op("vector", lambda e: e.tensor_mul(u[:, 0:W], u[:, 0:W], a[:, 0:W]),
         ins=[kb + "u", kb + "t"], outs=[kb + "z"])
    p.op("vector", lambda e: e.tensor_single_scalar(a[:, 0:W], u[:, 0:W], 7.5,
                                                    ALU.is_gt),
         ins=[kb + "z"], outs=[kb + "cs"])
    p.op("vector", lambda e: e.tensor_scalar_mul(b[:, 0:W], a[:, 0:W], 8.0),
         ins=[kb + "cs"], outs=[kb + "r"])
    p.op("vector", lambda e: e.scalar_tensor_tensor(
        out=u[:, 0:W], in0=a[:, 0:W], scalar=-8.0, in1=u[:, 0:W],
        op0=ALU.mult, op1=ALU.add), ins=[kb + "cs", kb + "z"], outs=[kb + "z"])
    for w in (4.0, 2.0, 1.0, 1.0):
        p.op("vector", lambda e, w=w: e.tensor_single_scalar(a[:, 0:W], u[:, 0:W],
                                                             w - 0.5, ALU.is_gt),
             ins=[kb + "z"], outs=[kb + "cs"])
        p.op("vector", lambda e, w=w: e.scalar_tensor_tensor(
            out=b[:, 0:W], in0=a[:, 0:W], scalar=w, in1=b[:, 0:W],
            op0=ALU.mult, op1=ALU.add), ins=[kb + "cs", kb + "r"], outs=[kb + "r"])
        p.op("vector", lambda e, w=w: e.scalar_tensor_tensor(
            out=u[:, 0:W], in0=a[:, 0:W], scalar=-w, in1=u[:, 0:W],
            op0=ALU.mult, op1=ALU.add), ins=[kb + "cs", kb + "z"], outs=[kb + "z"])
    p.op("vector", lambda e: e.tensor_mul(b[:, 0:W], b[:, 0:W], s_t[:, 0:W]),
         ins=[kb + "r", kb + "s"], outs=[kb + "q"])
    bg = b[:, 0:W].rearrange("p (n i) -> p n i", i=blk)
    p.op("vector", lambda e: e.tensor_tensor(bg, bg, scb, ALU.mult),
         ins=[kb + "q", kb + "sc"], outs=[kb + "q"])
    p.op("vector", lambda e: e.tensor_scalar(u[:, 0:W], y[:, 0:W], 0.0, 2.0,
                                             ALU.is_gt, ALU.mult),
         ins=[kb + "y"], outs=[kb + "ssel"])
    p.op("vector", lambda e: e.tensor_mul(a[:, 0:W], b[:, 0:W], u[:, 0:W]),
         ins=[kb + "q", kb + "ssel"], outs=[kb + "d2"])
    p.op("vector", lambda e: e.tensor_sub(out_ap, a[:, 0:W], b[:, 0:W]),
         ins=[kb + "d2", kb + "q"], outs=[outkey])




def _pool_quant_pre(p, x_ap, xkey, scr, kb, nblk, blk, qmax, eps):
    if isinstance(xkey, str):
        xkey = [xkey]
    W = nblk * blk
    bm, y, u = scr["bm"], scr["y"], scr["u"]
    xg = x_ap.rearrange("p (n i) -> p n i", i=blk)
    p.op("vector", lambda e: e.tensor_reduce(bm[:, 0:nblk].unsqueeze(2), xg, AX.X,
                                             ALU.max, apply_absolute_value=True),
         ins=xkey, outs=[kb + "bm"])
    p.op("vector", lambda e: e.tensor_scalar(bm[:, nblk:2 * nblk], bm[:, 0:nblk],
                                             1.0 / qmax, eps, ALU.mult, ALU.max),
         ins=[kb + "bm"], outs=[kb + "sc"])
    p.op("vector", lambda e: e.reciprocal(bm[:, 2 * nblk:3 * nblk],
                                          bm[:, nblk:2 * nblk]),
         ins=[kb + "sc"], outs=[kb + "rs"])
    rb = bm[:, 2 * nblk:3 * nblk].unsqueeze(2).broadcast_to([128, nblk, blk])
    yg = y[:, 0:W].rearrange("p (n i) -> p n i", i=blk)
    p.op("gpsimd", lambda e: e.tensor_tensor(yg, xg, rb, ALU.mult),
         ins=xkey + [kb + "rs"], outs=[kb + "y"])
    a = scr["a"]
    p.op("gpsimd", lambda e: e.tensor_single_scalar(a[:, 0:W], y[:, 0:W], 0.0,
                                                    ALU.is_gt),
         ins=[kb + "y"], outs=[kb + "pm"])
    p.op("gpsimd", lambda e: e.tensor_scalar(a[:, 0:W], a[:, 0:W], 2.0, -1.0,
                                             ALU.mult, ALU.add),
         ins=[kb + "pm"], outs=[kb + "sgn"])
    p.op("gpsimd", lambda e: e.tensor_tensor(u[:, 0:W], y[:, 0:W], a[:, 0:W],
                                             ALU.mult),
         ins=[kb + "y", kb + "sgn"], outs=[kb + "u"])


def emit_quant_fp4_pool(p, x_ap, xkey, out_ap, outkey, scr, kb, nblk, blk=16):
    W = nblk * blk
    bm, u, a, b, q = scr["bm"], scr["u"], scr["a"], scr["b"], scr["y"]
    _pool_quant_pre(p, x_ap, xkey, scr, kb, nblk, blk, 6.0, 1e-12)
    p.op("vector", lambda e: e.tensor_scalar_mul(bm[:, 3 * nblk:4 * nblk],
                                                 bm[:, nblk:2 * nblk], 0.5),
         ins=[kb + "sc"], outs=[kb + "sc2"])
    p.op("gpsimd", lambda e: e.tensor_single_scalar(b[:, 0:W], u[:, 0:W], 0.25,
                                                    ALU.is_gt),
         ins=[kb + "u"], outs=[kb + "q2"])
    for th in (0.75, 1.25, 1.75):
        p.op("gpsimd", lambda e, th=th: e.tensor_single_scalar(
            q[:, 0:W], u[:, 0:W], th, ALU.is_gt),
            ins=[kb + "u"], outs=[kb + "c"])
        p.op("gpsimd", lambda e: e.tensor_tensor(b[:, 0:W], b[:, 0:W], q[:, 0:W],
                                                 ALU.add),
             ins=[kb + "c", kb + "q2"], outs=[kb + "q2"])
    for th, wgt in ((2.5, 2.0), (3.5, 2.0), (5.0, 4.0)):
        p.op("gpsimd", lambda e, th=th, wgt=wgt: e.tensor_scalar(
            q[:, 0:W], u[:, 0:W], th, wgt, ALU.is_gt, ALU.mult),
            ins=[kb + "u"], outs=[kb + "c"])
        p.op("gpsimd", lambda e: e.tensor_tensor(b[:, 0:W], b[:, 0:W], q[:, 0:W],
                                                 ALU.add),
             ins=[kb + "c", kb + "q2"], outs=[kb + "q2"])
    scb = bm[:, 3 * nblk:4 * nblk].unsqueeze(2).broadcast_to([128, nblk, blk])
    bg = b[:, 0:W].rearrange("p (n i) -> p n i", i=blk)
    p.op("gpsimd", lambda e: e.tensor_tensor(bg, bg, scb, ALU.mult),
         ins=[kb + "q2", kb + "sc2"], outs=[kb + "d1"])
    p.op("gpsimd", lambda e: e.tensor_tensor(out_ap, b[:, 0:W], a[:, 0:W],
                                             ALU.mult),
         ins=[kb + "d1", kb + "sgn"], outs=[outkey])


def emit_quant_fp6_pool(p, x_ap, xkey, out_ap, outkey, scr, kb, nblk, blk=32):
    W = nblk * blk
    bm, u, a, b, q, s_t = (scr["bm"], scr["u"], scr["a"], scr["b"], scr["y"],
                           scr["s"])
    _pool_quant_pre(p, x_ap, xkey, scr, kb, nblk, blk, 7.5, 1e-30)
    p.op("gpsimd", lambda e: e.tensor_single_scalar(q[:, 0:W], u[:, 0:W], 2.0,
                                                    ALU.is_gt),
         ins=[kb + "u"], outs=[kb + "c1"])
    p.op("gpsimd", lambda e: e.tensor_scalar(b[:, 0:W], q[:, 0:W], -4.0, 8.0,
                                             ALU.mult, ALU.add),
         ins=[kb + "c1"], outs=[kb + "t"])
    p.op("gpsimd", lambda e: e.tensor_scalar(s_t[:, 0:W], q[:, 0:W], 0.125, 0.125,
                                             ALU.mult, ALU.add),
         ins=[kb + "c1"], outs=[kb + "s"])
    p.op("gpsimd", lambda e: e.tensor_single_scalar(q[:, 0:W], u[:, 0:W], 3.875,
                                                    ALU.is_gt),
         ins=[kb + "u"], outs=[kb + "c2"])
    p.op("gpsimd", lambda e: e.tensor_scalar(q[:, 0:W], q[:, 0:W], 2.0, 0.0,
                                             ALU.mult, ALU.add),
         ins=[kb + "c2"], outs=[kb + "c22"])
    p.op("gpsimd", lambda e: e.tensor_tensor(b[:, 0:W], b[:, 0:W], q[:, 0:W],
                                             ALU.subtract),
         ins=[kb + "c22", kb + "t"], outs=[kb + "t"])
    p.op("gpsimd", lambda e: e.tensor_scalar(q[:, 0:W], q[:, 0:W], 0.125, 0.0,
                                             ALU.mult, ALU.add),
         ins=[kb + "c22"], outs=[kb + "s2"])
    p.op("gpsimd", lambda e: e.tensor_tensor(s_t[:, 0:W], s_t[:, 0:W], q[:, 0:W],
                                             ALU.add),
         ins=[kb + "s2", kb + "s"], outs=[kb + "s"])
    p.op("gpsimd", lambda e: e.tensor_tensor(u[:, 0:W], u[:, 0:W], b[:, 0:W],
                                             ALU.mult),
         ins=[kb + "u", kb + "t"], outs=[kb + "z"])
    p.op("gpsimd", lambda e: e.tensor_single_scalar(q[:, 0:W], u[:, 0:W], 7.5,
                                                    ALU.is_gt),
         ins=[kb + "z"], outs=[kb + "cs"])
    p.op("gpsimd", lambda e: e.tensor_scalar(b[:, 0:W], q[:, 0:W], 8.0, 0.0,
                                             ALU.mult, ALU.add),
         ins=[kb + "cs"], outs=[kb + "r"])
    p.op("gpsimd", lambda e: e.tensor_tensor(u[:, 0:W], u[:, 0:W], b[:, 0:W],
                                             ALU.subtract),
         ins=[kb + "r", kb + "z"], outs=[kb + "z"])
    for w in (4.0, 2.0, 1.0, 1.0):
        p.op("gpsimd", lambda e, w=w: e.tensor_scalar(
            q[:, 0:W], u[:, 0:W], w - 0.5, w, ALU.is_gt, ALU.mult),
            ins=[kb + "z"], outs=[kb + "csw"])
        p.op("gpsimd", lambda e: e.tensor_tensor(b[:, 0:W], b[:, 0:W], q[:, 0:W],
                                                 ALU.add),
             ins=[kb + "csw", kb + "r"], outs=[kb + "r"])
        p.op("gpsimd", lambda e: e.tensor_tensor(u[:, 0:W], u[:, 0:W], q[:, 0:W],
                                                 ALU.subtract),
             ins=[kb + "csw", kb + "z"], outs=[kb + "z"])
    p.op("gpsimd", lambda e: e.tensor_tensor(b[:, 0:W], b[:, 0:W], s_t[:, 0:W],
                                             ALU.mult),
         ins=[kb + "r", kb + "s"], outs=[kb + "qv"])
    scb = bm[:, nblk:2 * nblk].unsqueeze(2).broadcast_to([128, nblk, blk])
    bg = b[:, 0:W].rearrange("p (n i) -> p n i", i=blk)
    p.op("gpsimd", lambda e: e.tensor_tensor(bg, bg, scb, ALU.mult),
         ins=[kb + "qv", kb + "sc"], outs=[kb + "d1"])
    p.op("gpsimd", lambda e: e.tensor_tensor(out_ap, b[:, 0:W], a[:, 0:W],
                                             ALU.mult),
         ins=[kb + "d1", kb + "sgn"], outs=[outkey])


def gen_wprep(p, w_d, stg_d, scr, wch, wqb, qkind, nchunks, width, slab,
              tag="", engine="dve", unit_sel=None):
    emitq = {("dve", 6): emit_quant_fp6, ("dve", 4): emit_quant_fp4,
             ("pool", 6): emit_quant_fp6_pool,
             ("pool", 4): emit_quant_fp4_pool}[(engine, qkind)]
    blk = 32 if qkind == 6 else 16
    units = [(c, s0) for c in range(nchunks) for s0 in range(0, width, slab)]
    if unit_sel is not None:
        units = [u for i, u in enumerate(units) if unit_sel(i)]

    def load(i):
        c, s0 = units[i]
        bi = i % 2
        p.dma("sync", f"{tag}wprep{bi}", lambda e, c=c, s0=s0, bi=bi: e.dma_start(
            out=wch[bi][:, 0:slab],
            in_=w_d.ap()[c * 128:(c + 1) * 128, s0:s0 + slab]),
            outs=[f"{tag}wprep_ch{bi}"])

    load(0)
    for i, (c, s0) in enumerate(units):
        bi = i % 2
        if i + 1 < len(units):
            load(i + 1)
        emitq(p, wch[bi][:, 0:slab], f"{tag}wprep_ch{bi}", wqb[bi][:, 0:slab],
              f"{tag}wprep_qb{bi}", scr, tag + "qW", slab // blk)
        p.dma("gpsimd", f"{tag}wstg{bi}", lambda e, c=c, s0=s0, bi=bi: e.dma_start(
            out=stg_d.ap()[c * 128:(c + 1) * 128, s0:s0 + slab],
            in_=wqb[bi][:, 0:slab]),
            ins=[f"{tag}wprep_qb{bi}"], outs=[f"stg_{stg_d.name}:{tag}{i}"])
        yield


def alloc_prep_bufs(p, slab=1408, nm="pw_"):
    scr = {k: p.sb(nm + k, [128, slab], F32) for k in ("y", "u", "a", "b", "s")}
    scr["bm"] = p.sb(nm + "bm", [128, 3 * 88], F32)
    wch = [p.sb(f"{nm}wch{i}", [128, slab], F32) for i in range(2)]
    wqb = [p.sb(f"{nm}wqb{i}", [128, slab], BF16) for i in range(2)]
    return scr, wch, wqb


_RR_END = object()


def rr_merge(*gens):
    gens = list(gens)
    while gens:
        for g in list(gens):
            v = next(g, _RR_END)
            if v is _RR_END:
                gens.remove(g)
            else:
                yield v


def f1_prep_gens(p, dram, scrW, wchW, wqbW, slab_gu=1024, slab_d=1408):
    return (gen_wprep(p, dram["wg"], dram["stg_g"], scrW, wchW, wqbW, 4, NI,
                      D, slab_gu),
            gen_wprep(p, dram["wu"], dram["stg_u"], scrW, wchW, wqbW, 4, NI,
                      D, slab_gu),
            gen_wprep(p, dram["wd"], dram["stg_dn"], scrW, wchW, wqbW, 4, NC,
                      I_DIM, slab_d))


def f1_prep_split(p, dram, bufsD, bufsP):
    scrD, wchD, wqbD = bufsD
    scrP, wchP, wqbP = bufsP
    pool = itertools.chain(
        gen_wprep(p, dram["wg"], dram["stg_g"], scrP, wchP, wqbP, 4, NI, D,
                  512, tag="fp_", engine="pool", unit_sel=lambda i: i % 3 != 2),
        gen_wprep(p, dram["wu"], dram["stg_u"], scrP, wchP, wqbP, 4, NI, D,
                  512, tag="fp_", engine="pool", unit_sel=lambda i: i % 3 != 2))
    dve = itertools.chain(
        gen_wprep(p, dram["wg"], dram["stg_g"], scrD, wchD, wqbD, 4, NI, D,
                  512, tag="fd_", engine="dve", unit_sel=lambda i: i % 3 == 2),
        gen_wprep(p, dram["wu"], dram["stg_u"], scrD, wchD, wqbD, 4, NI, D,
                  512, tag="fd_", engine="dve", unit_sel=lambda i: i % 3 == 2),
        gen_wprep(p, dram["wd"], dram["stg_dn"], scrD, wchD, wqbD, 4, NC,
                  I_DIM, 352, tag="fd_", engine="dve"))
    return pool, dve


def load_wT_staged(p, stg_d, dst, dstkey, ncols_chunks, chunk_rows):
    W = chunk_rows
    for ec in range(ncols_chunks):
        p.dma("sync", f"wl{ec % 2}", lambda e, ec=ec: e.dma_start(
            out=dst[:, ec * W:(ec + 1) * W],
            in_=stg_d.ap()[:, ec * 128:(ec + 1) * 128], transpose=True),
            ins=[f"stg_{stg_d.name}"], outs=[f"{dstkey}:{ec}"])



def declare_attn_dram(nc, layer, pre, x_d, y_d, eye_d):
    d = {"x": x_d, "y": y_d, "eye": eye_d}
    for nm, shape in (("wqT", [D, D]), ("wkT", [D, D]), ("wvT", [D, D]),
                      ("woT", [D, D]), ("bq_pc", [128, NC]), ("bk_pc", [128, NC]),
                      ("bv_row", [1, D]), ("bo_row", [1, D])):
        d[nm] = nc.dram_tensor(pre + nm, shape, F32, kind="ExternalInput")
    if layer == 0:
        d["dpmix"] = nc.dram_tensor(pre + "dpmix", [1, 64], F32, kind="ExternalInput")
        d["mdiff"] = nc.dram_tensor(pre + "mdiff", [128, WM], F32, kind="ExternalInput")
    else:
        d["gb"] = nc.dram_tensor(pre + "gb", [1, 2 * D], F32, kind="ExternalInput")
    return d


def emit_attn(p, dram, layer, prep=None, prep2=None):
    def prep_step(n=1, site=3):
        g = prep2 if (site == 2 and prep2 is not None) else prep
        if g is not None:
            for _ in range(n):
                next(g, None)

    nc = p.nc
    x_d, y_d, eye_d = dram["x"], dram["y"], dram["eye"]
    wqT_d, wkT_d, wvT_d, woT_d = dram["wqT"], dram["wkT"], dram["wvT"], dram["woT"]
    p.mh = p.sb("mh", [128, 1], F32)
    p.op("gpsimd", lambda e: e.memset(p.mh[:], -0.5), outs=["mh"])
    bq_d, bk_d, bv_d, bo_d = dram["bq_pc"], dram["bk_pc"], dram["bv_row"], dram["bo_row"]
    if layer == 0:
        dpmix_d, mdiff_d = dram["dpmix"], dram["mdiff"]
    else:
        gb_d = dram["gb"]
    wqkv = [p.sb(f"wqkv{i}", [128, 2 * D], BF16) for i in range(2)]
    bq = p.sb("bq", [128, NC], F32)
    bk = p.sb("bk", [128, NC], F32)
    bvr = p.sb("bvr", [1, D], BF16)
    bor = p.sb("bor", [1, D], BF16)
    eye = p.sb("eye", [128, 128], BF16)
    ones = p.sb("ones", [128, 128], BF16)
    ones32 = p.sb("ones32", [1, 128], F32)
    u_fm = p.sb("u_fm", [128, NC * T], BF16)
    attn = p.sb("attn", [128, NHP * T], BF16)
    stats = p.sb("stats", [128, 4 * NT], F32)
    stats2 = p.sb("stats2", [128, 4 * 2], F32)
    stats3 = p.sb("stats3", [128, 4 * 2], F32)
    qp = p.sb("qp", [128, T], BF16)
    kp = p.sb("kp", [128, T], BF16)
    v_all = p.sb("v_all", [128, NT * 1040], BF16)
    ebf = p.sb("ebf", [128, 2 * 1024], BF16)
    msk = p.sb("msk", [128, 2 * 1024 if layer == 0 else 1024], F32)
    sq_scr = msk[:, 0:D]
    dsc = p.sb("dsc", [128, 512], F32)
    dbf = p.sb("dbf", [128, 512], BF16)
    asb = [p.sb(f"asb{i}", [128, 512], F32) for i in range(2)]
    dnb = [p.sb(f"dnb{i}", [128, 512], F32) for i in range(2)]
    sc_mask = ExitStack()
    if layer == 0:
        _oact = p.actx
        p.actx = sc_mask
        mdiff = p.sb("mdiff_s", [128, WM], F32)
        emask = p.sb("emask", [128, 2 * WM], F16)
        etmp = p.sb("etmp", [128, 1024], F32)
        ball = p.sb("ball", [128, 64], F32)
        brow = p.sb("brow", [1, 64], F32)
        mrow = p.sb("mrow", [1, 64], F32)
        p.actx = _oact
    pq = p.ps("pq", [128, 512], F32)
    po0 = p.ps("po0", [128, 512], F32)
    po1 = p.ps("po1", [128, 512], F32)

    p.dma("gpsimd", "c_bv", lambda e: e.dma_start(out=bvr[:], in_=bv_d.ap()),
          outs=["bvr"])
    p.dma("gpsimd", "c_bo", lambda e: e.dma_start(out=bor[:], in_=bo_d.ap()),
          outs=["bor"])
    p.dma("sync", "c_bq", lambda e: e.dma_start(out=bq[:], in_=bq_d.ap()),
          outs=["bq"])
    p.dma("sync", "c_bk", lambda e: e.dma_start(out=bk[:], in_=bk_d.ap()),
          outs=["bk"])
    p.dma("sync", "c_eye", lambda e: e.dma_start(out=eye[:], in_=eye_d.ap()),
          outs=["eye"])
    p.op("vector", lambda e: e.memset(ones[:], 1.0), outs=["ones"])
    p.op("vector", lambda e: e.memset(ones32[:], 1.0), outs=["ones32"])
    if layer == 0:
        p.dma("sync", "c_md", lambda e: e.dma_start(out=mdiff[:], in_=mdiff_d.ap()),
              outs=["mdiff"])
        p.dma("sync", "c_dm", lambda e: e.dma_start(out=mrow[0:1, :],
                                                    in_=dpmix_d.ap()),
              outs=["mrow"])

    if layer == 0:
        p.op("vector", lambda e: e.tensor_scalar(brow[0:1, 0:32], mrow[0:1, 0:32],
                                                 0.01, 0.9999, ALU.max, ALU.min),
             ins=["mrow"], outs=["dpc"])
        p.op("scalar", lambda e: e.activation(out=brow[0:1, 0:32],
                                              in_=brow[0:1, 0:32], func=AF.Ln),
             ins=["dpc"], outs=["lndp"])
        p.op("scalar", lambda e: e.activation(out=mrow[0:1, 32:64],
                                              in_=mrow[0:1, 32:64], func=AF.Exp),
             ins=["mrow"], outs=["mexp"])
        mv = mrow[0:1, 32:64].rearrange("p (h n) -> p h n", n=2)
        p.op("vector", lambda e: e.tensor_reduce(
            mrow[0:1, 16:32].unsqueeze(2), mv, AX.X, ALU.add),
             ins=["mexp"], outs=["msum"])
        p.op("vector", lambda e: e.reciprocal(mrow[0:1, 16:32], mrow[0:1, 16:32]),
             ins=["msum"], outs=["mrec"])
        mrb = mrow[0:1, 16:32].unsqueeze(2).broadcast_to([1, 16, 2])
        p.op("vector", lambda e: e.tensor_tensor(mv, mv, mrb, ALU.mult),
             ins=["mexp", "mrec"], outs=["mnorm"])
        p.op("scalar", lambda e: e.activation(out=brow[0:1, 32:64],
                                              in_=mrow[0:1, 32:64], func=AF.Ln),
             ins=["mnorm"], outs=["lnm"])
        p.op("vector", lambda e: e.tensor_scalar_add(
            brow[0:1, 32:64], brow[0:1, 32:64], float(-np.log(8.0))),
             ins=["lnm"], outs=["lnm"])
        p.op("tensor", lambda e: e.matmul(pq[:, 0:64], ones32[0:1, :],
                                          brow[0:1, :], start=True, stop=True),
             ins=["ones32", "lndp", "lnm"], outs=["pq"])
        p.op("scalar", lambda e: e.activation(out=ball[:], in_=pq[:, 0:64],
                                              func=AF.Identity, bias=np.float32(0.0)),
             ins=["pq"], outs=["ball"])

    outer_actx = p.actx
    with ExitStack() as sc1:
        p.actx = sc1
        ptr = [p.ps(f"ptr{i}", [128, 128], BF16) for i in range(2)]
        xt = [p.sb(f"xt{i}", [128, D], F32) for i in range(2)]
        ubf = [p.sb(f"ubf{i}", [128, D], BF16) for i in range(2)]
        p.actx = outer_actx
        for t in range(NT):
            bi = t % 2
            p.dma("sync", f"x{bi}", lambda e, t=t, bi=bi: e.dma_start(
                out=xt[bi][:], in_=x_d.ap()[t * 128:(t + 1) * 128, :]),
                outs=[f"xt{bi}"])
            ln_stats_and_u(p, xt[bi][:], f"xt{bi}", sq_scr[:], "sq_scr",
                           stats, "st", t, ubf[bi][:], f"ubf{bi}")
            transpose_tile_to_fm(p, lambda c, bi=bi: ubf[bi][:, c * 128:(c + 1) * 128],
                                 f"ubf{bi}", t, u_fm, "ufm", ptr, eye)
        p.barrier()
    psW = [p.ps("psW0", [128, 1024], F32), p.ps("psW1", [128, 1024], F32)]
    pq2 = p.ps("pq2", [128, 512], F32)
    ufm_all = fm_keys("ufm")

    with ExitStack() as scv:
        _oact = p.actx
        p.actx = scv
        wv_all = p.sb("wv_all", [128, NC * D], BF16)
        p.actx = _oact
        p.dma("gpsimd", "wvl", lambda e: e.dma_start(
            out=wv_all[:].rearrange("p (c f) -> p c f", c=NC),
            in_=wvT_d.ap().rearrange("(c p) f -> p c f", p=128)),
            outs=["wv_all"])
        v3o = v_all[:].rearrange("p (g c) -> p g c", c=130)
        p.op("vector", lambda e: e.memset(v3o[:, :, 64:65], 1.0), outs=["vo"])
        p.op("vector", lambda e: e.memset(v3o[:, :, 129:130], 1.0), outs=["vo"])
        for t in range(NT):
            for half in range(2):
                psel, kpsel = (pq, "pq") if half == 0 else (pq2, "pq2")

                def mmvq(e, t=t, half=half, psel=psel):
                    for c in range(NC):
                        e.matmul(psel[:],
                                 u_fm[:, c * T + t * 128: c * T + (t + 1) * 128],
                                 wv_all[:, c * D + half * 512: c * D + half * 512 + 512],
                                 start=(c == 0), stop=False)
                    return e.matmul(psel[:], ones[0:1, :],
                                    bvr[0:1, half * 512: (half + 1) * 512],
                                    start=False, stop=True)
                p.op("tensor", mmvq, ins=["wv_all", "bvr", "ones"] + ufm_all,
                     outs=[kpsel])
                v3 = v_all[:, t * 1040 + half * 520: t * 1040 + (half + 1) * 520] \
                    .rearrange("p (g c) -> p g c", c=130)
                p3 = psel[:].rearrange("p (g hc) -> p g hc", g=4)
                p.op("vector", lambda e, v3=v3, p3=p3: e.tensor_copy(
                    v3[:, :, 0:64], p3[:, :, 0:64]),
                    ins=[kpsel], outs=[f"va{t}:{half}"])
                p.op("vector", lambda e, v3=v3, p3=p3: e.tensor_copy(
                    v3[:, :, 65:129], p3[:, :, 64:128]),
                    ins=[kpsel], outs=[f"va{t}:{half}"])

    for hp in range(NHP):
        wb = hp % 2
        wkey = f"wqkv{wb}"
        for j, src in enumerate((wqT_d, wkT_d)):
            p.dma("gpsimd", f"wpair{wb}", lambda e, j=j, src=src, wb=wb, hp=hp:
                  e.dma_start(
                      out=wqkv[wb][:, j * D:(j + 1) * D],
                      in_=src.ap().rearrange("(c p) f -> p c f", p=128)
                      [:, :, hp * 128:(hp + 1) * 128]),
                  outs=[wkey + str(j)])
        qkeys = [f"qp{i}" for i in range(4)]
        kkeys = [f"kp{i}" for i in range(4)]
        for j, (bsb, bkey, dst, dks) in enumerate(
                ((bq, "bq", qp, "qp"), (bk, "bk", kp, "kp"))):
            for ts4 in range(4):
                sl = slice(ts4 * 512, (ts4 + 1) * 512)
                psel, kpsel = (pq, "pq") if ts4 % 2 == 0 else (pq2, "pq2")

                def mmq(e, j=j, sl=sl, wb=wb, psel=psel):
                    for c in range(NC):
                        inst = e.matmul(
                            psel[:], wqkv[wb][:, j * D + c * 128: j * D + c * 128 + 128],
                            u_fm[:, c * T:(c + 1) * T][:, sl],
                            start=(c == 0), stop=(c == NC - 1))
                    return inst
                p.op("tensor", mmq, ins=[wkey + str(j)] + ufm_all, outs=[kpsel])
                p.op("scalar", lambda e, sl=sl, bsb=bsb, hp=hp, dst=dst,
                     psel=psel: e.activation(
                    out=dst[:, sl], in_=psel[:], func=AF.Identity,
                    bias=bsb[:, hp:hp + 1]),
                    ins=[kpsel, bkey], outs=[dks + str(ts4)])
        prep_step(3, site=2)

        if layer == 0:
            for h2 in range(2):
                h = 2 * hp + h2
                dst = emask[:, h2 * WM:(h2 + 1) * WM]
                p.op("scalar", lambda e, dst=dst, h=h: e.activation(
                    out=dst, in_=mdiff[:], func=AF.Exp,
                    scale=ball[:, 2 * h: 2 * h + 1],
                    bias=ball[:, 32 + 2 * h: 32 + 2 * h + 1]),
                    ins=["mdiff", "ball"], outs=[f"emA{h2}"])
                for cs in range(0, WM, 1024):
                    cw = min(1024, WM - cs)
                    p.op("scalar", lambda e, h=h, cs=cs, cw=cw: e.activation(
                        out=etmp[:, 0:cw], in_=mdiff[:, cs:cs + cw], func=AF.Exp,
                        scale=ball[:, 2 * h + 1: 2 * h + 2],
                        bias=ball[:, 32 + 2 * h + 1: 32 + 2 * h + 2]),
                        ins=["mdiff", "ball", f"emsk{h2}:{cs}"], outs=["etmp"])
                    p.op("vector", lambda e, dst=dst, cs=cs, cw=cw: e.tensor_add(
                        dst[:, cs:cs + cw], dst[:, cs:cs + cw], etmp[:, 0:cw]),
                        ins=[f"emA{h2}", "etmp"], outs=[f"emask{h2}",
                                                        f"emsk{h2}:{cs}"])

        for qb in range(4):
            qsl = slice(qb * 512, (qb + 1) * 512)

            def scores(kc):
                par = kc % 2
                psx = psW[par]
                kA, kB = f"psW{par}h0", f"psW{par}h1"
                p.op("tensor", lambda e, kc=kc, psx=psx, qsl=qsl: e.matmul(
                    psx[:, 0:512], kp[0:64, kc * 128:(kc + 1) * 128], qp[0:64, qsl],
                    start=True, stop=True, tile_position=(0, 0)),
                    ins=qkeys + kkeys, outs=[kA])
                p.op("tensor", lambda e, kc=kc, psx=psx, qsl=qsl: e.matmul(
                    psx[:, 512:1024], kp[64:128, kc * 128:(kc + 1) * 128],
                    qp[64:128, qsl],
                    start=True, stop=True, tile_position=(64, 0)),
                    ins=qkeys + kkeys, outs=[kB])
                esl = slice(par * 1024, (par + 1) * 1024)
                ek = f"ebfp{par}"
                if layer == 0:
                    c0 = (T - 1) + qb * 512 - kc * 128
                    m3 = msk[:, esl].rearrange("p (h w) -> p h w", h=2)
                    p3 = psx[:].rearrange("p (h w) -> p h w", h=2)
                    e3 = emask[:].rearrange("p (h w) -> p h w", h=2)[:, :, c0:c0 + 512]
                    p.op("vector", lambda e, m3=m3, p3=p3, e3=e3: e.tensor_tensor(
                        m3, p3, e3, ALU.mult),
                        ins=[kA, kB, "emask0", "emask1"], outs=[f"mskp{par}"])
                    p.op("scalar", lambda e, esl=esl: e.activation(
                        out=ebf[:, esl], in_=msk[:, esl], func=AF.Exp),
                        ins=[f"mskp{par}"], outs=[ek])
                else:
                    p.op("scalar", lambda e, psx=psx, esl=esl: e.activation(
                        out=ebf[:, esl], in_=psx[:], func=AF.Exp,
                        scale=np.float32(0.125)),
                        ins=[kA, kB], outs=[ek])

            def pv(kc):
                par = kc % 2
                eo = par * 1024
                vb = kc * 1040 + hp * 130
                vkey = f"va{kc}:{hp // 4}"
                p.op("tensor", lambda e, kc=kc, eo=eo, vb=vb: e.matmul(
                    po0[0:65, :], v_all[:, vb: vb + 65],
                    ebf[:, eo: eo + 512],
                    start=(kc == 0), stop=(kc == NT - 1)),
                    ins=[f"ebfp{par}", vkey, "vo"], outs=["po0"])
                p.op("tensor", lambda e, kc=kc, eo=eo, vb=vb: e.matmul(
                    po1[0:65, :], v_all[:, vb + 65: vb + 130],
                    ebf[:, eo + 512: eo + 1024],
                    start=(kc == 0), stop=(kc == NT - 1)),
                    ins=[f"ebfp{par}", vkey, "vo"], outs=["po1"])

            for kc in range(NT + 1):
                if kc < NT:
                    scores(kc)
                if kc > 0:
                    pv(kc - 1)
            p.op("scalar", lambda e: e.activation(out=dsc[64:65, 0:512],
                                                  in_=po0[64:65, :], func=AF.Ln),
                 ins=["po0"], outs=["dscA"])
            p.op("scalar", lambda e: e.activation(out=dbf[64:65, 0:512],
                                                  in_=dsc[64:65, 0:512], func=AF.Exp,
                                                  scale=np.float32(-1.0)),
                 ins=["dscA"], outs=["dbfA"])
            p.op("scalar", lambda e, qb=qb: e.activation(out=asb[qb % 2][0:64, :],
                                                  in_=po0[0:64, :],
                                                  func=AF.Identity, bias=np.float32(0.0)),
                 ins=["po0"], outs=[f"asbA{qb % 2}"])
            p.op("tensor", lambda e: e.matmul(pq[0:64, :], ones[64:65, 0:64],
                                              dbf[64:65, 0:512],
                                              start=True, stop=True),
                 ins=["dbfA", "ones"], outs=["pq"])
            p.op("scalar", lambda e, qb=qb: e.activation(
                out=dnb[qb % 2][0:64, :], in_=pq[0:64, :],
                func=AF.Identity, bias=np.float32(0.0)),
                ins=["pq"], outs=[f"dnbA{qb % 2}"])
            p.op("vector", lambda e, hp=hp, qsl=qsl, qb=qb: e.tensor_mul(
                attn[0:64, hp * T: (hp + 1) * T][:, qsl], dnb[qb % 2][0:64, :],
                asb[qb % 2][0:64, :]),
                ins=[f"dnbA{qb % 2}", f"asbA{qb % 2}"], outs=[f"attn:{hp}:0:{qb}"])
            p.op("scalar", lambda e: e.activation(out=dsc[0:1, 0:512],
                                                  in_=po1[64:65, :], func=AF.Ln),
                 ins=["po1"], outs=["dscB"])
            p.op("scalar", lambda e: e.activation(out=dbf[0:1, 0:512],
                                                  in_=dsc[0:1, 0:512],
                                                  func=AF.Exp,
                                                  scale=np.float32(-1.0)),
                 ins=["dscB"], outs=["dbfB"])
            p.op("scalar", lambda e, qb=qb: e.activation(out=asb[qb % 2][64:128, :],
                                                  in_=po1[0:64, :],
                                                  func=AF.Identity, bias=np.float32(0.0)),
                 ins=["po1"], outs=[f"asbB{qb % 2}"])
            p.op("tensor", lambda e: e.matmul(pq2[64:128, :], ones[0:1, 0:64],
                                              dbf[0:1, 0:512],
                                              start=True, stop=True),
                 ins=["dbfB", "ones"], outs=["pq2"])
            p.op("scalar", lambda e, qb=qb: e.activation(
                out=dnb[qb % 2][64:128, :], in_=pq2[64:128, :],
                func=AF.Identity, bias=np.float32(0.0)),
                ins=["pq2"], outs=[f"dnbB{qb % 2}"])
            p.op("vector", lambda e, hp=hp, qsl=qsl, qb=qb: e.tensor_mul(
                attn[64:128, hp * T: (hp + 1) * T][:, qsl],
                dnb[qb % 2][64:128, :], asb[qb % 2][64:128, :]),
                ins=[f"dnbB{qb % 2}", f"asbB{qb % 2}"], outs=[f"attn:{hp}:1:{qb}"])
            prep_step(4)

    attn_all = [f"attn:{g}:{h2}:{qb}" for g in range(NHP) for h2 in range(2)
                for qb in range(4)]
    p.barrier()
    sc_mask.close()
    wo = p.sb("wo", [128, NC * D], BF16)
    xtw = [p.sb(f"xtw{i}", [128, D], F32) for i in range(2)]
    p.dma("gpsimd", "c_wo", lambda e: e.dma_start(
        out=wo[:].rearrange("p (c f) -> p c f", c=NC),
        in_=woT_d.ap().rearrange("(c p) f -> p c f", p=128)),
        outs=["wo"])
    if layer == 1:
        gb_b = p.sb("gb_b", [128, 2 * D], F32)
        u32 = [p.sb("u32_s", [128, D], F32)] * 2
        ysb = p.sb("ysb", [128, D], F32)
        p.dma("sync", "c_gb", lambda e: e.dma_start(out=gb_b[0:1, :],
                                                    in_=gb_d.ap()),
              outs=["gbrow"])
        for half in range(2):
            for fs in range(0, D, 512):
                o = half * D + fs
                p.op("tensor", lambda e, o=o: e.matmul(
                    pq[:, 0:512], ones32[0:1, :], gb_b[0:1, o:o + 512],
                    start=True, stop=True),
                    ins=["gbrow", "ones32"], outs=["pq"])
                p.op("scalar", lambda e, o=o: e.activation(
                    out=gb_b[:, o:o + 512], in_=pq[:, 0:512],
                    func=AF.Identity, bias=np.float32(0.0)),
                    ins=["pq"], outs=["gb_b"])
    if "dbg_attn" in dram:
        p.dma("sync", "dbg1", lambda e: e.dma_start(out=dram["dbg_attn"].ap(),
                                                    in_=attn[:]), ins=attn_all,
              outs=["dbgA"])
        p.dma("sync", "dbg2", lambda e: e.dma_start(out=dram["dbg_qp"].ap(),
                                                    in_=qp[:]),
              ins=[f"qp{i}" for i in range(4)], outs=["dbgQ"])
        p.dma("sync", "dbg3", lambda e: e.dma_start(out=dram["dbg_kp"].ap(),
                                                    in_=kp[:]),
              ins=[f"kp{i}" for i in range(4)], outs=["dbgK"])
        p.dma("sync", "dbg4", lambda e: e.dma_start(out=dram["dbg_vaug"].ap(),
                                                    in_=v_all[:, 0:NT * 130]),
              outs=["dbgV"])

    for t in range(NT):
        bi = t % 2
        p.dma("sync", f"x{bi}", lambda e, t=t, bi=bi: e.dma_start(
            out=xtw[bi][:], in_=x_d.ap()[t * 128:(t + 1) * 128, :]),
            outs=[f"xt{bi}"])
        for fs in range(2):
            fsl = slice(fs * 512, (fs + 1) * 512)
            psx, kpsx = (psW[0][:, 0:512], "psW0h0") if fs == 0 else (
                psW[0][:, 512:1024], "psW0h1")

            def mwo(e, t=t, fsl=fsl, psx=psx):
                for g in range(NHP):
                    e.matmul(psx,
                             attn[:, g * T + t * 128: g * T + (t + 1) * 128],
                             wo[:, g * D:(g + 1) * D][:, fsl],
                             start=(g == 0), stop=False)
                return e.matmul(psx, ones[0:1, :], bor[0:1, fsl],
                                start=False, stop=True)
            p.op("tensor", mwo, ins=attn_all + ["wo", "bor", "ones"], outs=[kpsx])
        if layer == 0:
            for fs in range(2):
                fsl = slice(fs * 512, (fs + 1) * 512)
                psx, kpsx = (psW[0][:, 0:512], "psW0h0") if fs == 0 else (
                    psW[0][:, 512:1024], "psW0h1")
                p.op("vector", lambda e, bi=bi, fsl=fsl, psx=psx: e.tensor_add(
                    xtw[bi][:, fsl], psx, xtw[bi][:, fsl]),
                    ins=[kpsx, f"xt{bi}"], outs=[f"xt{bi}"])
        else:
            ln_stats_and_u(p, xtw[bi][:], f"xt{bi}", sq_scr[:], "sq_scr",
                           stats2, "wt", bi, u32[bi][:], "u32s")
            for fs in range(2):
                fsl = slice(fs * 512, (fs + 1) * 512)
                psx, kpsx = (psW[0][:, 0:512], "psW0h0") if fs == 0 else (
                    psW[0][:, 512:1024], "psW0h1")
                p.op("vector", lambda e, bi=bi, fsl=fsl, psx=psx: e.tensor_add(
                    ysb[:, fsl], psx, u32[bi][:, fsl]),
                    ins=[kpsx, "u32s"], outs=["ysb"])
            ln_stats_and_u(p, ysb[:], "ysb", sq_scr[:], "sq_scr",
                           stats3, "yt", bi, ysb[:], "ysb")
            p.op("vector", lambda e: e.tensor_mul(ysb[:], ysb[:], gb_b[:, 0:D]),
                 ins=["ysb", "gb_b"], outs=["ysb"])
            p.op("vector", lambda e: e.tensor_add(ysb[:], ysb[:], gb_b[:, D:2 * D]),
                 ins=["ysb", "gb_b"], outs=["ysb"])
            p.op("vector", lambda e, bi=bi: e.tensor_add(xtw[bi][:], ysb[:],
                                                         xtw[bi][:]),
                 ins=["ysb", f"xt{bi}"], outs=[f"xt{bi}"])
        p.dma("sync", "st", lambda e, t=t, bi=bi: e.dma_start(
            out=y_d.ap()[t * 128:(t + 1) * 128, :], in_=xtw[bi][:]),
            ins=[f"xt{bi}"], outs=[f"y{t}"])


def build_attn(layer):
    nc = bass.Bass("TRN2", target_bir_lowering=False)
    ctx = ExitStack()
    x_d = nc.dram_tensor("x", [T, D], F32, kind="ExternalInput")
    y_d = nc.dram_tensor("y", [T, D], F32, kind="ExternalOutput")
    eye_d = nc.inline_tensor(np.eye(128, dtype=BFNP), name="eye_bf")
    dram = declare_attn_dram(nc, layer, "", x_d, y_d, eye_d)
    import os as _os
    if _os.environ.get("DBG_ATTN") == "1":
        dram["dbg_attn"] = nc.dram_tensor("dbg_attn", [128, NHP * T], BF16,
                                          kind="ExternalOutput")
        dram["dbg_qp"] = nc.dram_tensor("dbg_qp", [128, T], BF16,
                                        kind="ExternalOutput")
        dram["dbg_kp"] = nc.dram_tensor("dbg_kp", [128, T], BF16,
                                        kind="ExternalOutput")
        dram["dbg_vaug"] = nc.dram_tensor("dbg_vaug", [128, NT * 130], BF16,
                                          kind="ExternalOutput")
    with ctx:
        p = Prog(nc, ctx)
        emit_attn(p, dram, layer)
        p.barrier()
        p.emit()
    return nc



def declare_ffn0_dram(nc, pre, x_d, y_d, eye_d):
    return {"x": x_d, "y": y_d, "eye": eye_d,
            "w": nc.dram_tensor(pre + "w", [D, D], F32, kind="ExternalInput"),
            "b_row": nc.dram_tensor(pre + "b_row", [1, D], F32, kind="ExternalInput"),
            "stg": nc.dram_tensor(pre + "stg", [D, D], BF16, kind="Internal")}


def emit_ffn0(p, dram, prep_done=False):
    nc = p.nc
    x_d, y_d, eye_d = dram["x"], dram["y"], dram["eye"]
    b_d = dram["b_row"]
    stg_d = dram["stg"]
    p.mh = p.sb("mh", [128, 1], F32)
    p.op("gpsimd", lambda e: e.memset(p.mh[:], -0.5), outs=["mh"])
    if not prep_done:
        _oact = p.actx
        with ExitStack() as sw:
            p.actx = sw
            scrW, wchW, wqbW = alloc_prep_bufs(p)
            p.actx = _oact
            for _ in gen_wprep(p, dram["w"], stg_d, scrW, wchW, wqbW, 6, NC, D,
                               1024):
                pass
            p.barrier()
    wqT = p.sb("wqT", [128, NC * D], BF16)
    br = p.sb("br", [1, D], BF16)
    eye = p.sb("eye", [128, 128], BF16)
    ones = p.sb("ones", [1, 128], BF16)
    stats = p.sb("stats", [128, 4 * 2], F32)
    xt = [p.sb(f"xt{i}", [128, D], F32) for i in range(2)]
    xtw = [p.sb(f"xtw{i}", [128, D], F32) for i in range(2)]
    u32 = [p.sb(f"u32_{i}", [128, D], F32) for i in range(2)]
    xqb = [p.sb(f"xqb{i}", [128, D], BF16) for i in range(2)]
    xqf = [p.sb(f"xqf{i}", [128, D], BF16) for i in range(4)]
    sq_scr = p.sb("sq_scr", [128, D], F32)
    scr = {k: p.sb("qs_" + k, [128, D], F32) for k in ("y", "u", "a", "b", "s")}
    scr["bm"] = p.sb("qs_bm", [128, 3 * 32], F32)
    ptr = [p.ps(f"ptr{i}", [128, 128], BF16) for i in range(2)]
    pd = [p.ps(f"pd{i}", [128, 512], F32) for i in range(4)]

    p.dma("sync", "c_eye", lambda e: e.dma_start(out=eye[:], in_=eye_d.ap()),
          outs=["eye"])
    p.dma("gpsimd", "c_b", lambda e: e.dma_start(out=br[:], in_=b_d.ap()),
          outs=["br"])
    p.op("vector", lambda e: e.memset(ones[:], 1.0), outs=["ones"])

    load_wT_staged(p, stg_d, wqT, "wqT", NC, D)
    wqT_all = [f"wqT:{ec}" for ec in range(NC)]

    def front(t):
        bi, ri = t % 2, t % 4
        p.dma("sync", f"x{bi}", lambda e, t=t, bi=bi: e.dma_start(
            out=xt[bi][:], in_=x_d.ap()[t * 128:(t + 1) * 128, :]),
            outs=[f"xt{bi}"])
        ln_stats_and_u(p, xt[bi][:], f"xt{bi}", sq_scr[:], "sq_scr",
                       stats, "st", bi, u32[bi][:], "u32s", rstd_via="pool")
        emit_quant_fp6(p, u32[bi][:], "u32s", xqb[bi][:], "xqbs",
                       scr, "qs", 32)
        for ec in range(NC):
            pk = "ptr" + str(ec % 2)
            p.op("tensor", lambda e, ec=ec, bi=bi: e.transpose(
                ptr[ec % 2][:], xqb[bi][:, ec * 128:(ec + 1) * 128], eye[:]),
                ins=["xqbs", "eye"], outs=[pk])
            p.op("scalar", lambda e, ec=ec, ri=ri: e.activation(
                out=xqf[ri][:, ec * 128:(ec + 1) * 128],
                in_=ptr[ec % 2][:], func=AF.Identity, bias=np.float32(0.0)),
                ins=[pk], outs=[f"xqf{ri}_{ec}"])

    def back(t):
        bi, ri = t % 2, t % 4
        xqf_all = [f"xqf{ri}_{ec}" for ec in range(NC)]
        p.dma("gpsimd", f"xw{bi}", lambda e, t=t, bi=bi: e.dma_start(
            out=xtw[bi][:], in_=x_d.ap()[t * 128:(t + 1) * 128, :]),
            outs=[f"xtw{bi}"])
        for fs in range(2):
            fsl = slice(fs * 512, (fs + 1) * 512)
            psx, kpsx = pd[(t % 2) * 2 + fs], f"pd{(t % 2) * 2 + fs}"

            def mmd(e, fsl=fsl, psx=psx, ri=ri):
                for ec in range(NC):
                    e.matmul(psx[:], xqf[ri][:, ec * 128:(ec + 1) * 128],
                             wqT[:, ec * D:(ec + 1) * D][:, fsl],
                             start=(ec == 0), stop=False)
                return e.matmul(psx[:], ones[0:1, :], br[0:1, fsl],
                                start=False, stop=True)
            p.op("tensor", mmd, ins=xqf_all + wqT_all + ["br", "ones"], outs=[kpsx])
            p.op("vector", lambda e, bi=bi, fsl=fsl, psx=psx: e.tensor_add(
                xtw[bi][:, fsl], psx[:], xtw[bi][:, fsl]),
                ins=[kpsx, f"xtw{bi}"], outs=[f"xtw{bi}"])
        p.dma("sync", "st", lambda e, t=t, bi=bi: e.dma_start(
            out=y_d.ap()[t * 128:(t + 1) * 128, :], in_=xtw[bi][:]),
            ins=[f"xtw{bi}"], outs=[f"y{t}"])

    front(0)
    front(1)
    for t in range(NT):
        back(t)
        if t + 2 < NT:
            front(t + 2)


def build_ffn0():
    nc = bass.Bass("TRN2", target_bir_lowering=False)
    ctx = ExitStack()
    x_d = nc.dram_tensor("x", [T, D], F32, kind="ExternalInput")
    y_d = nc.dram_tensor("y", [T, D], F32, kind="ExternalOutput")
    eye_d = nc.inline_tensor(np.eye(128, dtype=BFNP), name="eye_bf")
    dram = declare_ffn0_dram(nc, "", x_d, y_d, eye_d)
    with ctx:
        p = Prog(nc, ctx)
        emit_ffn0(p, dram)
        p.barrier()
        p.emit()
    return nc



NI = 22
I_DIM = 2816


def declare_ffn1_dram(nc, pre, x_d, y_d, eye_d):
    return {"x": x_d, "y": y_d, "eye": eye_d,
            "wg": nc.dram_tensor(pre + "wg", [I_DIM, D], F32, kind="ExternalInput"),
            "wu": nc.dram_tensor(pre + "wu", [I_DIM, D], F32, kind="ExternalInput"),
            "wd": nc.dram_tensor(pre + "wd", [D, I_DIM], F32, kind="ExternalInput"),
            "stg_g": nc.dram_tensor(pre + "stg_g", [I_DIM, D], BF16, kind="Internal"),
            "stg_u": nc.dram_tensor(pre + "stg_u", [I_DIM, D], BF16, kind="Internal"),
            "stg_dn": nc.dram_tensor(pre + "stg_dn", [D, I_DIM], BF16, kind="Internal")}


def emit_ffn1(p, dram, prep_done=False):
    nc = p.nc
    x_d, y_d, eye_d = dram["x"], dram["y"], dram["eye"]
    stg_g, stg_u, stg_dn = dram["stg_g"], dram["stg_u"], dram["stg_dn"]
    HF = I_DIM // 2
    p.mh = p.sb("mh", [128, 1], F32)
    p.op("gpsimd", lambda e: e.memset(p.mh[:], -0.5), outs=["mh"])
    if not prep_done:
        _oact = p.actx
        with ExitStack() as sw:
            p.actx = sw
            scrW, wchW, wqbW = alloc_prep_bufs(p)
            p.actx = _oact
            for g in f1_prep_gens(p, dram, scrW, wchW, wqbW):
                for _ in g:
                    pass
            p.barrier()
    wgT = p.sb("wgT", [128, NC * I_DIM], BF16)
    wuT = p.sb("wuT", [128, NC * I_DIM], BF16)
    wdT = p.sb("wdT", [128, NI * D], BF16)
    eye = p.sb("eye", [128, 128], BF16)
    stats = p.sb("stats", [128, 4 * 2], F32)
    xt = [p.sb(f"xt{i}", [128, D], F32) for i in range(2)]
    u32 = [p.sb("u32_s", [128, D], F32)] * 2
    xqb = [p.sb(f"xqb{i}", [128, D], BF16) for i in range(2)]
    xqf = [p.sb(f"xqf{i}", [128, D], BF16) for i in range(2)]
    gs = [p.sb(f"gs{i}", [128, I_DIM], F32) for i in range(2)]
    upe = p.sb("upe", [128, 2 * 512], F32)
    hq = [p.sb("hq_s", [128, I_DIM], BF16)] * 2
    hqT = p.sb("hqT", [128, I_DIM], BF16)
    scr = {k: p.sb("qs_" + k, [128, 1024], F32) for k in ("y", "u", "a", "b")}
    scr["bm"] = p.sb("qs_bm", [128, 3 * 64], F32)
    ptr = [p.ps(f"ptr{i}", [128, 128], BF16) for i in range(2)]
    pg = [p.ps("pg0", [128, 512], F32), p.ps("pg1", [128, 512], F32)]
    pu = [p.ps("pu0", [128, 512], F32), p.ps("pu1", [128, 512], F32)]
    pd0 = p.ps("pd0", [128, 512], F32)
    pd1 = p.ps("pd1", [128, 512], F32)

    p.dma("sync", "c_eye", lambda e: e.dma_start(out=eye[:], in_=eye_d.ap()),
          outs=["eye"])
    load_wT_staged(p, stg_g, wgT, "wgT", NC, I_DIM)
    load_wT_staged(p, stg_u, wuT, "wuT", NC, I_DIM)
    load_wT_staged(p, stg_dn, wdT, "wdT", NI, D)
    wgT_all = [f"wgT:{ec}" for ec in range(NC)]
    wuT_all = [f"wuT:{ec}" for ec in range(NC)]
    wdT_all = [f"wdT:{ic}" for ic in range(NI)]

    ISL = [(k * 512, min((k + 1) * 512, I_DIM)) for k in range(6)]

    def front(t):
        bi = t % 2
        p.dma("sync", f"x{bi}", lambda e, t=t, bi=bi: e.dma_start(
            out=xt[bi][:], in_=x_d.ap()[t * 128:(t + 1) * 128, :]),
            outs=[f"xt{bi}"])
        ln_stats_and_u(p, xt[bi][:], f"xt{bi}", u32[bi][:], "u32s",
                       stats, "st", bi, u32[bi][:], "u32s", rstd_via="pool")
        emit_quant_fp4(p, u32[bi][:], "u32s", xqb[bi][:], f"xqb{bi}",
                       scr, "qs", 64)
        for ec in range(NC):
            pk = "ptr" + str(ec % 2)
            p.op("tensor", lambda e, ec=ec, bi=bi: e.transpose(
                ptr[ec % 2][:], xqb[bi][:, ec * 128:(ec + 1) * 128], eye[:]),
                ins=[f"xqb{bi}", "eye"], outs=[pk])
            p.op("scalar", lambda e, ec=ec, bi=bi: e.activation(
                out=xqf[bi][:, ec * 128:(ec + 1) * 128],
                in_=ptr[ec % 2][:], func=AF.Identity, bias=np.float32(0.0)),
                ins=[pk], outs=[f"xqf{bi}_{ec}"])

    def back(t):
        bi = t % 2
        xqf_all = [f"xqf{bi}_{ec}" for ec in range(NC)]
        for k, (i0, i1) in enumerate(ISL):
            isl = slice(i0, i1)
            pgk, kpg = pg[k % 2], f"pg{k % 2}"
            puk, kpu = pu[k % 2], f"pu{k % 2}"

            def mmg(e, isl=isl, i0=i0, i1=i1, bi=bi, pgk=pgk):
                for ec in range(NC):
                    inst = e.matmul(pgk[:, 0:i1 - i0],
                                    xqf[bi][:, ec * 128:(ec + 1) * 128],
                                    wgT[:, ec * I_DIM:(ec + 1) * I_DIM][:, isl],
                                    start=(ec == 0), stop=(ec == NC - 1))
                return inst
            p.op("tensor", mmg, ins=xqf_all + wgT_all, outs=[kpg])
            p.op("scalar", lambda e, isl=isl, i0=i0, i1=i1, bi=bi,
                 pgk=pgk: e.activation(
                out=gs[bi][:, isl], in_=pgk[:, 0:i1 - i0], func=AF.Silu),
                ins=[kpg], outs=[f"gs{bi}:{k}"])

            def mmu(e, isl=isl, i0=i0, i1=i1, bi=bi, puk=puk):
                for ec in range(NC):
                    inst = e.matmul(puk[:, 0:i1 - i0],
                                    xqf[bi][:, ec * 128:(ec + 1) * 128],
                                    wuT[:, ec * I_DIM:(ec + 1) * I_DIM][:, isl],
                                    start=(ec == 0), stop=(ec == NC - 1))
                return inst
            p.op("tensor", mmu, ins=xqf_all + wuT_all, outs=[kpu])
            usl = slice((k % 2) * 512, (k % 2) * 512 + (i1 - i0))
            p.op("scalar", lambda e, usl=usl, i0=i0, i1=i1, puk=puk: e.activation(
                out=upe[:, usl], in_=puk[:, 0:i1 - i0], func=AF.Identity,
                bias=np.float32(0.0)),
                ins=[kpu], outs=[f"upe{k % 2}"])
            p.op("gpsimd", lambda e, isl=isl, usl=usl, bi=bi: e.tensor_tensor(
                gs[bi][:, isl], upe[:, usl], gs[bi][:, isl], ALU.mult),
                ins=[f"upe{k % 2}", f"gs{bi}:{k}"], outs=[f"gs{bi}:{k}"])
            if k == 1:
                emit_quant_fp4(p, gs[bi][:, 0:1024], [f"gs{bi}:0", f"gs{bi}:1"],
                               hq[bi][:, 0:1024], "hqs_0", scr, "qs", 64)
            if k == 3:
                emit_quant_fp4(p, gs[bi][:, 1024:2048],
                               [f"gs{bi}:2", f"gs{bi}:3"],
                               hq[bi][:, 1024:2048], "hqs_1", scr, "qs", 64)
        emit_quant_fp4(p, gs[bi][:, 2048:2816], [f"gs{bi}:4", f"gs{bi}:5"],
                       hq[bi][:, 2048:2816], "hqs_2", scr, "qs", 48)
        for ic in range(NI):
            pk = "ptr" + str(ic % 2)
            hqk = "hqs_" + ("0" if ic < 8 else ("1" if ic < 16 else "2"))
            p.op("tensor", lambda e, ic=ic, bi=bi: e.transpose(
                ptr[ic % 2][:], hq[bi][:, ic * 128:(ic + 1) * 128], eye[:]),
                ins=[hqk, "eye"], outs=[pk])
            p.op("scalar", lambda e, ic=ic: e.activation(
                out=hqT[:, ic * 128:(ic + 1) * 128],
                in_=ptr[ic % 2][:], func=AF.Identity, bias=np.float32(0.0)),
                ins=[pk], outs=[f"hqT{ic}"])
        hqT_all = [f"hqT{ic}" for ic in range(NI)]
        for fs in range(2):
            fsl = slice(fs * 512, (fs + 1) * 512)
            psx, kpsx = (pd0, "pd0") if fs == 0 else (pd1, "pd1")

            def mmd(e, fsl=fsl, psx=psx):
                for ic in range(NI):
                    inst = e.matmul(psx[:], hqT[:, ic * 128:(ic + 1) * 128],
                                    wdT[:, ic * D:(ic + 1) * D][:, fsl],
                                    start=(ic == 0), stop=(ic == NI - 1))
                return inst
            p.op("tensor", mmd, ins=hqT_all + wdT_all, outs=[kpsx])
            p.op("vector", lambda e, bi=bi, fsl=fsl, psx=psx: e.tensor_add(
                xt[bi][:, fsl], psx[:], xt[bi][:, fsl]),
                ins=[kpsx, f"xt{bi}"], outs=[f"xt{bi}"])
        p.dma("sync", "st", lambda e, t=t, bi=bi: e.dma_start(
            out=y_d.ap()[t * 128:(t + 1) * 128, :], in_=xt[bi][:]),
            ins=[f"xt{bi}"], outs=[f"y{t}"])

    front(0)
    for t in range(NT):
        if t + 1 < NT:
            front(t + 1)
        back(t)


def build_ffn1():
    nc = bass.Bass("TRN2", target_bir_lowering=False)
    ctx = ExitStack()
    x_d = nc.dram_tensor("x", [T, D], F32, kind="ExternalInput")
    y_d = nc.dram_tensor("y", [T, D], F32, kind="ExternalOutput")
    eye_d = nc.inline_tensor(np.eye(128, dtype=BFNP), name="eye_bf")
    dram = declare_ffn1_dram(nc, "", x_d, y_d, eye_d)
    with ctx:
        p = Prog(nc, ctx)
        emit_ffn1(p, dram)
        p.barrier()
        p.emit()
    return nc



def _mdiff_np():
    c = np.arange(WM, dtype=np.float32)[None, :]
    i = np.arange(128, dtype=np.float32)[:, None]
    return np.ascontiguousarray(np.abs(c - (T - 1) - i).astype(np.float32))


def _attn_host_inputs(Wq, bq, Wk, bk, Wv, bv, Wo, bo):
    d = {}
    d["wqT"] = np.ascontiguousarray(Wq.T)
    d["wkT"] = np.ascontiguousarray(Wk.T)
    d["wvT"] = np.ascontiguousarray(Wv.T)
    d["woT"] = np.ascontiguousarray(Wo.T)
    d["bq_pc"] = np.ascontiguousarray(bq.reshape(NC, 128).T)
    d["bk_pc"] = np.ascontiguousarray(bk.reshape(NC, 128).T)
    d["bv_row"] = np.ascontiguousarray(bv.reshape(1, D))
    d["bo_row"] = np.ascontiguousarray(bo.reshape(1, D))
    return d


_progs = {}


def _get_prog(name, builder):
    if name not in _progs:
        _progs[name] = builder()
    return _progs[name]


def build_fused():
    nc = bass.Bass("TRN2", target_bir_lowering=False)
    ctx = ExitStack()
    x0 = nc.dram_tensor("x", [T, D], F32, kind="ExternalInput")
    y = nc.dram_tensor("y", [T, D], F32, kind="ExternalOutput")
    x1 = nc.dram_tensor("xs1", [T, D], F32, kind="Internal")
    x2 = nc.dram_tensor("xs2", [T, D], F32, kind="Internal")
    x3 = nc.dram_tensor("xs3", [T, D], F32, kind="Internal")
    eye_d = nc.inline_tensor(np.eye(128, dtype=BFNP), name="eye_bf")
    dA = declare_attn_dram(nc, 0, "a0_", x0, x1, eye_d)
    dB = declare_ffn0_dram(nc, "f0_", x1, x2, eye_d)
    dC = declare_attn_dram(nc, 1, "a1_", x2, x3, eye_d)
    dD = declare_ffn1_dram(nc, "f1_", x3, y, eye_d)
    with ctx:
        p = Prog(nc, ctx)
        with ExitStack() as prepctx0:
            p.pre, p.actx = "f0p_", prepctx0
            bufs0 = alloc_prep_bufs(p, slab=256, nm="p0_")
            gen0 = gen_wprep(p, dB["w"], dB["stg"], *bufs0, 6, NC, D, 256,
                             tag="f0_")
            with ExitStack() as actx:
                p.pre, p.actx = "a0_", actx
                emit_attn(p, dA, 0, prep2=gen0)
                for _ in gen0:
                    pass
                p.barrier()
        with ExitStack() as actx:
            p.pre, p.actx = "f0_", actx
            emit_ffn0(p, dB, prep_done=True)
            p.barrier()
        with ExitStack() as prepctx:
            p.pre, p.actx = "f1_", prepctx
            scrW, wchW, wqbW = alloc_prep_bufs(p, slab=512)
            gen = itertools.chain(*f1_prep_gens(p, dD, scrW, wchW, wqbW,
                                                slab_gu=512, slab_d=352))
            with ExitStack() as actx:
                p.pre, p.actx = "a1_", actx
                emit_attn(p, dC, 1, prep=gen)
                for _ in gen:
                    pass
                p.barrier()
        with ExitStack() as actx:
            p.pre, p.actx = "f1_", actx
            emit_ffn1(p, dD, prep_done=True)
            p.barrier()
        p.pre, p.actx = "", ctx
        p.emit()
    return nc


def kernel(L0_attn_x, L0_attn_Wq, L0_attn_bq, L0_attn_Wk, L0_attn_bk,
           L0_attn_Wv, L0_attn_bv, L0_attn_Wo, L0_attn_bo,
           L0_attn_decay_params, L0_attn_mix_weights,
           L0_ffn_weight, L0_ffn_bias,
           L1_attn_Wq, L1_attn_bq, L1_attn_Wk, L1_attn_bk, L1_attn_Wv,
           L1_attn_bv, L1_attn_Wo, L1_attn_bo, L1_attn_ln_gamma,
           L1_attn_ln_beta, L1_ffn_w_gate, L1_ffn_w_up, L1_ffn_w_down):
    ncore = 8
    f32 = np.float32
    x = np.ascontiguousarray(np.asarray(L0_attn_x, dtype=f32))
    cores = list(range(ncore))

    ins = {}
    for pre, args in (
            ("a0_", (L0_attn_Wq, L0_attn_bq, L0_attn_Wk, L0_attn_bk,
                     L0_attn_Wv, L0_attn_bv, L0_attn_Wo, L0_attn_bo)),
            ("a1_", (L1_attn_Wq, L1_attn_bq, L1_attn_Wk, L1_attn_bk,
                     L1_attn_Wv, L1_attn_bv, L1_attn_Wo, L1_attn_bo))):
        h = _attn_host_inputs(*[np.asarray(a, f32) for a in args])
        ins.update({pre + k: v for k, v in h.items()})
    ins["a0_dpmix"] = np.ascontiguousarray(np.concatenate(
        [np.asarray(L0_attn_decay_params, f32).reshape(1, 32),
         np.asarray(L0_attn_mix_weights, f32).reshape(1, 32)], axis=1))
    ins["a0_mdiff"] = _mdiff_np()
    ins["a1_gb"] = np.ascontiguousarray(np.concatenate(
        [np.asarray(L1_attn_ln_gamma, f32),
         np.asarray(L1_attn_ln_beta, f32)]).reshape(1, 2 * D))
    ins["f0_w"] = np.ascontiguousarray(np.asarray(L0_ffn_weight, f32))
    ins["f0_b_row"] = np.ascontiguousarray(np.asarray(L0_ffn_bias, f32).reshape(1, D))
    ins["f1_wg"] = np.ascontiguousarray(np.asarray(L1_ffn_w_gate, f32))
    ins["f1_wu"] = np.ascontiguousarray(np.asarray(L1_ffn_w_up, f32))
    ins["f1_wd"] = np.ascontiguousarray(np.asarray(L1_ffn_w_down, f32))

    ncF = _get_prog("fused", build_fused)
    res = run_bass_kernel_spmd(
        ncF, [dict(ins, x=np.ascontiguousarray(x[b])) for b in cores],
        core_ids=cores)
    return np.stack([np.asarray(res.results[b]["y"]) for b in cores],
                    axis=0).astype(np.float32)


# version 22
# speedup vs baseline: 1.0103x; 1.0103x over previous
import itertools
import numpy as np
import ml_dtypes
from contextlib import ExitStack

import concourse.bass as bass
import concourse.mybir as mybir
from concourse.bass_utils import run_bass_kernel_spmd

F32 = mybir.dt.float32
BF16 = mybir.dt.bfloat16
F16 = mybir.dt.float16
AF = mybir.ActivationFunctionType
ALU = mybir.AluOpType
AX = mybir.AxisListType
BFNP = ml_dtypes.bfloat16

T = 2048
D = 1024
NT = T // 128
NC = D // 128
NH = 16
HD = 64
NHP = 8
WM = 2 * T - 1
LNEPS = 1e-5

ENGINES = ("sync", "scalar", "vector", "tensor", "gpsimd")


class Prog:

    def __init__(self, nc, ctx):
        self.nc, self.ctx = nc, ctx
        self.pre = ""
        self.actx = ctx
        self.sem = {e: ctx.enter_context(nc.semaphore("S_" + e)) for e in ENGINES}
        self.cnt = {e: 0 for e in ENGINES}
        self.dsem = {}
        self.dcnt = {}
        self.writer = {}
        self.readers = {}
        self.waited = {e: {} for e in ENGINES}
        self.recs = {e: [] for e in ENGINES}

    def chan(self, name):
        if name not in self.dsem:
            self.dsem[name] = self.ctx.enter_context(self.nc.semaphore("D_" + name))
            self.dcnt[name] = 0
        return name

    def _need(self, eng, ins, outs):
        need = []
        for k in ins:
            w = self.writer.get(k)
            if w:
                need.append(w)
        for k in outs:
            w = self.writer.get(k)
            if w:
                need.append(w)
            need += self.readers.get(k, ())
        best = {}
        for typ, a, c in need:
            if best.get((typ, a), -1) < c:
                best[(typ, a)] = c
        waits = []
        wt = self.waited[eng]
        for (typ, a), c in sorted(best.items()):
            if typ == "d":
                c = self.dcnt[a]
            if wt.get((typ, a), 0) >= c:
                continue
            wt[(typ, a)] = c
            waits.append((typ, a, c))
        return waits

    def _book(self, me, ins, outs):
        for k in outs:
            self.writer[k] = me
            self.readers[k] = []
        for k in ins:
            self.readers.setdefault(k, []).append(me)

    def op(self, eng, fn, ins=(), outs=()):
        waits = self._need(eng, ins, outs)
        self.cnt[eng] += 1
        self._book(("e", eng, self.cnt[eng]), ins, outs)
        self.recs[eng].append((waits, fn, None))

    def dma(self, eng, ch, fn, ins=(), outs=()):
        ch = self.chan(ch)
        waits = self._need(eng, ins, outs)
        self.dcnt[ch] += 1
        self._book(("d", ch, self.dcnt[ch]), ins, outs)
        self.recs[eng].append((waits, fn, ch))

    def barrier(self):
        snap_e = dict(self.cnt)
        snap_d = dict(self.dcnt)
        for e in ENGINES:
            waits = []
            wt = self.waited[e]
            for e2 in ENGINES:
                c = snap_e[e2]
                if e2 != e and c > 0 and wt.get(("e", e2), 0) < c:
                    wt[("e", e2)] = c
                    waits.append(("e", e2, c))
            for ch, c in snap_d.items():
                if c > 0 and wt.get(("d", ch), 0) < c:
                    wt[("d", ch)] = c
                    waits.append(("d", ch, c))
            if waits:
                self.recs[e].append((waits, None, None))
        self.writer = {}
        self.readers = {}

    def emit(self):
        block = self.ctx.enter_context(self.nc.Block())
        for eng in ENGINES:
            recs = self.recs[eng]

            def mk(recs_, eng_):
                def body(e):
                    for waits, fn, ch in recs_:
                        for typ, a, c in waits:
                            if typ == "e":
                                e.wait_ge(self.sem[a], c)
                            else:
                                e.wait_ge(self.dsem[a], 16 * c)
                        if fn is None:
                            continue
                        inst = fn(e)
                        if ch is None:
                            inst.then_inc(self.sem[eng_], 1)
                        else:
                            inst.then_inc(self.dsem[ch], 16)
                return body

            getattr(block, eng)(mk(recs, eng))

    def sb(self, name, shape, dtype):
        return self.actx.enter_context(
            self.nc.sbuf_tensor(self.pre + name, shape, dtype))

    def ps(self, name, shape, dtype=F32):
        return self.actx.enter_context(
            self.nc.psum_tensor(self.pre + name, shape, dtype))



def ln_stats_and_u(p, xt_ap, xkey, sq_scr, sqkey, stats, stkey, col, u_out, ukey,
                   extra_u=None, rstd_via="act"):
    ncol = stats.shape[1] // 4
    s1 = stats[:, col:col + 1]
    rstd = stats[:, ncol + col:ncol + col + 1]
    negmr = stats[:, 2 * ncol + col:2 * ncol + col + 1]
    ssq = stats[:, 3 * ncol + col:3 * ncol + col + 1]
    kst = stkey + str(col) + "_"
    p.op("vector", lambda e: e.reduce_sum(s1, xt_ap, AX.X), ins=[xkey], outs=[kst + "s1"])
    p.op("vector", lambda e: e.tensor_scalar_mul(s1, s1, -1.0 / D),
         ins=[kst + "s1"], outs=[kst + "s1"])
    p.op("scalar", lambda e: e.activation(out=sq_scr, in_=xt_ap, func=AF.Square,
                                          bias=s1, accum_out=ssq),
         ins=[xkey, kst + "s1"], outs=[sqkey, kst + "ssq"])
    if rstd_via == "act":
        p.op("scalar", lambda e: e.activation(out=rstd, in_=ssq, func=AF.Ln,
                                              scale=np.float32(1.0 / D),
                                              bias=np.float32(LNEPS)),
             ins=[kst + "ssq"], outs=[kst + "rstd"])
        p.op("scalar", lambda e: e.activation(out=rstd, in_=rstd, func=AF.Exp,
                                              scale=np.float32(-0.5)),
             ins=[kst + "rstd"], outs=[kst + "rstd"])
    else:
        p.op("vector", lambda e: e.tensor_scalar(rstd, ssq, 1.0 / D, LNEPS,
                                                 ALU.mult, ALU.add),
             ins=[kst + "ssq"], outs=[kst + "rstd"])
        p.op("gpsimd", lambda e: e.tensor_tensor(rstd, rstd, p.mh[:], ALU.pow),
             ins=[kst + "rstd", "mh"], outs=[kst + "rstd"])
    p.op("vector", lambda e: e.tensor_mul(negmr, s1, rstd),
         ins=[kst + "s1", kst + "rstd"], outs=[kst + "negmr"])
    p.op("scalar", lambda e: e.activation(out=u_out, in_=xt_ap, func=AF.Identity,
                                          scale=rstd, bias=negmr),
         ins=[xkey, kst + "rstd", kst + "negmr"], outs=[ukey])
    if extra_u is not None:
        ap2, key2 = extra_u
        p.op("scalar", lambda e: e.activation(out=ap2, in_=xt_ap, func=AF.Identity,
                                              scale=rstd, bias=negmr),
             ins=[xkey, kst + "rstd", kst + "negmr"], outs=[key2])


def transpose_tile_to_fm(p, u_src_fn, ukey, t, fm_sb, fmkey, ptr, eye, tcols=T):
    for c in range(NC):
        pk = "ptr" + str(c % 2)
        p.op("tensor",
             lambda e, c=c: e.transpose(ptr[c % 2][:], u_src_fn(c), eye[:]),
             ins=[ukey, "eye"], outs=[pk])
        p.op("vector",
             lambda e, c=c: e.tensor_copy(
                 fm_sb[:, c * tcols + t * 128: c * tcols + (t + 1) * 128],
                 ptr[c % 2][:]),
             ins=[pk], outs=[f"{fmkey}:{c}:{t}"])


def fm_keys(fmkey, ncs=NC, nts=NT):
    return [f"{fmkey}:{c}:{t}" for c in range(ncs) for t in range(nts)]



def emit_quant_fp4(p, x_ap, xkey, out_ap, outkey, scr, kb, nblk, blk=16):
    if isinstance(xkey, str):
        xkey = [xkey]
    W = nblk * blk
    bm, y, u, a, b = scr["bm"], scr["y"], scr["u"], scr["a"], scr["b"]
    xg = x_ap.rearrange("p (n i) -> p n i", i=blk)
    p.op("vector", lambda e: e.tensor_reduce(bm[:, 0:nblk].unsqueeze(2), xg, AX.X,
                                             ALU.max, apply_absolute_value=True),
         ins=xkey, outs=[kb + "bm"])
    p.op("vector", lambda e: e.tensor_scalar(bm[:, nblk:2 * nblk], bm[:, 0:nblk],
                                             1.0 / 6.0, 1e-12, ALU.mult, ALU.max),
         ins=[kb + "bm"], outs=[kb + "sc"])
    p.op("vector", lambda e: e.reciprocal(bm[:, 2 * nblk:3 * nblk],
                                          bm[:, nblk:2 * nblk]),
         ins=[kb + "sc"], outs=[kb + "rs"])
    rb = bm[:, 2 * nblk:3 * nblk].unsqueeze(2).broadcast_to([128, nblk, blk])
    scb = bm[:, nblk:2 * nblk].unsqueeze(2).broadcast_to([128, nblk, blk])
    yg = y[:, 0:W].rearrange("p (n i) -> p n i", i=blk)
    p.op("vector", lambda e: e.tensor_tensor(yg, xg, rb, ALU.mult),
         ins=xkey + [kb + "rs"], outs=[kb + "y"])
    p.op("vector", lambda e: e.scalar_tensor_tensor(
        out=u[:, 0:W], in0=y[:, 0:W], scalar=-1.0, in1=y[:, 0:W],
        op0=ALU.mult, op1=ALU.max),
         ins=[kb + "y"], outs=[kb + "u"])
    p.op("vector", lambda e: e.tensor_single_scalar(a[:, 0:W], u[:, 0:W], 0.25,
                                                    ALU.is_gt),
         ins=[kb + "u"], outs=[kb + "a"])
    for th in (0.75, 1.25, 1.75):
        p.op("vector", lambda e, th=th: e.scalar_tensor_tensor(
            out=a[:, 0:W], in0=u[:, 0:W], scalar=th, in1=a[:, 0:W],
            op0=ALU.is_gt, op1=ALU.add), ins=[kb + "u", kb + "a"], outs=[kb + "a"])
    p.op("vector", lambda e: e.tensor_single_scalar(b[:, 0:W], u[:, 0:W], 2.5,
                                                    ALU.is_gt),
         ins=[kb + "u"], outs=[kb + "b"])
    p.op("vector", lambda e: e.scalar_tensor_tensor(
        out=b[:, 0:W], in0=u[:, 0:W], scalar=3.5, in1=b[:, 0:W],
        op0=ALU.is_gt, op1=ALU.add), ins=[kb + "u", kb + "b"], outs=[kb + "b"])
    p.op("vector", lambda e: e.tensor_single_scalar(u[:, 0:W], u[:, 0:W], 5.0,
                                                    ALU.is_gt),
         ins=[kb + "u"], outs=[kb + "cc"])
    p.op("vector", lambda e: e.scalar_tensor_tensor(
        out=b[:, 0:W], in0=u[:, 0:W], scalar=2.0, in1=b[:, 0:W],
        op0=ALU.mult, op1=ALU.add), ins=[kb + "cc", kb + "b"], outs=[kb + "b"])
    p.op("vector", lambda e: e.scalar_tensor_tensor(
        out=b[:, 0:W], in0=a[:, 0:W], scalar=0.5, in1=b[:, 0:W],
        op0=ALU.mult, op1=ALU.add), ins=[kb + "a", kb + "b"], outs=[kb + "b"])
    bg = b[:, 0:W].rearrange("p (n i) -> p n i", i=blk)
    p.op("vector", lambda e: e.tensor_tensor(bg, bg, scb, ALU.mult),
         ins=[kb + "b", kb + "sc"], outs=[kb + "b"])
    p.op("vector", lambda e: e.tensor_scalar(u[:, 0:W], y[:, 0:W], 0.0, 2.0,
                                             ALU.is_gt, ALU.mult),
         ins=[kb + "y"], outs=[kb + "ssel"])
    p.op("vector", lambda e: e.tensor_mul(a[:, 0:W], b[:, 0:W], u[:, 0:W]),
         ins=[kb + "b", kb + "ssel"], outs=[kb + "d2"])
    p.op("vector", lambda e: e.tensor_sub(out_ap, a[:, 0:W], b[:, 0:W]),
         ins=[kb + "d2", kb + "b"], outs=[outkey])


def emit_quant_fp6(p, x_ap, xkey, out_ap, outkey, scr, kb, nblk, blk=32):
    if isinstance(xkey, str):
        xkey = [xkey]
    W = nblk * blk
    bm, y, u, a, b, s_t = scr["bm"], scr["y"], scr["u"], scr["a"], scr["b"], scr["s"]
    xg = x_ap.rearrange("p (n i) -> p n i", i=blk)
    p.op("vector", lambda e: e.tensor_reduce(bm[:, 0:nblk].unsqueeze(2), xg, AX.X,
                                             ALU.max, apply_absolute_value=True),
         ins=xkey, outs=[kb + "bm"])
    p.op("vector", lambda e: e.tensor_scalar(bm[:, nblk:2 * nblk], bm[:, 0:nblk],
                                             1.0 / 7.5, 1e-30, ALU.mult, ALU.max),
         ins=[kb + "bm"], outs=[kb + "sc"])
    p.op("vector", lambda e: e.reciprocal(bm[:, 2 * nblk:3 * nblk],
                                          bm[:, nblk:2 * nblk]),
         ins=[kb + "sc"], outs=[kb + "rs"])
    rb = bm[:, 2 * nblk:3 * nblk].unsqueeze(2).broadcast_to([128, nblk, blk])
    scb = bm[:, nblk:2 * nblk].unsqueeze(2).broadcast_to([128, nblk, blk])
    yg = y[:, 0:W].rearrange("p (n i) -> p n i", i=blk)
    p.op("vector", lambda e: e.tensor_tensor(yg, xg, rb, ALU.mult),
         ins=xkey + [kb + "rs"], outs=[kb + "y"])
    p.op("vector", lambda e: e.scalar_tensor_tensor(
        out=u[:, 0:W], in0=y[:, 0:W], scalar=-1.0, in1=y[:, 0:W],
        op0=ALU.mult, op1=ALU.max),
         ins=[kb + "y"], outs=[kb + "u"])
    p.op("vector", lambda e: e.tensor_single_scalar(a[:, 0:W], u[:, 0:W], 2.0,
                                                    ALU.is_gt),
         ins=[kb + "u"], outs=[kb + "c1"])
    p.op("vector", lambda e: e.tensor_single_scalar(b[:, 0:W], u[:, 0:W], 3.875,
                                                    ALU.is_gt),
         ins=[kb + "u"], outs=[kb + "c2"])
    p.op("vector", lambda e: e.tensor_scalar(s_t[:, 0:W], a[:, 0:W], 0.125, 0.125,
                                             ALU.mult, ALU.add),
         ins=[kb + "c1"], outs=[kb + "s"])
    p.op("vector", lambda e: e.scalar_tensor_tensor(
        out=s_t[:, 0:W], in0=b[:, 0:W], scalar=0.25, in1=s_t[:, 0:W],
        op0=ALU.mult, op1=ALU.add), ins=[kb + "c2", kb + "s"], outs=[kb + "s"])
    p.op("vector", lambda e: e.tensor_scalar(a[:, 0:W], a[:, 0:W], -4.0, 8.0,
                                             ALU.mult, ALU.add),
         ins=[kb + "c1"], outs=[kb + "t"])
    p.op("vector", lambda e: e.scalar_tensor_tensor(
        out=a[:, 0:W], in0=b[:, 0:W], scalar=-2.0, in1=a[:, 0:W],
        op0=ALU.mult, op1=ALU.add), ins=[kb + "c2", kb + "t"], outs=[kb + "t"])
    p.op("vector", lambda e: e.tensor_mul(u[:, 0:W], u[:, 0:W], a[:, 0:W]),
         ins=[kb + "u", kb + "t"], outs=[kb + "z"])
    p.op("vector", lambda e: e.tensor_single_scalar(a[:, 0:W], u[:, 0:W], 7.5,
                                                    ALU.is_gt),
         ins=[kb + "z"], outs=[kb + "cs"])
    p.op("vector", lambda e: e.tensor_scalar_mul(b[:, 0:W], a[:, 0:W], 8.0),
         ins=[kb + "cs"], outs=[kb + "r"])
    p.op("vector", lambda e: e.scalar_tensor_tensor(
        out=u[:, 0:W], in0=a[:, 0:W], scalar=-8.0, in1=u[:, 0:W],
        op0=ALU.mult, op1=ALU.add), ins=[kb + "cs", kb + "z"], outs=[kb + "z"])
    for w in (4.0, 2.0, 1.0, 1.0):
        p.op("vector", lambda e, w=w: e.tensor_single_scalar(a[:, 0:W], u[:, 0:W],
                                                             w - 0.5, ALU.is_gt),
             ins=[kb + "z"], outs=[kb + "cs"])
        p.op("vector", lambda e, w=w: e.scalar_tensor_tensor(
            out=b[:, 0:W], in0=a[:, 0:W], scalar=w, in1=b[:, 0:W],
            op0=ALU.mult, op1=ALU.add), ins=[kb + "cs", kb + "r"], outs=[kb + "r"])
        p.op("vector", lambda e, w=w: e.scalar_tensor_tensor(
            out=u[:, 0:W], in0=a[:, 0:W], scalar=-w, in1=u[:, 0:W],
            op0=ALU.mult, op1=ALU.add), ins=[kb + "cs", kb + "z"], outs=[kb + "z"])
    p.op("vector", lambda e: e.tensor_mul(b[:, 0:W], b[:, 0:W], s_t[:, 0:W]),
         ins=[kb + "r", kb + "s"], outs=[kb + "q"])
    bg = b[:, 0:W].rearrange("p (n i) -> p n i", i=blk)
    p.op("vector", lambda e: e.tensor_tensor(bg, bg, scb, ALU.mult),
         ins=[kb + "q", kb + "sc"], outs=[kb + "q"])
    p.op("vector", lambda e: e.tensor_scalar(u[:, 0:W], y[:, 0:W], 0.0, 2.0,
                                             ALU.is_gt, ALU.mult),
         ins=[kb + "y"], outs=[kb + "ssel"])
    p.op("vector", lambda e: e.tensor_mul(a[:, 0:W], b[:, 0:W], u[:, 0:W]),
         ins=[kb + "q", kb + "ssel"], outs=[kb + "d2"])
    p.op("vector", lambda e: e.tensor_sub(out_ap, a[:, 0:W], b[:, 0:W]),
         ins=[kb + "d2", kb + "q"], outs=[outkey])




def _pool_quant_pre(p, x_ap, xkey, scr, kb, nblk, blk, qmax, eps):
    if isinstance(xkey, str):
        xkey = [xkey]
    W = nblk * blk
    bm, y, u = scr["bm"], scr["y"], scr["u"]
    xg = x_ap.rearrange("p (n i) -> p n i", i=blk)
    p.op("vector", lambda e: e.tensor_reduce(bm[:, 0:nblk].unsqueeze(2), xg, AX.X,
                                             ALU.max, apply_absolute_value=True),
         ins=xkey, outs=[kb + "bm"])
    p.op("vector", lambda e: e.tensor_scalar(bm[:, nblk:2 * nblk], bm[:, 0:nblk],
                                             1.0 / qmax, eps, ALU.mult, ALU.max),
         ins=[kb + "bm"], outs=[kb + "sc"])
    p.op("vector", lambda e: e.reciprocal(bm[:, 2 * nblk:3 * nblk],
                                          bm[:, nblk:2 * nblk]),
         ins=[kb + "sc"], outs=[kb + "rs"])
    rb = bm[:, 2 * nblk:3 * nblk].unsqueeze(2).broadcast_to([128, nblk, blk])
    yg = y[:, 0:W].rearrange("p (n i) -> p n i", i=blk)
    p.op("gpsimd", lambda e: e.tensor_tensor(yg, xg, rb, ALU.mult),
         ins=xkey + [kb + "rs"], outs=[kb + "y"])
    a = scr["a"]
    p.op("gpsimd", lambda e: e.tensor_single_scalar(a[:, 0:W], y[:, 0:W], 0.0,
                                                    ALU.is_gt),
         ins=[kb + "y"], outs=[kb + "pm"])
    p.op("gpsimd", lambda e: e.tensor_scalar(a[:, 0:W], a[:, 0:W], 2.0, -1.0,
                                             ALU.mult, ALU.add),
         ins=[kb + "pm"], outs=[kb + "sgn"])
    p.op("gpsimd", lambda e: e.tensor_tensor(u[:, 0:W], y[:, 0:W], a[:, 0:W],
                                             ALU.mult),
         ins=[kb + "y", kb + "sgn"], outs=[kb + "u"])


def emit_quant_fp4_pool(p, x_ap, xkey, out_ap, outkey, scr, kb, nblk, blk=16):
    W = nblk * blk
    bm, u, a, b, q = scr["bm"], scr["u"], scr["a"], scr["b"], scr["y"]
    _pool_quant_pre(p, x_ap, xkey, scr, kb, nblk, blk, 6.0, 1e-12)
    p.op("vector", lambda e: e.tensor_scalar_mul(bm[:, 3 * nblk:4 * nblk],
                                                 bm[:, nblk:2 * nblk], 0.5),
         ins=[kb + "sc"], outs=[kb + "sc2"])
    p.op("gpsimd", lambda e: e.tensor_single_scalar(b[:, 0:W], u[:, 0:W], 0.25,
                                                    ALU.is_gt),
         ins=[kb + "u"], outs=[kb + "q2"])
    for th in (0.75, 1.25, 1.75):
        p.op("gpsimd", lambda e, th=th: e.tensor_single_scalar(
            q[:, 0:W], u[:, 0:W], th, ALU.is_gt),
            ins=[kb + "u"], outs=[kb + "c"])
        p.op("gpsimd", lambda e: e.tensor_tensor(b[:, 0:W], b[:, 0:W], q[:, 0:W],
                                                 ALU.add),
             ins=[kb + "c", kb + "q2"], outs=[kb + "q2"])
    for th, wgt in ((2.5, 2.0), (3.5, 2.0), (5.0, 4.0)):
        p.op("gpsimd", lambda e, th=th, wgt=wgt: e.tensor_scalar(
            q[:, 0:W], u[:, 0:W], th, wgt, ALU.is_gt, ALU.mult),
            ins=[kb + "u"], outs=[kb + "c"])
        p.op("gpsimd", lambda e: e.tensor_tensor(b[:, 0:W], b[:, 0:W], q[:, 0:W],
                                                 ALU.add),
             ins=[kb + "c", kb + "q2"], outs=[kb + "q2"])
    scb = bm[:, 3 * nblk:4 * nblk].unsqueeze(2).broadcast_to([128, nblk, blk])
    bg = b[:, 0:W].rearrange("p (n i) -> p n i", i=blk)
    p.op("gpsimd", lambda e: e.tensor_tensor(bg, bg, scb, ALU.mult),
         ins=[kb + "q2", kb + "sc2"], outs=[kb + "d1"])
    p.op("gpsimd", lambda e: e.tensor_tensor(out_ap, b[:, 0:W], a[:, 0:W],
                                             ALU.mult),
         ins=[kb + "d1", kb + "sgn"], outs=[outkey])


def emit_quant_fp6_pool(p, x_ap, xkey, out_ap, outkey, scr, kb, nblk, blk=32):
    W = nblk * blk
    bm, u, a, b, q, s_t = (scr["bm"], scr["u"], scr["a"], scr["b"], scr["y"],
                           scr["s"])
    _pool_quant_pre(p, x_ap, xkey, scr, kb, nblk, blk, 7.5, 1e-30)
    p.op("gpsimd", lambda e: e.tensor_single_scalar(q[:, 0:W], u[:, 0:W], 2.0,
                                                    ALU.is_gt),
         ins=[kb + "u"], outs=[kb + "c1"])
    p.op("gpsimd", lambda e: e.tensor_scalar(b[:, 0:W], q[:, 0:W], -4.0, 8.0,
                                             ALU.mult, ALU.add),
         ins=[kb + "c1"], outs=[kb + "t"])
    p.op("gpsimd", lambda e: e.tensor_scalar(s_t[:, 0:W], q[:, 0:W], 0.125, 0.125,
                                             ALU.mult, ALU.add),
         ins=[kb + "c1"], outs=[kb + "s"])
    p.op("gpsimd", lambda e: e.tensor_single_scalar(q[:, 0:W], u[:, 0:W], 3.875,
                                                    ALU.is_gt),
         ins=[kb + "u"], outs=[kb + "c2"])
    p.op("gpsimd", lambda e: e.tensor_scalar(q[:, 0:W], q[:, 0:W], 2.0, 0.0,
                                             ALU.mult, ALU.add),
         ins=[kb + "c2"], outs=[kb + "c22"])
    p.op("gpsimd", lambda e: e.tensor_tensor(b[:, 0:W], b[:, 0:W], q[:, 0:W],
                                             ALU.subtract),
         ins=[kb + "c22", kb + "t"], outs=[kb + "t"])
    p.op("gpsimd", lambda e: e.tensor_scalar(q[:, 0:W], q[:, 0:W], 0.125, 0.0,
                                             ALU.mult, ALU.add),
         ins=[kb + "c22"], outs=[kb + "s2"])
    p.op("gpsimd", lambda e: e.tensor_tensor(s_t[:, 0:W], s_t[:, 0:W], q[:, 0:W],
                                             ALU.add),
         ins=[kb + "s2", kb + "s"], outs=[kb + "s"])
    p.op("gpsimd", lambda e: e.tensor_tensor(u[:, 0:W], u[:, 0:W], b[:, 0:W],
                                             ALU.mult),
         ins=[kb + "u", kb + "t"], outs=[kb + "z"])
    p.op("gpsimd", lambda e: e.tensor_single_scalar(q[:, 0:W], u[:, 0:W], 7.5,
                                                    ALU.is_gt),
         ins=[kb + "z"], outs=[kb + "cs"])
    p.op("gpsimd", lambda e: e.tensor_scalar(b[:, 0:W], q[:, 0:W], 8.0, 0.0,
                                             ALU.mult, ALU.add),
         ins=[kb + "cs"], outs=[kb + "r"])
    p.op("gpsimd", lambda e: e.tensor_tensor(u[:, 0:W], u[:, 0:W], b[:, 0:W],
                                             ALU.subtract),
         ins=[kb + "r", kb + "z"], outs=[kb + "z"])
    for w in (4.0, 2.0, 1.0, 1.0):
        p.op("gpsimd", lambda e, w=w: e.tensor_scalar(
            q[:, 0:W], u[:, 0:W], w - 0.5, w, ALU.is_gt, ALU.mult),
            ins=[kb + "z"], outs=[kb + "csw"])
        p.op("gpsimd", lambda e: e.tensor_tensor(b[:, 0:W], b[:, 0:W], q[:, 0:W],
                                                 ALU.add),
             ins=[kb + "csw", kb + "r"], outs=[kb + "r"])
        p.op("gpsimd", lambda e: e.tensor_tensor(u[:, 0:W], u[:, 0:W], q[:, 0:W],
                                                 ALU.subtract),
             ins=[kb + "csw", kb + "z"], outs=[kb + "z"])
    p.op("gpsimd", lambda e: e.tensor_tensor(b[:, 0:W], b[:, 0:W], s_t[:, 0:W],
                                             ALU.mult),
         ins=[kb + "r", kb + "s"], outs=[kb + "qv"])
    scb = bm[:, nblk:2 * nblk].unsqueeze(2).broadcast_to([128, nblk, blk])
    bg = b[:, 0:W].rearrange("p (n i) -> p n i", i=blk)
    p.op("gpsimd", lambda e: e.tensor_tensor(bg, bg, scb, ALU.mult),
         ins=[kb + "qv", kb + "sc"], outs=[kb + "d1"])
    p.op("gpsimd", lambda e: e.tensor_tensor(out_ap, b[:, 0:W], a[:, 0:W],
                                             ALU.mult),
         ins=[kb + "d1", kb + "sgn"], outs=[outkey])


def gen_wprep(p, w_d, stg_d, scr, wch, wqb, qkind, nchunks, width, slab,
              tag="", engine="dve", unit_sel=None):
    emitq = {("dve", 6): emit_quant_fp6, ("dve", 4): emit_quant_fp4,
             ("pool", 6): emit_quant_fp6_pool,
             ("pool", 4): emit_quant_fp4_pool}[(engine, qkind)]
    blk = 32 if qkind == 6 else 16
    units = [(c, s0) for c in range(nchunks) for s0 in range(0, width, slab)]
    if unit_sel is not None:
        units = [u for i, u in enumerate(units) if unit_sel(i)]

    def load(i):
        c, s0 = units[i]
        bi = i % 2
        p.dma("sync", f"{tag}wprep{bi}", lambda e, c=c, s0=s0, bi=bi: e.dma_start(
            out=wch[bi][:, 0:slab],
            in_=w_d.ap()[c * 128:(c + 1) * 128, s0:s0 + slab]),
            outs=[f"{tag}wprep_ch{bi}"])

    load(0)
    for i, (c, s0) in enumerate(units):
        bi = i % 2
        if i + 1 < len(units):
            load(i + 1)
        emitq(p, wch[bi][:, 0:slab], f"{tag}wprep_ch{bi}", wqb[bi][:, 0:slab],
              f"{tag}wprep_qb{bi}", scr, tag + "qW", slab // blk)
        p.dma("sync", f"{tag}wstg{bi}", lambda e, c=c, s0=s0, bi=bi: e.dma_start(
            out=stg_d.ap()[c * 128:(c + 1) * 128, s0:s0 + slab],
            in_=wqb[bi][:, 0:slab]),
            ins=[f"{tag}wprep_qb{bi}"], outs=[f"stg_{stg_d.name}:{tag}{i}"])
        yield


def alloc_prep_bufs(p, slab=1408, nm="pw_"):
    scr = {k: p.sb(nm + k, [128, slab], F32) for k in ("y", "u", "a", "b", "s")}
    scr["bm"] = p.sb(nm + "bm", [128, 3 * 88], F32)
    wch = [p.sb(f"{nm}wch{i}", [128, slab], F32) for i in range(2)]
    wqb = [p.sb(f"{nm}wqb{i}", [128, slab], BF16) for i in range(2)]
    return scr, wch, wqb


_RR_END = object()


def rr_merge(*gens):
    gens = list(gens)
    while gens:
        for g in list(gens):
            v = next(g, _RR_END)
            if v is _RR_END:
                gens.remove(g)
            else:
                yield v


def f1_prep_gens(p, dram, scrW, wchW, wqbW, slab_gu=1024, slab_d=1408):
    return (gen_wprep(p, dram["wg"], dram["stg_g"], scrW, wchW, wqbW, 4, NI,
                      D, slab_gu),
            gen_wprep(p, dram["wu"], dram["stg_u"], scrW, wchW, wqbW, 4, NI,
                      D, slab_gu),
            gen_wprep(p, dram["wd"], dram["stg_dn"], scrW, wchW, wqbW, 4, NC,
                      I_DIM, slab_d))


def f1_prep_split(p, dram, bufsD, bufsP):
    scrD, wchD, wqbD = bufsD
    scrP, wchP, wqbP = bufsP
    pool = itertools.chain(
        gen_wprep(p, dram["wg"], dram["stg_g"], scrP, wchP, wqbP, 4, NI, D,
                  512, tag="fp_", engine="pool", unit_sel=lambda i: i % 3 != 2),
        gen_wprep(p, dram["wu"], dram["stg_u"], scrP, wchP, wqbP, 4, NI, D,
                  512, tag="fp_", engine="pool", unit_sel=lambda i: i % 3 != 2))
    dve = itertools.chain(
        gen_wprep(p, dram["wg"], dram["stg_g"], scrD, wchD, wqbD, 4, NI, D,
                  512, tag="fd_", engine="dve", unit_sel=lambda i: i % 3 == 2),
        gen_wprep(p, dram["wu"], dram["stg_u"], scrD, wchD, wqbD, 4, NI, D,
                  512, tag="fd_", engine="dve", unit_sel=lambda i: i % 3 == 2),
        gen_wprep(p, dram["wd"], dram["stg_dn"], scrD, wchD, wqbD, 4, NC,
                  I_DIM, 352, tag="fd_", engine="dve"))
    return pool, dve


def load_wT_staged(p, stg_d, dst, dstkey, ncols_chunks, chunk_rows,
                   eng="scalar"):
    W = chunk_rows
    for ec in range(ncols_chunks):
        p.dma(eng, f"wl{ec % 2}", lambda e, ec=ec: e.dma_start(
            out=dst[:, ec * W:(ec + 1) * W],
            in_=stg_d.ap()[:, ec * 128:(ec + 1) * 128], transpose=True),
            ins=[f"stg_{stg_d.name}"], outs=[f"{dstkey}:{ec}"])



def declare_attn_dram(nc, layer, pre, x_d, y_d, eye_d):
    d = {"x": x_d, "y": y_d, "eye": eye_d}
    for nm, shape in (("wqT", [D, D]), ("wkT", [D, D]), ("wvT", [D, D]),
                      ("woT", [D, D]), ("bq_pc", [128, NC]), ("bk_pc", [128, NC]),
                      ("bv_row", [1, D]), ("bo_row", [1, D])):
        d[nm] = nc.dram_tensor(pre + nm, shape, F32, kind="ExternalInput")
    if layer == 0:
        d["dpmix"] = nc.dram_tensor(pre + "dpmix", [1, 64], F32, kind="ExternalInput")
        d["mdiff"] = nc.dram_tensor(pre + "mdiff", [128, WM], F32, kind="ExternalInput")
    else:
        d["gb"] = nc.dram_tensor(pre + "gb", [1, 2 * D], F32, kind="ExternalInput")
    return d


def emit_attn(p, dram, layer, prep=None, prep2=None):
    def prep_step(n=1, site=3):
        g = prep2 if (site == 2 and prep2 is not None) else prep
        if g is not None:
            for _ in range(n):
                next(g, None)

    nc = p.nc
    x_d, y_d, eye_d = dram["x"], dram["y"], dram["eye"]
    wqT_d, wkT_d, wvT_d, woT_d = dram["wqT"], dram["wkT"], dram["wvT"], dram["woT"]
    p.mh = p.sb("mh", [128, 1], F32)
    p.op("gpsimd", lambda e: e.memset(p.mh[:], -0.5), outs=["mh"])
    bq_d, bk_d, bv_d, bo_d = dram["bq_pc"], dram["bk_pc"], dram["bv_row"], dram["bo_row"]
    if layer == 0:
        dpmix_d, mdiff_d = dram["dpmix"], dram["mdiff"]
    else:
        gb_d = dram["gb"]
    wqkv = [p.sb(f"wqkv{i}", [128, 2 * D], BF16) for i in range(2)]
    bq = p.sb("bq", [128, NC], F32)
    bk = p.sb("bk", [128, NC], F32)
    bvr = p.sb("bvr", [1, D], BF16)
    bor = p.sb("bor", [1, D], BF16)
    eye = p.sb("eye", [128, 128], BF16)
    ones = p.sb("ones", [128, 128], BF16)
    ones32 = p.sb("ones32", [1, 128], F32)
    u_fm = p.sb("u_fm", [128, NC * T], BF16)
    attn = p.sb("attn", [128, NHP * T], BF16)
    stats = p.sb("stats", [128, 4 * NT], F32)
    stats2 = p.sb("stats2", [128, 4 * 2], F32)
    stats3 = p.sb("stats3", [128, 4 * 2], F32)
    qp = p.sb("qp", [128, T], BF16)
    kp = p.sb("kp", [128, T], BF16)
    v_all = p.sb("v_all", [128, NT * 1040], BF16)
    ebf = p.sb("ebf", [128, 2 * 1024], BF16)
    msk = p.sb("msk", [128, 2 * 1024 if layer == 0 else 1024], F32)
    sq_scr = msk[:, 0:D]
    dsc = p.sb("dsc", [128, 512], F32)
    dbf = p.sb("dbf", [128, 512], BF16)
    asb = [p.sb(f"asb{i}", [128, 512], F32) for i in range(2)]
    dnb = [p.sb(f"dnb{i}", [128, 512], F32) for i in range(2)]
    sc_mask = ExitStack()
    if layer == 0:
        _oact = p.actx
        p.actx = sc_mask
        mdiff = p.sb("mdiff_s", [128, WM], F32)
        emask = p.sb("emask", [128, 2 * WM], F16)
        etmp = p.sb("etmp", [128, 1024], F32)
        ball = p.sb("ball", [128, 64], F32)
        brow = p.sb("brow", [1, 64], F32)
        mrow = p.sb("mrow", [1, 64], F32)
        p.actx = _oact
    pq = p.ps("pq", [128, 512], F32)
    po0 = p.ps("po0", [128, 512], F32)
    po1 = p.ps("po1", [128, 512], F32)

    p.dma("gpsimd", "c_bv", lambda e: e.dma_start(out=bvr[:], in_=bv_d.ap()),
          outs=["bvr"])
    p.dma("gpsimd", "c_bo", lambda e: e.dma_start(out=bor[:], in_=bo_d.ap()),
          outs=["bor"])
    p.dma("sync", "c_bq", lambda e: e.dma_start(out=bq[:], in_=bq_d.ap()),
          outs=["bq"])
    p.dma("sync", "c_bk", lambda e: e.dma_start(out=bk[:], in_=bk_d.ap()),
          outs=["bk"])
    p.dma("sync", "c_eye", lambda e: e.dma_start(out=eye[:], in_=eye_d.ap()),
          outs=["eye"])
    p.op("vector", lambda e: e.memset(ones[:], 1.0), outs=["ones"])
    p.op("vector", lambda e: e.memset(ones32[:], 1.0), outs=["ones32"])
    if layer == 0:
        p.dma("sync", "c_md", lambda e: e.dma_start(out=mdiff[:], in_=mdiff_d.ap()),
              outs=["mdiff"])
        p.dma("sync", "c_dm", lambda e: e.dma_start(out=mrow[0:1, :],
                                                    in_=dpmix_d.ap()),
              outs=["mrow"])

    if layer == 0:
        p.op("vector", lambda e: e.tensor_scalar(brow[0:1, 0:32], mrow[0:1, 0:32],
                                                 0.01, 0.9999, ALU.max, ALU.min),
             ins=["mrow"], outs=["dpc"])
        p.op("scalar", lambda e: e.activation(out=brow[0:1, 0:32],
                                              in_=brow[0:1, 0:32], func=AF.Ln),
             ins=["dpc"], outs=["lndp"])
        p.op("scalar", lambda e: e.activation(out=mrow[0:1, 32:64],
                                              in_=mrow[0:1, 32:64], func=AF.Exp),
             ins=["mrow"], outs=["mexp"])
        mv = mrow[0:1, 32:64].rearrange("p (h n) -> p h n", n=2)
        p.op("vector", lambda e: e.tensor_reduce(
            mrow[0:1, 16:32].unsqueeze(2), mv, AX.X, ALU.add),
             ins=["mexp"], outs=["msum"])
        p.op("vector", lambda e: e.reciprocal(mrow[0:1, 16:32], mrow[0:1, 16:32]),
             ins=["msum"], outs=["mrec"])
        mrb = mrow[0:1, 16:32].unsqueeze(2).broadcast_to([1, 16, 2])
        p.op("vector", lambda e: e.tensor_tensor(mv, mv, mrb, ALU.mult),
             ins=["mexp", "mrec"], outs=["mnorm"])
        p.op("scalar", lambda e: e.activation(out=brow[0:1, 32:64],
                                              in_=mrow[0:1, 32:64], func=AF.Ln),
             ins=["mnorm"], outs=["lnm"])
        p.op("vector", lambda e: e.tensor_scalar_add(
            brow[0:1, 32:64], brow[0:1, 32:64], float(-np.log(8.0))),
             ins=["lnm"], outs=["lnm"])
        p.op("tensor", lambda e: e.matmul(pq[:, 0:64], ones32[0:1, :],
                                          brow[0:1, :], start=True, stop=True),
             ins=["ones32", "lndp", "lnm"], outs=["pq"])
        p.op("scalar", lambda e: e.activation(out=ball[:], in_=pq[:, 0:64],
                                              func=AF.Identity, bias=np.float32(0.0)),
             ins=["pq"], outs=["ball"])

    outer_actx = p.actx
    with ExitStack() as sc1:
        p.actx = sc1
        ptr = [p.ps(f"ptr{i}", [128, 128], BF16) for i in range(2)]
        xt = [p.sb(f"xt{i}", [128, D], F32) for i in range(2)]
        ubf = [p.sb(f"ubf{i}", [128, D], BF16) for i in range(2)]
        p.actx = outer_actx
        for t in range(NT):
            bi = t % 2
            p.dma("sync", f"x{bi}", lambda e, t=t, bi=bi: e.dma_start(
                out=xt[bi][:], in_=x_d.ap()[t * 128:(t + 1) * 128, :]),
                outs=[f"xt{bi}"])
            ln_stats_and_u(p, xt[bi][:], f"xt{bi}", sq_scr[:], "sq_scr",
                           stats, "st", t, ubf[bi][:], f"ubf{bi}")
            transpose_tile_to_fm(p, lambda c, bi=bi: ubf[bi][:, c * 128:(c + 1) * 128],
                                 f"ubf{bi}", t, u_fm, "ufm", ptr, eye)
        p.barrier()
    psW = [p.ps("psW0", [128, 1024], F32), p.ps("psW1", [128, 1024], F32)]
    pq2 = p.ps("pq2", [128, 512], F32)
    ufm_all = fm_keys("ufm")

    with ExitStack() as scv:
        _oact = p.actx
        p.actx = scv
        wv_all = p.sb("wv_all", [128, NC * D], BF16)
        p.actx = _oact
        p.dma("gpsimd", "wvl", lambda e: e.dma_start(
            out=wv_all[:].rearrange("p (c f) -> p c f", c=NC),
            in_=wvT_d.ap().rearrange("(c p) f -> p c f", p=128)),
            outs=["wv_all"])
        v3o = v_all[:].rearrange("p (g c) -> p g c", c=130)
        p.op("vector", lambda e: e.memset(v3o[:, :, 64:65], 1.0), outs=["vo"])
        p.op("vector", lambda e: e.memset(v3o[:, :, 129:130], 1.0), outs=["vo"])
        for t in range(NT):
            for half in range(2):
                psel, kpsel = (pq, "pq") if half == 0 else (pq2, "pq2")

                def mmvq(e, t=t, half=half, psel=psel):
                    for c in range(NC):
                        e.matmul(psel[:],
                                 u_fm[:, c * T + t * 128: c * T + (t + 1) * 128],
                                 wv_all[:, c * D + half * 512: c * D + half * 512 + 512],
                                 start=(c == 0), stop=False)
                    return e.matmul(psel[:], ones[0:1, :],
                                    bvr[0:1, half * 512: (half + 1) * 512],
                                    start=False, stop=True)
                p.op("tensor", mmvq, ins=["wv_all", "bvr", "ones"] + ufm_all,
                     outs=[kpsel])
                v3 = v_all[:, t * 1040 + half * 520: t * 1040 + (half + 1) * 520] \
                    .rearrange("p (g c) -> p g c", c=130)
                p3 = psel[:].rearrange("p (g hc) -> p g hc", g=4)
                p.op("vector", lambda e, v3=v3, p3=p3: e.tensor_copy(
                    v3[:, :, 0:64], p3[:, :, 0:64]),
                    ins=[kpsel], outs=[f"va{t}:{half}"])
                p.op("vector", lambda e, v3=v3, p3=p3: e.tensor_copy(
                    v3[:, :, 65:129], p3[:, :, 64:128]),
                    ins=[kpsel], outs=[f"va{t}:{half}"])

    for hp in range(NHP):
        wb = hp % 2
        wkey = f"wqkv{wb}"
        for j, src in enumerate((wqT_d, wkT_d)):
            p.dma("gpsimd", f"wpair{wb}", lambda e, j=j, src=src, wb=wb, hp=hp:
                  e.dma_start(
                      out=wqkv[wb][:, j * D:(j + 1) * D],
                      in_=src.ap().rearrange("(c p) f -> p c f", p=128)
                      [:, :, hp * 128:(hp + 1) * 128]),
                  outs=[wkey + str(j)])
        qkeys = [f"qp{i}" for i in range(4)]
        kkeys = [f"kp{i}" for i in range(4)]
        for j, (bsb, bkey, dst, dks) in enumerate(
                ((bq, "bq", qp, "qp"), (bk, "bk", kp, "kp"))):
            for ts4 in range(4):
                sl = slice(ts4 * 512, (ts4 + 1) * 512)
                psel, kpsel = (pq, "pq") if ts4 % 2 == 0 else (pq2, "pq2")

                def mmq(e, j=j, sl=sl, wb=wb, psel=psel):
                    for c in range(NC):
                        inst = e.matmul(
                            psel[:], wqkv[wb][:, j * D + c * 128: j * D + c * 128 + 128],
                            u_fm[:, c * T:(c + 1) * T][:, sl],
                            start=(c == 0), stop=(c == NC - 1))
                    return inst
                p.op("tensor", mmq, ins=[wkey + str(j)] + ufm_all, outs=[kpsel])
                p.op("scalar", lambda e, sl=sl, bsb=bsb, hp=hp, dst=dst,
                     psel=psel: e.activation(
                    out=dst[:, sl], in_=psel[:], func=AF.Identity,
                    bias=bsb[:, hp:hp + 1]),
                    ins=[kpsel, bkey], outs=[dks + str(ts4)])
        prep_step(3, site=2)

        if layer == 0:
            for h2 in range(2):
                h = 2 * hp + h2
                dst = emask[:, h2 * WM:(h2 + 1) * WM]
                p.op("scalar", lambda e, dst=dst, h=h: e.activation(
                    out=dst, in_=mdiff[:], func=AF.Exp,
                    scale=ball[:, 2 * h: 2 * h + 1],
                    bias=ball[:, 32 + 2 * h: 32 + 2 * h + 1]),
                    ins=["mdiff", "ball"], outs=[f"emA{h2}"])
                for cs in range(0, WM, 1024):
                    cw = min(1024, WM - cs)
                    p.op("scalar", lambda e, h=h, cs=cs, cw=cw: e.activation(
                        out=etmp[:, 0:cw], in_=mdiff[:, cs:cs + cw], func=AF.Exp,
                        scale=ball[:, 2 * h + 1: 2 * h + 2],
                        bias=ball[:, 32 + 2 * h + 1: 32 + 2 * h + 2]),
                        ins=["mdiff", "ball", f"emsk{h2}:{cs}"], outs=["etmp"])
                    p.op("vector", lambda e, dst=dst, cs=cs, cw=cw: e.tensor_add(
                        dst[:, cs:cs + cw], dst[:, cs:cs + cw], etmp[:, 0:cw]),
                        ins=[f"emA{h2}", "etmp"], outs=[f"emask{h2}",
                                                        f"emsk{h2}:{cs}"])

        for qb in range(4):
            qsl = slice(qb * 512, (qb + 1) * 512)

            def scores(kc):
                par = kc % 2
                psx = psW[par]
                kA, kB = f"psW{par}h0", f"psW{par}h1"
                p.op("tensor", lambda e, kc=kc, psx=psx, qsl=qsl: e.matmul(
                    psx[:, 0:512], kp[0:64, kc * 128:(kc + 1) * 128], qp[0:64, qsl],
                    start=True, stop=True, tile_position=(0, 0)),
                    ins=qkeys + kkeys, outs=[kA])
                p.op("tensor", lambda e, kc=kc, psx=psx, qsl=qsl: e.matmul(
                    psx[:, 512:1024], kp[64:128, kc * 128:(kc + 1) * 128],
                    qp[64:128, qsl],
                    start=True, stop=True, tile_position=(64, 0)),
                    ins=qkeys + kkeys, outs=[kB])
                esl = slice(par * 1024, (par + 1) * 1024)
                ek = f"ebfp{par}"
                if layer == 0:
                    c0 = (T - 1) + qb * 512 - kc * 128
                    m3 = msk[:, esl].rearrange("p (h w) -> p h w", h=2)
                    p3 = psx[:].rearrange("p (h w) -> p h w", h=2)
                    e3 = emask[:].rearrange("p (h w) -> p h w", h=2)[:, :, c0:c0 + 512]
                    p.op("vector", lambda e, m3=m3, p3=p3, e3=e3: e.tensor_tensor(
                        m3, p3, e3, ALU.mult),
                        ins=[kA, kB, "emask0", "emask1"], outs=[f"mskp{par}"])
                    p.op("scalar", lambda e, esl=esl: e.activation(
                        out=ebf[:, esl], in_=msk[:, esl], func=AF.Exp),
                        ins=[f"mskp{par}"], outs=[ek])
                else:
                    p.op("scalar", lambda e, psx=psx, esl=esl: e.activation(
                        out=ebf[:, esl], in_=psx[:], func=AF.Exp,
                        scale=np.float32(0.125)),
                        ins=[kA, kB], outs=[ek])

            def pv(kc):
                par = kc % 2
                eo = par * 1024
                vb = kc * 1040 + hp * 130
                vkey = f"va{kc}:{hp // 4}"
                p.op("tensor", lambda e, kc=kc, eo=eo, vb=vb: e.matmul(
                    po0[0:65, :], v_all[:, vb: vb + 65],
                    ebf[:, eo: eo + 512],
                    start=(kc == 0), stop=(kc == NT - 1)),
                    ins=[f"ebfp{par}", vkey, "vo"], outs=["po0"])
                p.op("tensor", lambda e, kc=kc, eo=eo, vb=vb: e.matmul(
                    po1[0:65, :], v_all[:, vb + 65: vb + 130],
                    ebf[:, eo + 512: eo + 1024],
                    start=(kc == 0), stop=(kc == NT - 1)),
                    ins=[f"ebfp{par}", vkey, "vo"], outs=["po1"])

            for kc in range(NT + 1):
                if kc < NT:
                    scores(kc)
                if kc > 0:
                    pv(kc - 1)
            p.op("scalar", lambda e: e.activation(out=dsc[64:65, 0:512],
                                                  in_=po0[64:65, :], func=AF.Ln),
                 ins=["po0"], outs=["dscA"])
            p.op("scalar", lambda e: e.activation(out=dbf[64:65, 0:512],
                                                  in_=dsc[64:65, 0:512], func=AF.Exp,
                                                  scale=np.float32(-1.0)),
                 ins=["dscA"], outs=["dbfA"])
            p.op("scalar", lambda e, qb=qb: e.activation(out=asb[qb % 2][0:64, :],
                                                  in_=po0[0:64, :],
                                                  func=AF.Identity, bias=np.float32(0.0)),
                 ins=["po0"], outs=[f"asbA{qb % 2}"])
            p.op("tensor", lambda e: e.matmul(pq[0:64, :], ones[64:65, 0:64],
                                              dbf[64:65, 0:512],
                                              start=True, stop=True),
                 ins=["dbfA", "ones"], outs=["pq"])
            p.op("scalar", lambda e, qb=qb: e.activation(
                out=dnb[qb % 2][0:64, :], in_=pq[0:64, :],
                func=AF.Identity, bias=np.float32(0.0)),
                ins=["pq"], outs=[f"dnbA{qb % 2}"])
            p.op("vector", lambda e, hp=hp, qsl=qsl, qb=qb: e.tensor_mul(
                attn[0:64, hp * T: (hp + 1) * T][:, qsl], dnb[qb % 2][0:64, :],
                asb[qb % 2][0:64, :]),
                ins=[f"dnbA{qb % 2}", f"asbA{qb % 2}"], outs=[f"attn:{hp}:0:{qb}"])
            p.op("scalar", lambda e: e.activation(out=dsc[0:1, 0:512],
                                                  in_=po1[64:65, :], func=AF.Ln),
                 ins=["po1"], outs=["dscB"])
            p.op("scalar", lambda e: e.activation(out=dbf[0:1, 0:512],
                                                  in_=dsc[0:1, 0:512],
                                                  func=AF.Exp,
                                                  scale=np.float32(-1.0)),
                 ins=["dscB"], outs=["dbfB"])
            p.op("scalar", lambda e, qb=qb: e.activation(out=asb[qb % 2][64:128, :],
                                                  in_=po1[0:64, :],
                                                  func=AF.Identity, bias=np.float32(0.0)),
                 ins=["po1"], outs=[f"asbB{qb % 2}"])
            p.op("tensor", lambda e: e.matmul(pq2[64:128, :], ones[0:1, 0:64],
                                              dbf[0:1, 0:512],
                                              start=True, stop=True),
                 ins=["dbfB", "ones"], outs=["pq2"])
            p.op("scalar", lambda e, qb=qb: e.activation(
                out=dnb[qb % 2][64:128, :], in_=pq2[64:128, :],
                func=AF.Identity, bias=np.float32(0.0)),
                ins=["pq2"], outs=[f"dnbB{qb % 2}"])
            p.op("vector", lambda e, hp=hp, qsl=qsl, qb=qb: e.tensor_mul(
                attn[64:128, hp * T: (hp + 1) * T][:, qsl],
                dnb[qb % 2][64:128, :], asb[qb % 2][64:128, :]),
                ins=[f"dnbB{qb % 2}", f"asbB{qb % 2}"], outs=[f"attn:{hp}:1:{qb}"])
            prep_step(4)

    attn_all = [f"attn:{g}:{h2}:{qb}" for g in range(NHP) for h2 in range(2)
                for qb in range(4)]
    p.barrier()
    sc_mask.close()
    wo = p.sb("wo", [128, NC * D], BF16)
    xtw = [p.sb(f"xtw{i}", [128, D], F32) for i in range(2)]
    p.dma("gpsimd", "c_wo", lambda e: e.dma_start(
        out=wo[:].rearrange("p (c f) -> p c f", c=NC),
        in_=woT_d.ap().rearrange("(c p) f -> p c f", p=128)),
        outs=["wo"])
    if layer == 1:
        gb_b = p.sb("gb_b", [128, 2 * D], F32)
        u32 = [p.sb("u32_s", [128, D], F32)] * 2
        ysb = p.sb("ysb", [128, D], F32)
        p.dma("sync", "c_gb", lambda e: e.dma_start(out=gb_b[0:1, :],
                                                    in_=gb_d.ap()),
              outs=["gbrow"])
        for half in range(2):
            for fs in range(0, D, 512):
                o = half * D + fs
                p.op("tensor", lambda e, o=o: e.matmul(
                    pq[:, 0:512], ones32[0:1, :], gb_b[0:1, o:o + 512],
                    start=True, stop=True),
                    ins=["gbrow", "ones32"], outs=["pq"])
                p.op("scalar", lambda e, o=o: e.activation(
                    out=gb_b[:, o:o + 512], in_=pq[:, 0:512],
                    func=AF.Identity, bias=np.float32(0.0)),
                    ins=["pq"], outs=["gb_b"])
    if "dbg_attn" in dram:
        p.dma("sync", "dbg1", lambda e: e.dma_start(out=dram["dbg_attn"].ap(),
                                                    in_=attn[:]), ins=attn_all,
              outs=["dbgA"])
        p.dma("sync", "dbg2", lambda e: e.dma_start(out=dram["dbg_qp"].ap(),
                                                    in_=qp[:]),
              ins=[f"qp{i}" for i in range(4)], outs=["dbgQ"])
        p.dma("sync", "dbg3", lambda e: e.dma_start(out=dram["dbg_kp"].ap(),
                                                    in_=kp[:]),
              ins=[f"kp{i}" for i in range(4)], outs=["dbgK"])
        p.dma("sync", "dbg4", lambda e: e.dma_start(out=dram["dbg_vaug"].ap(),
                                                    in_=v_all[:, 0:NT * 130]),
              outs=["dbgV"])

    for t in range(NT):
        bi = t % 2
        p.dma("sync", f"x{bi}", lambda e, t=t, bi=bi: e.dma_start(
            out=xtw[bi][:], in_=x_d.ap()[t * 128:(t + 1) * 128, :]),
            outs=[f"xt{bi}"])
        for fs in range(2):
            fsl = slice(fs * 512, (fs + 1) * 512)
            psx, kpsx = (psW[0][:, 0:512], "psW0h0") if fs == 0 else (
                psW[0][:, 512:1024], "psW0h1")

            def mwo(e, t=t, fsl=fsl, psx=psx):
                for g in range(NHP):
                    e.matmul(psx,
                             attn[:, g * T + t * 128: g * T + (t + 1) * 128],
                             wo[:, g * D:(g + 1) * D][:, fsl],
                             start=(g == 0), stop=False)
                return e.matmul(psx, ones[0:1, :], bor[0:1, fsl],
                                start=False, stop=True)
            p.op("tensor", mwo, ins=attn_all + ["wo", "bor", "ones"], outs=[kpsx])
        if layer == 0:
            for fs in range(2):
                fsl = slice(fs * 512, (fs + 1) * 512)
                psx, kpsx = (psW[0][:, 0:512], "psW0h0") if fs == 0 else (
                    psW[0][:, 512:1024], "psW0h1")
                p.op("vector", lambda e, bi=bi, fsl=fsl, psx=psx: e.tensor_add(
                    xtw[bi][:, fsl], psx, xtw[bi][:, fsl]),
                    ins=[kpsx, f"xt{bi}"], outs=[f"xt{bi}"])
        else:
            ln_stats_and_u(p, xtw[bi][:], f"xt{bi}", sq_scr[:], "sq_scr",
                           stats2, "wt", bi, u32[bi][:], "u32s")
            for fs in range(2):
                fsl = slice(fs * 512, (fs + 1) * 512)
                psx, kpsx = (psW[0][:, 0:512], "psW0h0") if fs == 0 else (
                    psW[0][:, 512:1024], "psW0h1")
                p.op("vector", lambda e, bi=bi, fsl=fsl, psx=psx: e.tensor_add(
                    ysb[:, fsl], psx, u32[bi][:, fsl]),
                    ins=[kpsx, "u32s"], outs=["ysb"])
            ln_stats_and_u(p, ysb[:], "ysb", sq_scr[:], "sq_scr",
                           stats3, "yt", bi, ysb[:], "ysb")
            p.op("vector", lambda e: e.tensor_mul(ysb[:], ysb[:], gb_b[:, 0:D]),
                 ins=["ysb", "gb_b"], outs=["ysb"])
            p.op("vector", lambda e: e.tensor_add(ysb[:], ysb[:], gb_b[:, D:2 * D]),
                 ins=["ysb", "gb_b"], outs=["ysb"])
            p.op("vector", lambda e, bi=bi: e.tensor_add(xtw[bi][:], ysb[:],
                                                         xtw[bi][:]),
                 ins=["ysb", f"xt{bi}"], outs=[f"xt{bi}"])
        p.dma("sync", "st", lambda e, t=t, bi=bi: e.dma_start(
            out=y_d.ap()[t * 128:(t + 1) * 128, :], in_=xtw[bi][:]),
            ins=[f"xt{bi}"], outs=[f"y{t}"])


def build_attn(layer):
    nc = bass.Bass("TRN2", target_bir_lowering=False)
    ctx = ExitStack()
    x_d = nc.dram_tensor("x", [T, D], F32, kind="ExternalInput")
    y_d = nc.dram_tensor("y", [T, D], F32, kind="ExternalOutput")
    eye_d = nc.inline_tensor(np.eye(128, dtype=BFNP), name="eye_bf")
    dram = declare_attn_dram(nc, layer, "", x_d, y_d, eye_d)
    import os as _os
    if _os.environ.get("DBG_ATTN") == "1":
        dram["dbg_attn"] = nc.dram_tensor("dbg_attn", [128, NHP * T], BF16,
                                          kind="ExternalOutput")
        dram["dbg_qp"] = nc.dram_tensor("dbg_qp", [128, T], BF16,
                                        kind="ExternalOutput")
        dram["dbg_kp"] = nc.dram_tensor("dbg_kp", [128, T], BF16,
                                        kind="ExternalOutput")
        dram["dbg_vaug"] = nc.dram_tensor("dbg_vaug", [128, NT * 130], BF16,
                                          kind="ExternalOutput")
    with ctx:
        p = Prog(nc, ctx)
        emit_attn(p, dram, layer)
        p.barrier()
        p.emit()
    return nc



def declare_ffn0_dram(nc, pre, x_d, y_d, eye_d):
    return {"x": x_d, "y": y_d, "eye": eye_d,
            "w": nc.dram_tensor(pre + "w", [D, D], F32, kind="ExternalInput"),
            "b_row": nc.dram_tensor(pre + "b_row", [1, D], F32, kind="ExternalInput"),
            "stg": nc.dram_tensor(pre + "stg", [D, D], BF16, kind="Internal")}


def emit_ffn0(p, dram, prep_done=False):
    nc = p.nc
    x_d, y_d, eye_d = dram["x"], dram["y"], dram["eye"]
    b_d = dram["b_row"]
    stg_d = dram["stg"]
    p.mh = p.sb("mh", [128, 1], F32)
    p.op("gpsimd", lambda e: e.memset(p.mh[:], -0.5), outs=["mh"])
    if not prep_done:
        _oact = p.actx
        with ExitStack() as sw:
            p.actx = sw
            scrW, wchW, wqbW = alloc_prep_bufs(p)
            p.actx = _oact
            for _ in gen_wprep(p, dram["w"], stg_d, scrW, wchW, wqbW, 6, NC, D,
                               1024):
                pass
            p.barrier()
    wqT = p.sb("wqT", [128, NC * D], BF16)
    br = p.sb("br", [1, D], BF16)
    eye = p.sb("eye", [128, 128], BF16)
    ones = p.sb("ones", [1, 128], BF16)
    stats = p.sb("stats", [128, 4 * 2], F32)
    xt = [p.sb(f"xt{i}", [128, D], F32) for i in range(2)]
    xtw = [p.sb(f"xtw{i}", [128, D], F32) for i in range(2)]
    u32 = [p.sb(f"u32_{i}", [128, D], F32) for i in range(2)]
    xqb = [p.sb(f"xqb{i}", [128, D], BF16) for i in range(2)]
    xqf = [p.sb(f"xqf{i}", [128, D], BF16) for i in range(4)]
    sq_scr = p.sb("sq_scr", [128, D], F32)
    scr = {k: p.sb("qs_" + k, [128, D], F32) for k in ("y", "u", "a", "b", "s")}
    scr["bm"] = p.sb("qs_bm", [128, 3 * 32], F32)
    ptr = [p.ps(f"ptr{i}", [128, 128], BF16) for i in range(2)]
    pd = [p.ps(f"pd{i}", [128, 512], F32) for i in range(4)]

    p.dma("sync", "c_eye", lambda e: e.dma_start(out=eye[:], in_=eye_d.ap()),
          outs=["eye"])
    p.dma("gpsimd", "c_b", lambda e: e.dma_start(out=br[:], in_=b_d.ap()),
          outs=["br"])
    p.op("vector", lambda e: e.memset(ones[:], 1.0), outs=["ones"])

    load_wT_staged(p, stg_d, wqT, "wqT", NC, D)
    wqT_all = [f"wqT:{ec}" for ec in range(NC)]

    def front(t):
        bi, ri = t % 2, t % 4
        p.dma("sync", f"x{bi}", lambda e, t=t, bi=bi: e.dma_start(
            out=xt[bi][:], in_=x_d.ap()[t * 128:(t + 1) * 128, :]),
            outs=[f"xt{bi}"])
        ln_stats_and_u(p, xt[bi][:], f"xt{bi}", sq_scr[:], "sq_scr",
                       stats, "st", bi, u32[bi][:], "u32s", rstd_via="pool")
        emit_quant_fp6(p, u32[bi][:], "u32s", xqb[bi][:], "xqbs",
                       scr, "qs", 32)
        for ec in range(NC):
            pk = "ptr" + str(ec % 2)
            p.op("tensor", lambda e, ec=ec, bi=bi: e.transpose(
                ptr[ec % 2][:], xqb[bi][:, ec * 128:(ec + 1) * 128], eye[:]),
                ins=["xqbs", "eye"], outs=[pk])
            p.op("scalar", lambda e, ec=ec, ri=ri: e.activation(
                out=xqf[ri][:, ec * 128:(ec + 1) * 128],
                in_=ptr[ec % 2][:], func=AF.Identity, bias=np.float32(0.0)),
                ins=[pk], outs=[f"xqf{ri}_{ec}"])

    def back_mm(t):
        bi, ri = t % 2, t % 4
        xqf_all = [f"xqf{ri}_{ec}" for ec in range(NC)]
        p.dma("gpsimd", f"xw{bi}", lambda e, t=t, bi=bi: e.dma_start(
            out=xtw[bi][:], in_=x_d.ap()[t * 128:(t + 1) * 128, :]),
            outs=[f"xtw{bi}"])
        for fs in range(2):
            fsl = slice(fs * 512, (fs + 1) * 512)
            psx, kpsx = pd[(t % 2) * 2 + fs], f"pd{(t % 2) * 2 + fs}"

            def mmd(e, fsl=fsl, psx=psx, ri=ri):
                for ec in range(NC):
                    e.matmul(psx[:], xqf[ri][:, ec * 128:(ec + 1) * 128],
                             wqT[:, ec * D:(ec + 1) * D][:, fsl],
                             start=(ec == 0), stop=False)
                return e.matmul(psx[:], ones[0:1, :], br[0:1, fsl],
                                start=False, stop=True)
            p.op("tensor", mmd, ins=xqf_all + wqT_all + ["br", "ones"], outs=[kpsx])

    def back_fin(t):
        bi = t % 2
        for fs in range(2):
            fsl = slice(fs * 512, (fs + 1) * 512)
            psx, kpsx = pd[(t % 2) * 2 + fs], f"pd{(t % 2) * 2 + fs}"
            p.op("vector", lambda e, bi=bi, fsl=fsl, psx=psx: e.tensor_add(
                xtw[bi][:, fsl], psx[:], xtw[bi][:, fsl]),
                ins=[kpsx, f"xtw{bi}"], outs=[f"xtw{bi}"])
        p.dma("sync", "st", lambda e, t=t, bi=bi: e.dma_start(
            out=y_d.ap()[t * 128:(t + 1) * 128, :], in_=xtw[bi][:]),
            ins=[f"xtw{bi}"], outs=[f"y{t}"])

    front(0)
    front(1)
    for t in range(NT):
        back_mm(t)
        if t + 2 < NT:
            front(t + 2)
        back_fin(t)


def build_ffn0():
    nc = bass.Bass("TRN2", target_bir_lowering=False)
    ctx = ExitStack()
    x_d = nc.dram_tensor("x", [T, D], F32, kind="ExternalInput")
    y_d = nc.dram_tensor("y", [T, D], F32, kind="ExternalOutput")
    eye_d = nc.inline_tensor(np.eye(128, dtype=BFNP), name="eye_bf")
    dram = declare_ffn0_dram(nc, "", x_d, y_d, eye_d)
    with ctx:
        p = Prog(nc, ctx)
        emit_ffn0(p, dram)
        p.barrier()
        p.emit()
    return nc



NI = 22
I_DIM = 2816


def declare_ffn1_dram(nc, pre, x_d, y_d, eye_d):
    return {"x": x_d, "y": y_d, "eye": eye_d,
            "wg": nc.dram_tensor(pre + "wg", [I_DIM, D], F32, kind="ExternalInput"),
            "wu": nc.dram_tensor(pre + "wu", [I_DIM, D], F32, kind="ExternalInput"),
            "wd": nc.dram_tensor(pre + "wd", [D, I_DIM], F32, kind="ExternalInput"),
            "stg_g": nc.dram_tensor(pre + "stg_g", [I_DIM, D], BF16, kind="Internal"),
            "stg_u": nc.dram_tensor(pre + "stg_u", [I_DIM, D], BF16, kind="Internal"),
            "stg_dn": nc.dram_tensor(pre + "stg_dn", [D, I_DIM], BF16, kind="Internal")}


def emit_ffn1(p, dram, prep_done=False):
    nc = p.nc
    x_d, y_d, eye_d = dram["x"], dram["y"], dram["eye"]
    stg_g, stg_u, stg_dn = dram["stg_g"], dram["stg_u"], dram["stg_dn"]
    HF = I_DIM // 2
    p.mh = p.sb("mh", [128, 1], F32)
    p.op("gpsimd", lambda e: e.memset(p.mh[:], -0.5), outs=["mh"])
    if not prep_done:
        _oact = p.actx
        with ExitStack() as sw:
            p.actx = sw
            scrW, wchW, wqbW = alloc_prep_bufs(p)
            p.actx = _oact
            for g in f1_prep_gens(p, dram, scrW, wchW, wqbW):
                for _ in g:
                    pass
            p.barrier()
    wgT = p.sb("wgT", [128, NC * I_DIM], BF16)
    wuT = p.sb("wuT", [128, NC * I_DIM], BF16)
    wdT = p.sb("wdT", [128, NI * D], BF16)
    eye = p.sb("eye", [128, 128], BF16)
    stats = p.sb("stats", [128, 4 * 2], F32)
    xt = [p.sb(f"xt{i}", [128, D], F32) for i in range(3)]
    u32 = [p.sb("u32_s", [128, D], F32)] * 2
    xqb = [p.sb("xqb_s", [128, D], BF16)] * 2
    xqf = [p.sb(f"xqf{i}", [128, D], BF16) for i in range(3)]
    gs = [p.sb(f"gs{i}", [128, I_DIM], F32) for i in range(2)]
    upe = p.sb("upe", [128, 2 * 512], F32)
    hq = [p.sb("hq_s", [128, I_DIM], BF16)] * 2
    hqT = p.sb("hqT", [128, I_DIM], BF16)
    scr = {k: p.sb("qs_" + k, [128, 512], F32) for k in ("y", "u", "a", "b")}
    scr["bm"] = p.sb("qs_bm", [128, 3 * 64], F32)
    ptr = [p.ps(f"ptr{i}", [128, 128], BF16) for i in range(2)]
    pg = [p.ps("pg0", [128, 512], F32), p.ps("pg1", [128, 512], F32)]
    pu = [p.ps("pu0", [128, 512], F32), p.ps("pu1", [128, 512], F32)]
    pd0 = p.ps("pd0", [128, 512], F32)
    pd1 = p.ps("pd1", [128, 512], F32)

    p.dma("sync", "c_eye", lambda e: e.dma_start(out=eye[:], in_=eye_d.ap()),
          outs=["eye"])
    load_wT_staged(p, stg_g, wgT, "wgT", NC, I_DIM)
    load_wT_staged(p, stg_u, wuT, "wuT", NC, I_DIM)
    load_wT_staged(p, stg_dn, wdT, "wdT", NI, D)
    wgT_all = [f"wgT:{ec}" for ec in range(NC)]
    wuT_all = [f"wuT:{ec}" for ec in range(NC)]
    wdT_all = [f"wdT:{ic}" for ic in range(NI)]

    ISL = [(k * 512, min((k + 1) * 512, I_DIM)) for k in range(6)]

    def front(t):
        bi, ri = t % 2, t % 3
        p.dma("sync", f"x{ri}", lambda e, t=t, ri=ri: e.dma_start(
            out=xt[ri][:], in_=x_d.ap()[t * 128:(t + 1) * 128, :]),
            outs=[f"xt{ri}"])
        ln_stats_and_u(p, xt[ri][:], f"xt{ri}", u32[bi][:], "u32s",
                       stats, "st", bi, u32[bi][:], "u32s", rstd_via="pool")
        for h in range(2):
            emit_quant_fp4(p, u32[bi][:, h * 512:(h + 1) * 512], "u32s",
                           xqb[bi][:, h * 512:(h + 1) * 512], "xqbs",
                           scr, "qs", 32)
        for ec in range(NC):
            pk = "ptr" + str(ec % 2)
            p.op("tensor", lambda e, ec=ec, bi=bi: e.transpose(
                ptr[ec % 2][:], xqb[bi][:, ec * 128:(ec + 1) * 128], eye[:]),
                ins=["xqbs", "eye"], outs=[pk])
            p.op("scalar", lambda e, ec=ec, ri=ri: e.activation(
                out=xqf[ri][:, ec * 128:(ec + 1) * 128],
                in_=ptr[ec % 2][:], func=AF.Identity, bias=np.float32(0.0)),
                ins=[pk], outs=[f"xqf{ri}_{ec}"])

    def back_mid(t):
        bi, ri = t % 2, t % 3
        xqf_all = [f"xqf{ri}_{ec}" for ec in range(NC)]
        for k, (i0, i1) in enumerate(ISL):
            isl = slice(i0, i1)
            pgk, kpg = pg[k % 2], f"pg{k % 2}"
            puk, kpu = pu[k % 2], f"pu{k % 2}"

            def mmg(e, isl=isl, i0=i0, i1=i1, ri=ri, pgk=pgk):
                for ec in range(NC):
                    inst = e.matmul(pgk[:, 0:i1 - i0],
                                    xqf[ri][:, ec * 128:(ec + 1) * 128],
                                    wgT[:, ec * I_DIM:(ec + 1) * I_DIM][:, isl],
                                    start=(ec == 0), stop=(ec == NC - 1))
                return inst
            p.op("tensor", mmg, ins=xqf_all + wgT_all, outs=[kpg])
            p.op("scalar", lambda e, isl=isl, i0=i0, i1=i1, bi=bi,
                 pgk=pgk: e.activation(
                out=gs[bi][:, isl], in_=pgk[:, 0:i1 - i0], func=AF.Silu),
                ins=[kpg], outs=[f"gs{bi}:{k}"])

            def mmu(e, isl=isl, i0=i0, i1=i1, ri=ri, puk=puk):
                for ec in range(NC):
                    inst = e.matmul(puk[:, 0:i1 - i0],
                                    xqf[ri][:, ec * 128:(ec + 1) * 128],
                                    wuT[:, ec * I_DIM:(ec + 1) * I_DIM][:, isl],
                                    start=(ec == 0), stop=(ec == NC - 1))
                return inst
            p.op("tensor", mmu, ins=xqf_all + wuT_all, outs=[kpu])
            usl = slice((k % 2) * 512, (k % 2) * 512 + (i1 - i0))
            p.op("scalar", lambda e, usl=usl, i0=i0, i1=i1, puk=puk: e.activation(
                out=upe[:, usl], in_=puk[:, 0:i1 - i0], func=AF.Identity,
                bias=np.float32(0.0)),
                ins=[kpu], outs=[f"upe{k % 2}"])
            p.op("gpsimd", lambda e, isl=isl, usl=usl, bi=bi: e.tensor_tensor(
                gs[bi][:, isl], upe[:, usl], gs[bi][:, isl], ALU.mult),
                ins=[f"upe{k % 2}", f"gs{bi}:{k}"], outs=[f"gs{bi}:{k}"])
            emit_quant_fp4(p, gs[bi][:, isl], [f"gs{bi}:{k}"],
                           hq[bi][:, isl], f"hqs_{k}", scr, "qs",
                           (i1 - i0) // 16)

    def back_down(t):
        bi, ri = t % 2, t % 3
        for ic in range(NI):
            pk = "ptr" + str(ic % 2)
            hqk = f"hqs_{min(ic // 4, 5)}"
            p.op("tensor", lambda e, ic=ic, bi=bi: e.transpose(
                ptr[ic % 2][:], hq[bi][:, ic * 128:(ic + 1) * 128], eye[:]),
                ins=[hqk, "eye"], outs=[pk])
            p.op("scalar", lambda e, ic=ic: e.activation(
                out=hqT[:, ic * 128:(ic + 1) * 128],
                in_=ptr[ic % 2][:], func=AF.Identity, bias=np.float32(0.0)),
                ins=[pk], outs=[f"hqT{ic}"])
        hqT_all = [f"hqT{ic}" for ic in range(NI)]
        for fs in range(2):
            fsl = slice(fs * 512, (fs + 1) * 512)
            psx, kpsx = (pd0, "pd0") if fs == 0 else (pd1, "pd1")

            def mmd(e, fsl=fsl, psx=psx):
                for ic in range(NI):
                    inst = e.matmul(psx[:], hqT[:, ic * 128:(ic + 1) * 128],
                                    wdT[:, ic * D:(ic + 1) * D][:, fsl],
                                    start=(ic == 0), stop=(ic == NI - 1))
                return inst
            p.op("tensor", mmd, ins=hqT_all + wdT_all, outs=[kpsx])
            p.op("vector", lambda e, ri=ri, fsl=fsl, psx=psx: e.tensor_add(
                xt[ri][:, fsl], psx[:], xt[ri][:, fsl]),
                ins=[kpsx, f"xt{ri}"], outs=[f"xt{ri}"])
        p.dma("sync", "st", lambda e, t=t, ri=ri: e.dma_start(
            out=y_d.ap()[t * 128:(t + 1) * 128, :], in_=xt[ri][:]),
            ins=[f"xt{ri}"], outs=[f"y{t}"])

    front(0)
    front(1)
    for t in range(NT):
        back_mid(t)
        if t + 2 < NT:
            front(t + 2)
        back_down(t)


def build_ffn1():
    nc = bass.Bass("TRN2", target_bir_lowering=False)
    ctx = ExitStack()
    x_d = nc.dram_tensor("x", [T, D], F32, kind="ExternalInput")
    y_d = nc.dram_tensor("y", [T, D], F32, kind="ExternalOutput")
    eye_d = nc.inline_tensor(np.eye(128, dtype=BFNP), name="eye_bf")
    dram = declare_ffn1_dram(nc, "", x_d, y_d, eye_d)
    with ctx:
        p = Prog(nc, ctx)
        emit_ffn1(p, dram)
        p.barrier()
        p.emit()
    return nc



def _mdiff_np():
    c = np.arange(WM, dtype=np.float32)[None, :]
    i = np.arange(128, dtype=np.float32)[:, None]
    return np.ascontiguousarray(np.abs(c - (T - 1) - i).astype(np.float32))


def _attn_host_inputs(Wq, bq, Wk, bk, Wv, bv, Wo, bo):
    d = {}
    d["wqT"] = np.ascontiguousarray(Wq.T)
    d["wkT"] = np.ascontiguousarray(Wk.T)
    d["wvT"] = np.ascontiguousarray(Wv.T)
    d["woT"] = np.ascontiguousarray(Wo.T)
    d["bq_pc"] = np.ascontiguousarray(bq.reshape(NC, 128).T)
    d["bk_pc"] = np.ascontiguousarray(bk.reshape(NC, 128).T)
    d["bv_row"] = np.ascontiguousarray(bv.reshape(1, D))
    d["bo_row"] = np.ascontiguousarray(bo.reshape(1, D))
    return d


_progs = {}


def _get_prog(name, builder):
    if name not in _progs:
        _progs[name] = builder()
    return _progs[name]


def build_fused():
    nc = bass.Bass("TRN2", target_bir_lowering=False)
    ctx = ExitStack()
    x0 = nc.dram_tensor("x", [T, D], F32, kind="ExternalInput")
    y = nc.dram_tensor("y", [T, D], F32, kind="ExternalOutput")
    x1 = nc.dram_tensor("xs1", [T, D], F32, kind="Internal")
    x2 = nc.dram_tensor("xs2", [T, D], F32, kind="Internal")
    x3 = nc.dram_tensor("xs3", [T, D], F32, kind="Internal")
    eye_d = nc.inline_tensor(np.eye(128, dtype=BFNP), name="eye_bf")
    dA = declare_attn_dram(nc, 0, "a0_", x0, x1, eye_d)
    dB = declare_ffn0_dram(nc, "f0_", x1, x2, eye_d)
    dC = declare_attn_dram(nc, 1, "a1_", x2, x3, eye_d)
    dD = declare_ffn1_dram(nc, "f1_", x3, y, eye_d)
    with ctx:
        p = Prog(nc, ctx)
        with ExitStack() as prepctx0:
            p.pre, p.actx = "f0p_", prepctx0
            bufs0 = alloc_prep_bufs(p, slab=256, nm="p0_")
            gen0 = gen_wprep(p, dB["w"], dB["stg"], *bufs0, 6, NC, D, 256,
                             tag="f0_")
            with ExitStack() as actx:
                p.pre, p.actx = "a0_", actx
                emit_attn(p, dA, 0, prep2=gen0)
                for _ in gen0:
                    pass
                p.barrier()
        with ExitStack() as actx:
            p.pre, p.actx = "f0_", actx
            emit_ffn0(p, dB, prep_done=True)
            p.barrier()
        with ExitStack() as prepctx:
            p.pre, p.actx = "f1_", prepctx
            scrW, wchW, wqbW = alloc_prep_bufs(p, slab=512)
            gen = itertools.chain(*f1_prep_gens(p, dD, scrW, wchW, wqbW,
                                                slab_gu=512, slab_d=352))
            with ExitStack() as actx:
                p.pre, p.actx = "a1_", actx
                emit_attn(p, dC, 1, prep=gen)
                for _ in gen:
                    pass
                p.barrier()
        with ExitStack() as actx:
            p.pre, p.actx = "f1_", actx
            emit_ffn1(p, dD, prep_done=True)
            p.barrier()
        p.pre, p.actx = "", ctx
        p.emit()
    return nc


def kernel(L0_attn_x, L0_attn_Wq, L0_attn_bq, L0_attn_Wk, L0_attn_bk,
           L0_attn_Wv, L0_attn_bv, L0_attn_Wo, L0_attn_bo,
           L0_attn_decay_params, L0_attn_mix_weights,
           L0_ffn_weight, L0_ffn_bias,
           L1_attn_Wq, L1_attn_bq, L1_attn_Wk, L1_attn_bk, L1_attn_Wv,
           L1_attn_bv, L1_attn_Wo, L1_attn_bo, L1_attn_ln_gamma,
           L1_attn_ln_beta, L1_ffn_w_gate, L1_ffn_w_up, L1_ffn_w_down):
    ncore = 8
    f32 = np.float32
    x = np.ascontiguousarray(np.asarray(L0_attn_x, dtype=f32))
    cores = list(range(ncore))

    ins = {}
    for pre, args in (
            ("a0_", (L0_attn_Wq, L0_attn_bq, L0_attn_Wk, L0_attn_bk,
                     L0_attn_Wv, L0_attn_bv, L0_attn_Wo, L0_attn_bo)),
            ("a1_", (L1_attn_Wq, L1_attn_bq, L1_attn_Wk, L1_attn_bk,
                     L1_attn_Wv, L1_attn_bv, L1_attn_Wo, L1_attn_bo))):
        h = _attn_host_inputs(*[np.asarray(a, f32) for a in args])
        ins.update({pre + k: v for k, v in h.items()})
    ins["a0_dpmix"] = np.ascontiguousarray(np.concatenate(
        [np.asarray(L0_attn_decay_params, f32).reshape(1, 32),
         np.asarray(L0_attn_mix_weights, f32).reshape(1, 32)], axis=1))
    ins["a0_mdiff"] = _mdiff_np()
    ins["a1_gb"] = np.ascontiguousarray(np.concatenate(
        [np.asarray(L1_attn_ln_gamma, f32),
         np.asarray(L1_attn_ln_beta, f32)]).reshape(1, 2 * D))
    ins["f0_w"] = np.ascontiguousarray(np.asarray(L0_ffn_weight, f32))
    ins["f0_b_row"] = np.ascontiguousarray(np.asarray(L0_ffn_bias, f32).reshape(1, D))
    ins["f1_wg"] = np.ascontiguousarray(np.asarray(L1_ffn_w_gate, f32))
    ins["f1_wu"] = np.ascontiguousarray(np.asarray(L1_ffn_w_up, f32))
    ins["f1_wd"] = np.ascontiguousarray(np.asarray(L1_ffn_w_down, f32))

    ncF = _get_prog("fused", build_fused)
    res = run_bass_kernel_spmd(
        ncF, [dict(ins, x=np.ascontiguousarray(x[b])) for b in cores],
        core_ids=cores)
    return np.stack([np.asarray(res.results[b]["y"]) for b in cores],
                    axis=0).astype(np.float32)


# version 23 (final)
# speedup vs baseline: 1.0352x; 1.0246x over previous
import itertools
import numpy as np
import ml_dtypes
from contextlib import ExitStack

import concourse.bass as bass
import concourse.mybir as mybir
from concourse.bass_utils import run_bass_kernel_spmd

F32 = mybir.dt.float32
BF16 = mybir.dt.bfloat16
F16 = mybir.dt.float16
AF = mybir.ActivationFunctionType
ALU = mybir.AluOpType
AX = mybir.AxisListType
BFNP = ml_dtypes.bfloat16

T = 2048
D = 1024
NT = T // 128
NC = D // 128
NH = 16
HD = 64
NHP = 8
WM = 2 * T - 1
LNEPS = 1e-5

ENGINES = ("sync", "scalar", "vector", "tensor", "gpsimd")


class Prog:

    def __init__(self, nc, ctx):
        self.nc, self.ctx = nc, ctx
        self.pre = ""
        self.actx = ctx
        self.sem = {e: ctx.enter_context(nc.semaphore("S_" + e)) for e in ENGINES}
        self.cnt = {e: 0 for e in ENGINES}
        self.dsem = {}
        self.dcnt = {}
        self.writer = {}
        self.readers = {}
        self.waited = {e: {} for e in ENGINES}
        self.recs = {e: [] for e in ENGINES}

    def chan(self, name):
        if name not in self.dsem:
            self.dsem[name] = self.ctx.enter_context(self.nc.semaphore("D_" + name))
            self.dcnt[name] = 0
        return name

    def _need(self, eng, ins, outs):
        need = []
        for k in ins:
            w = self.writer.get(k)
            if w:
                need.append(w)
        for k in outs:
            w = self.writer.get(k)
            if w:
                need.append(w)
            need += self.readers.get(k, ())
        best = {}
        for typ, a, c in need:
            if best.get((typ, a), -1) < c:
                best[(typ, a)] = c
        waits = []
        wt = self.waited[eng]
        for (typ, a), c in sorted(best.items()):
            if typ == "d":
                c = self.dcnt[a]
            if wt.get((typ, a), 0) >= c:
                continue
            wt[(typ, a)] = c
            waits.append((typ, a, c))
        return waits

    def _book(self, me, ins, outs):
        for k in outs:
            self.writer[k] = me
            self.readers[k] = []
        for k in ins:
            self.readers.setdefault(k, []).append(me)

    def op(self, eng, fn, ins=(), outs=()):
        waits = self._need(eng, ins, outs)
        self.cnt[eng] += 1
        self._book(("e", eng, self.cnt[eng]), ins, outs)
        self.recs[eng].append((waits, fn, None))

    def dma(self, eng, ch, fn, ins=(), outs=()):
        ch = self.chan(ch)
        waits = self._need(eng, ins, outs)
        self.dcnt[ch] += 1
        self._book(("d", ch, self.dcnt[ch]), ins, outs)
        self.recs[eng].append((waits, fn, ch))

    def barrier(self):
        snap_e = dict(self.cnt)
        snap_d = dict(self.dcnt)
        for e in ENGINES:
            waits = []
            wt = self.waited[e]
            for e2 in ENGINES:
                c = snap_e[e2]
                if e2 != e and c > 0 and wt.get(("e", e2), 0) < c:
                    wt[("e", e2)] = c
                    waits.append(("e", e2, c))
            for ch, c in snap_d.items():
                if c > 0 and wt.get(("d", ch), 0) < c:
                    wt[("d", ch)] = c
                    waits.append(("d", ch, c))
            if waits:
                self.recs[e].append((waits, None, None))
        self.writer = {}
        self.readers = {}

    def emit(self):
        block = self.ctx.enter_context(self.nc.Block())
        for eng in ENGINES:
            recs = self.recs[eng]

            def mk(recs_, eng_):
                def body(e):
                    for waits, fn, ch in recs_:
                        for typ, a, c in waits:
                            if typ == "e":
                                e.wait_ge(self.sem[a], c)
                            else:
                                e.wait_ge(self.dsem[a], 16 * c)
                        if fn is None:
                            continue
                        inst = fn(e)
                        if ch is None:
                            inst.then_inc(self.sem[eng_], 1)
                        else:
                            inst.then_inc(self.dsem[ch], 16)
                return body

            getattr(block, eng)(mk(recs, eng))

    def sb(self, name, shape, dtype):
        return self.actx.enter_context(
            self.nc.sbuf_tensor(self.pre + name, shape, dtype))

    def ps(self, name, shape, dtype=F32):
        return self.actx.enter_context(
            self.nc.psum_tensor(self.pre + name, shape, dtype))



def ln_stats_and_u(p, xt_ap, xkey, sq_scr, sqkey, stats, stkey, col, u_out, ukey,
                   extra_u=None, rstd_via="act"):
    ncol = stats.shape[1] // 4
    s1 = stats[:, col:col + 1]
    rstd = stats[:, ncol + col:ncol + col + 1]
    negmr = stats[:, 2 * ncol + col:2 * ncol + col + 1]
    ssq = stats[:, 3 * ncol + col:3 * ncol + col + 1]
    kst = stkey + str(col) + "_"
    p.op("vector", lambda e: e.reduce_sum(s1, xt_ap, AX.X), ins=[xkey], outs=[kst + "s1"])
    p.op("vector", lambda e: e.tensor_scalar_mul(s1, s1, -1.0 / D),
         ins=[kst + "s1"], outs=[kst + "s1"])
    p.op("scalar", lambda e: e.activation(out=sq_scr, in_=xt_ap, func=AF.Square,
                                          bias=s1, accum_out=ssq),
         ins=[xkey, kst + "s1"], outs=[sqkey, kst + "ssq"])
    if rstd_via == "act":
        p.op("scalar", lambda e: e.activation(out=rstd, in_=ssq, func=AF.Ln,
                                              scale=np.float32(1.0 / D),
                                              bias=np.float32(LNEPS)),
             ins=[kst + "ssq"], outs=[kst + "rstd"])
        p.op("scalar", lambda e: e.activation(out=rstd, in_=rstd, func=AF.Exp,
                                              scale=np.float32(-0.5)),
             ins=[kst + "rstd"], outs=[kst + "rstd"])
    else:
        p.op("vector", lambda e: e.tensor_scalar(rstd, ssq, 1.0 / D, LNEPS,
                                                 ALU.mult, ALU.add),
             ins=[kst + "ssq"], outs=[kst + "rstd"])
        p.op("gpsimd", lambda e: e.tensor_tensor(rstd, rstd, p.mh[:], ALU.pow),
             ins=[kst + "rstd", "mh"], outs=[kst + "rstd"])
    p.op("vector", lambda e: e.tensor_mul(negmr, s1, rstd),
         ins=[kst + "s1", kst + "rstd"], outs=[kst + "negmr"])
    p.op("scalar", lambda e: e.activation(out=u_out, in_=xt_ap, func=AF.Identity,
                                          scale=rstd, bias=negmr),
         ins=[xkey, kst + "rstd", kst + "negmr"], outs=[ukey])
    if extra_u is not None:
        ap2, key2 = extra_u
        p.op("scalar", lambda e: e.activation(out=ap2, in_=xt_ap, func=AF.Identity,
                                              scale=rstd, bias=negmr),
             ins=[xkey, kst + "rstd", kst + "negmr"], outs=[key2])


def transpose_tile_to_fm(p, u_src_fn, ukey, t, fm_sb, fmkey, ptr, eye, tcols=T):
    for c in range(NC):
        pk = "ptr" + str(c % 2)
        p.op("tensor",
             lambda e, c=c: e.transpose(ptr[c % 2][:], u_src_fn(c), eye[:]),
             ins=[ukey, "eye"], outs=[pk])
        p.op("vector",
             lambda e, c=c: e.tensor_copy(
                 fm_sb[:, c * tcols + t * 128: c * tcols + (t + 1) * 128],
                 ptr[c % 2][:]),
             ins=[pk], outs=[f"{fmkey}:{c}:{t}"])


def fm_keys(fmkey, ncs=NC, nts=NT):
    return [f"{fmkey}:{c}:{t}" for c in range(ncs) for t in range(nts)]



def emit_quant_fp4(p, x_ap, xkey, out_ap, outkey, scr, kb, nblk, blk=16):
    if isinstance(xkey, str):
        xkey = [xkey]
    W = nblk * blk
    bm, y, u, a, b = scr["bm"], scr["y"], scr["u"], scr["a"], scr["b"]
    xg = x_ap.rearrange("p (n i) -> p n i", i=blk)
    p.op("vector", lambda e: e.tensor_reduce(bm[:, 0:nblk].unsqueeze(2), xg, AX.X,
                                             ALU.max, apply_absolute_value=True),
         ins=xkey, outs=[kb + "bm"])
    p.op("vector", lambda e: e.tensor_scalar(bm[:, nblk:2 * nblk], bm[:, 0:nblk],
                                             1.0 / 6.0, 1e-12, ALU.mult, ALU.max),
         ins=[kb + "bm"], outs=[kb + "sc"])
    p.op("vector", lambda e: e.reciprocal(bm[:, 2 * nblk:3 * nblk],
                                          bm[:, nblk:2 * nblk]),
         ins=[kb + "sc"], outs=[kb + "rs"])
    rb = bm[:, 2 * nblk:3 * nblk].unsqueeze(2).broadcast_to([128, nblk, blk])
    scb = bm[:, nblk:2 * nblk].unsqueeze(2).broadcast_to([128, nblk, blk])
    yg = y[:, 0:W].rearrange("p (n i) -> p n i", i=blk)
    p.op("vector", lambda e: e.tensor_tensor(yg, xg, rb, ALU.mult),
         ins=xkey + [kb + "rs"], outs=[kb + "y"])
    p.op("vector", lambda e: e.scalar_tensor_tensor(
        out=u[:, 0:W], in0=y[:, 0:W], scalar=-1.0, in1=y[:, 0:W],
        op0=ALU.mult, op1=ALU.max),
         ins=[kb + "y"], outs=[kb + "u"])
    p.op("vector", lambda e: e.tensor_single_scalar(a[:, 0:W], u[:, 0:W], 0.25,
                                                    ALU.is_gt),
         ins=[kb + "u"], outs=[kb + "a"])
    for th in (0.75, 1.25, 1.75):
        p.op("vector", lambda e, th=th: e.scalar_tensor_tensor(
            out=a[:, 0:W], in0=u[:, 0:W], scalar=th, in1=a[:, 0:W],
            op0=ALU.is_gt, op1=ALU.add), ins=[kb + "u", kb + "a"], outs=[kb + "a"])
    p.op("vector", lambda e: e.tensor_single_scalar(b[:, 0:W], u[:, 0:W], 2.5,
                                                    ALU.is_gt),
         ins=[kb + "u"], outs=[kb + "b"])
    p.op("vector", lambda e: e.scalar_tensor_tensor(
        out=b[:, 0:W], in0=u[:, 0:W], scalar=3.5, in1=b[:, 0:W],
        op0=ALU.is_gt, op1=ALU.add), ins=[kb + "u", kb + "b"], outs=[kb + "b"])
    p.op("vector", lambda e: e.tensor_single_scalar(u[:, 0:W], u[:, 0:W], 5.0,
                                                    ALU.is_gt),
         ins=[kb + "u"], outs=[kb + "cc"])
    p.op("vector", lambda e: e.scalar_tensor_tensor(
        out=b[:, 0:W], in0=u[:, 0:W], scalar=2.0, in1=b[:, 0:W],
        op0=ALU.mult, op1=ALU.add), ins=[kb + "cc", kb + "b"], outs=[kb + "b"])
    p.op("vector", lambda e: e.scalar_tensor_tensor(
        out=b[:, 0:W], in0=a[:, 0:W], scalar=0.5, in1=b[:, 0:W],
        op0=ALU.mult, op1=ALU.add), ins=[kb + "a", kb + "b"], outs=[kb + "b"])
    bg = b[:, 0:W].rearrange("p (n i) -> p n i", i=blk)
    p.op("vector", lambda e: e.tensor_tensor(bg, bg, scb, ALU.mult),
         ins=[kb + "b", kb + "sc"], outs=[kb + "b"])
    p.op("vector", lambda e: e.tensor_scalar(u[:, 0:W], y[:, 0:W], 0.0, 2.0,
                                             ALU.is_gt, ALU.mult),
         ins=[kb + "y"], outs=[kb + "ssel"])
    p.op("vector", lambda e: e.tensor_mul(a[:, 0:W], b[:, 0:W], u[:, 0:W]),
         ins=[kb + "b", kb + "ssel"], outs=[kb + "d2"])
    p.op("vector", lambda e: e.tensor_sub(out_ap, a[:, 0:W], b[:, 0:W]),
         ins=[kb + "d2", kb + "b"], outs=[outkey])


def emit_quant_fp6(p, x_ap, xkey, out_ap, outkey, scr, kb, nblk, blk=32):
    if isinstance(xkey, str):
        xkey = [xkey]
    W = nblk * blk
    bm, y, u, a, b, s_t = scr["bm"], scr["y"], scr["u"], scr["a"], scr["b"], scr["s"]
    xg = x_ap.rearrange("p (n i) -> p n i", i=blk)
    p.op("vector", lambda e: e.tensor_reduce(bm[:, 0:nblk].unsqueeze(2), xg, AX.X,
                                             ALU.max, apply_absolute_value=True),
         ins=xkey, outs=[kb + "bm"])
    p.op("vector", lambda e: e.tensor_scalar(bm[:, nblk:2 * nblk], bm[:, 0:nblk],
                                             1.0 / 7.5, 1e-30, ALU.mult, ALU.max),
         ins=[kb + "bm"], outs=[kb + "sc"])
    p.op("vector", lambda e: e.reciprocal(bm[:, 2 * nblk:3 * nblk],
                                          bm[:, nblk:2 * nblk]),
         ins=[kb + "sc"], outs=[kb + "rs"])
    rb = bm[:, 2 * nblk:3 * nblk].unsqueeze(2).broadcast_to([128, nblk, blk])
    scb = bm[:, nblk:2 * nblk].unsqueeze(2).broadcast_to([128, nblk, blk])
    yg = y[:, 0:W].rearrange("p (n i) -> p n i", i=blk)
    p.op("vector", lambda e: e.tensor_tensor(yg, xg, rb, ALU.mult),
         ins=xkey + [kb + "rs"], outs=[kb + "y"])
    p.op("vector", lambda e: e.scalar_tensor_tensor(
        out=u[:, 0:W], in0=y[:, 0:W], scalar=-1.0, in1=y[:, 0:W],
        op0=ALU.mult, op1=ALU.max),
         ins=[kb + "y"], outs=[kb + "u"])
    p.op("vector", lambda e: e.tensor_single_scalar(a[:, 0:W], u[:, 0:W], 2.0,
                                                    ALU.is_gt),
         ins=[kb + "u"], outs=[kb + "c1"])
    p.op("vector", lambda e: e.tensor_single_scalar(b[:, 0:W], u[:, 0:W], 3.875,
                                                    ALU.is_gt),
         ins=[kb + "u"], outs=[kb + "c2"])
    p.op("vector", lambda e: e.tensor_scalar(s_t[:, 0:W], a[:, 0:W], 0.125, 0.125,
                                             ALU.mult, ALU.add),
         ins=[kb + "c1"], outs=[kb + "s"])
    p.op("vector", lambda e: e.scalar_tensor_tensor(
        out=s_t[:, 0:W], in0=b[:, 0:W], scalar=0.25, in1=s_t[:, 0:W],
        op0=ALU.mult, op1=ALU.add), ins=[kb + "c2", kb + "s"], outs=[kb + "s"])
    p.op("vector", lambda e: e.tensor_scalar(a[:, 0:W], a[:, 0:W], -4.0, 8.0,
                                             ALU.mult, ALU.add),
         ins=[kb + "c1"], outs=[kb + "t"])
    p.op("vector", lambda e: e.scalar_tensor_tensor(
        out=a[:, 0:W], in0=b[:, 0:W], scalar=-2.0, in1=a[:, 0:W],
        op0=ALU.mult, op1=ALU.add), ins=[kb + "c2", kb + "t"], outs=[kb + "t"])
    p.op("vector", lambda e: e.tensor_mul(u[:, 0:W], u[:, 0:W], a[:, 0:W]),
         ins=[kb + "u", kb + "t"], outs=[kb + "z"])
    p.op("vector", lambda e: e.tensor_single_scalar(a[:, 0:W], u[:, 0:W], 7.5,
                                                    ALU.is_gt),
         ins=[kb + "z"], outs=[kb + "cs"])
    p.op("vector", lambda e: e.tensor_scalar_mul(b[:, 0:W], a[:, 0:W], 8.0),
         ins=[kb + "cs"], outs=[kb + "r"])
    p.op("vector", lambda e: e.scalar_tensor_tensor(
        out=u[:, 0:W], in0=a[:, 0:W], scalar=-8.0, in1=u[:, 0:W],
        op0=ALU.mult, op1=ALU.add), ins=[kb + "cs", kb + "z"], outs=[kb + "z"])
    for w in (4.0, 2.0, 1.0, 1.0):
        p.op("vector", lambda e, w=w: e.tensor_single_scalar(a[:, 0:W], u[:, 0:W],
                                                             w - 0.5, ALU.is_gt),
             ins=[kb + "z"], outs=[kb + "cs"])
        p.op("vector", lambda e, w=w: e.scalar_tensor_tensor(
            out=b[:, 0:W], in0=a[:, 0:W], scalar=w, in1=b[:, 0:W],
            op0=ALU.mult, op1=ALU.add), ins=[kb + "cs", kb + "r"], outs=[kb + "r"])
        p.op("vector", lambda e, w=w: e.scalar_tensor_tensor(
            out=u[:, 0:W], in0=a[:, 0:W], scalar=-w, in1=u[:, 0:W],
            op0=ALU.mult, op1=ALU.add), ins=[kb + "cs", kb + "z"], outs=[kb + "z"])
    p.op("vector", lambda e: e.tensor_mul(b[:, 0:W], b[:, 0:W], s_t[:, 0:W]),
         ins=[kb + "r", kb + "s"], outs=[kb + "q"])
    bg = b[:, 0:W].rearrange("p (n i) -> p n i", i=blk)
    p.op("vector", lambda e: e.tensor_tensor(bg, bg, scb, ALU.mult),
         ins=[kb + "q", kb + "sc"], outs=[kb + "q"])
    p.op("vector", lambda e: e.tensor_scalar(u[:, 0:W], y[:, 0:W], 0.0, 2.0,
                                             ALU.is_gt, ALU.mult),
         ins=[kb + "y"], outs=[kb + "ssel"])
    p.op("vector", lambda e: e.tensor_mul(a[:, 0:W], b[:, 0:W], u[:, 0:W]),
         ins=[kb + "q", kb + "ssel"], outs=[kb + "d2"])
    p.op("vector", lambda e: e.tensor_sub(out_ap, a[:, 0:W], b[:, 0:W]),
         ins=[kb + "d2", kb + "q"], outs=[outkey])




def _pool_quant_pre(p, x_ap, xkey, scr, kb, nblk, blk, qmax, eps):
    if isinstance(xkey, str):
        xkey = [xkey]
    W = nblk * blk
    bm, y, u = scr["bm"], scr["y"], scr["u"]
    xg = x_ap.rearrange("p (n i) -> p n i", i=blk)
    p.op("vector", lambda e: e.tensor_reduce(bm[:, 0:nblk].unsqueeze(2), xg, AX.X,
                                             ALU.max, apply_absolute_value=True),
         ins=xkey, outs=[kb + "bm"])
    p.op("vector", lambda e: e.tensor_scalar(bm[:, nblk:2 * nblk], bm[:, 0:nblk],
                                             1.0 / qmax, eps, ALU.mult, ALU.max),
         ins=[kb + "bm"], outs=[kb + "sc"])
    p.op("vector", lambda e: e.reciprocal(bm[:, 2 * nblk:3 * nblk],
                                          bm[:, nblk:2 * nblk]),
         ins=[kb + "sc"], outs=[kb + "rs"])
    rb = bm[:, 2 * nblk:3 * nblk].unsqueeze(2).broadcast_to([128, nblk, blk])
    yg = y[:, 0:W].rearrange("p (n i) -> p n i", i=blk)
    p.op("gpsimd", lambda e: e.tensor_tensor(yg, xg, rb, ALU.mult),
         ins=xkey + [kb + "rs"], outs=[kb + "y"])
    a = scr["a"]
    p.op("gpsimd", lambda e: e.tensor_single_scalar(a[:, 0:W], y[:, 0:W], 0.0,
                                                    ALU.is_gt),
         ins=[kb + "y"], outs=[kb + "pm"])
    p.op("gpsimd", lambda e: e.tensor_scalar(a[:, 0:W], a[:, 0:W], 2.0, -1.0,
                                             ALU.mult, ALU.add),
         ins=[kb + "pm"], outs=[kb + "sgn"])
    p.op("gpsimd", lambda e: e.tensor_tensor(u[:, 0:W], y[:, 0:W], a[:, 0:W],
                                             ALU.mult),
         ins=[kb + "y", kb + "sgn"], outs=[kb + "u"])


def emit_quant_fp4_pool(p, x_ap, xkey, out_ap, outkey, scr, kb, nblk, blk=16):
    W = nblk * blk
    bm, u, a, b, q = scr["bm"], scr["u"], scr["a"], scr["b"], scr["y"]
    _pool_quant_pre(p, x_ap, xkey, scr, kb, nblk, blk, 6.0, 1e-12)
    p.op("vector", lambda e: e.tensor_scalar_mul(bm[:, 3 * nblk:4 * nblk],
                                                 bm[:, nblk:2 * nblk], 0.5),
         ins=[kb + "sc"], outs=[kb + "sc2"])
    p.op("gpsimd", lambda e: e.tensor_single_scalar(b[:, 0:W], u[:, 0:W], 0.25,
                                                    ALU.is_gt),
         ins=[kb + "u"], outs=[kb + "q2"])
    for th in (0.75, 1.25, 1.75):
        p.op("gpsimd", lambda e, th=th: e.tensor_single_scalar(
            q[:, 0:W], u[:, 0:W], th, ALU.is_gt),
            ins=[kb + "u"], outs=[kb + "c"])
        p.op("gpsimd", lambda e: e.tensor_tensor(b[:, 0:W], b[:, 0:W], q[:, 0:W],
                                                 ALU.add),
             ins=[kb + "c", kb + "q2"], outs=[kb + "q2"])
    for th, wgt in ((2.5, 2.0), (3.5, 2.0), (5.0, 4.0)):
        p.op("gpsimd", lambda e, th=th, wgt=wgt: e.tensor_scalar(
            q[:, 0:W], u[:, 0:W], th, wgt, ALU.is_gt, ALU.mult),
            ins=[kb + "u"], outs=[kb + "c"])
        p.op("gpsimd", lambda e: e.tensor_tensor(b[:, 0:W], b[:, 0:W], q[:, 0:W],
                                                 ALU.add),
             ins=[kb + "c", kb + "q2"], outs=[kb + "q2"])
    scb = bm[:, 3 * nblk:4 * nblk].unsqueeze(2).broadcast_to([128, nblk, blk])
    bg = b[:, 0:W].rearrange("p (n i) -> p n i", i=blk)
    p.op("gpsimd", lambda e: e.tensor_tensor(bg, bg, scb, ALU.mult),
         ins=[kb + "q2", kb + "sc2"], outs=[kb + "d1"])
    p.op("gpsimd", lambda e: e.tensor_tensor(out_ap, b[:, 0:W], a[:, 0:W],
                                             ALU.mult),
         ins=[kb + "d1", kb + "sgn"], outs=[outkey])


def emit_quant_fp6_pool(p, x_ap, xkey, out_ap, outkey, scr, kb, nblk, blk=32):
    W = nblk * blk
    bm, u, a, b, q, s_t = (scr["bm"], scr["u"], scr["a"], scr["b"], scr["y"],
                           scr["s"])
    _pool_quant_pre(p, x_ap, xkey, scr, kb, nblk, blk, 7.5, 1e-30)
    p.op("gpsimd", lambda e: e.tensor_single_scalar(q[:, 0:W], u[:, 0:W], 2.0,
                                                    ALU.is_gt),
         ins=[kb + "u"], outs=[kb + "c1"])
    p.op("gpsimd", lambda e: e.tensor_scalar(b[:, 0:W], q[:, 0:W], -4.0, 8.0,
                                             ALU.mult, ALU.add),
         ins=[kb + "c1"], outs=[kb + "t"])
    p.op("gpsimd", lambda e: e.tensor_scalar(s_t[:, 0:W], q[:, 0:W], 0.125, 0.125,
                                             ALU.mult, ALU.add),
         ins=[kb + "c1"], outs=[kb + "s"])
    p.op("gpsimd", lambda e: e.tensor_single_scalar(q[:, 0:W], u[:, 0:W], 3.875,
                                                    ALU.is_gt),
         ins=[kb + "u"], outs=[kb + "c2"])
    p.op("gpsimd", lambda e: e.tensor_scalar(q[:, 0:W], q[:, 0:W], 2.0, 0.0,
                                             ALU.mult, ALU.add),
         ins=[kb + "c2"], outs=[kb + "c22"])
    p.op("gpsimd", lambda e: e.tensor_tensor(b[:, 0:W], b[:, 0:W], q[:, 0:W],
                                             ALU.subtract),
         ins=[kb + "c22", kb + "t"], outs=[kb + "t"])
    p.op("gpsimd", lambda e: e.tensor_scalar(q[:, 0:W], q[:, 0:W], 0.125, 0.0,
                                             ALU.mult, ALU.add),
         ins=[kb + "c22"], outs=[kb + "s2"])
    p.op("gpsimd", lambda e: e.tensor_tensor(s_t[:, 0:W], s_t[:, 0:W], q[:, 0:W],
                                             ALU.add),
         ins=[kb + "s2", kb + "s"], outs=[kb + "s"])
    p.op("gpsimd", lambda e: e.tensor_tensor(u[:, 0:W], u[:, 0:W], b[:, 0:W],
                                             ALU.mult),
         ins=[kb + "u", kb + "t"], outs=[kb + "z"])
    p.op("gpsimd", lambda e: e.tensor_single_scalar(q[:, 0:W], u[:, 0:W], 7.5,
                                                    ALU.is_gt),
         ins=[kb + "z"], outs=[kb + "cs"])
    p.op("gpsimd", lambda e: e.tensor_scalar(b[:, 0:W], q[:, 0:W], 8.0, 0.0,
                                             ALU.mult, ALU.add),
         ins=[kb + "cs"], outs=[kb + "r"])
    p.op("gpsimd", lambda e: e.tensor_tensor(u[:, 0:W], u[:, 0:W], b[:, 0:W],
                                             ALU.subtract),
         ins=[kb + "r", kb + "z"], outs=[kb + "z"])
    for w in (4.0, 2.0, 1.0, 1.0):
        p.op("gpsimd", lambda e, w=w: e.tensor_scalar(
            q[:, 0:W], u[:, 0:W], w - 0.5, w, ALU.is_gt, ALU.mult),
            ins=[kb + "z"], outs=[kb + "csw"])
        p.op("gpsimd", lambda e: e.tensor_tensor(b[:, 0:W], b[:, 0:W], q[:, 0:W],
                                                 ALU.add),
             ins=[kb + "csw", kb + "r"], outs=[kb + "r"])
        p.op("gpsimd", lambda e: e.tensor_tensor(u[:, 0:W], u[:, 0:W], q[:, 0:W],
                                                 ALU.subtract),
             ins=[kb + "csw", kb + "z"], outs=[kb + "z"])
    p.op("gpsimd", lambda e: e.tensor_tensor(b[:, 0:W], b[:, 0:W], s_t[:, 0:W],
                                             ALU.mult),
         ins=[kb + "r", kb + "s"], outs=[kb + "qv"])
    scb = bm[:, nblk:2 * nblk].unsqueeze(2).broadcast_to([128, nblk, blk])
    bg = b[:, 0:W].rearrange("p (n i) -> p n i", i=blk)
    p.op("gpsimd", lambda e: e.tensor_tensor(bg, bg, scb, ALU.mult),
         ins=[kb + "qv", kb + "sc"], outs=[kb + "d1"])
    p.op("gpsimd", lambda e: e.tensor_tensor(out_ap, b[:, 0:W], a[:, 0:W],
                                             ALU.mult),
         ins=[kb + "d1", kb + "sgn"], outs=[outkey])


def gen_wprep(p, w_d, stg_d, scr, wch, wqb, qkind, nchunks, width, slab,
              tag="", engine="dve", unit_sel=None):
    emitq = {("dve", 6): emit_quant_fp6, ("dve", 4): emit_quant_fp4,
             ("pool", 6): emit_quant_fp6_pool,
             ("pool", 4): emit_quant_fp4_pool}[(engine, qkind)]
    blk = 32 if qkind == 6 else 16
    units = [(c, s0) for c in range(nchunks) for s0 in range(0, width, slab)]
    if unit_sel is not None:
        units = [u for i, u in enumerate(units) if unit_sel(i)]

    def load(i):
        c, s0 = units[i]
        bi = i % 2
        p.dma("sync", f"{tag}wprep{bi}", lambda e, c=c, s0=s0, bi=bi: e.dma_start(
            out=wch[bi][:, 0:slab],
            in_=w_d.ap()[c * 128:(c + 1) * 128, s0:s0 + slab]),
            outs=[f"{tag}wprep_ch{bi}"])

    load(0)
    for i, (c, s0) in enumerate(units):
        bi = i % 2
        if i + 1 < len(units):
            load(i + 1)
        emitq(p, wch[bi][:, 0:slab], f"{tag}wprep_ch{bi}", wqb[bi][:, 0:slab],
              f"{tag}wprep_qb{bi}", scr, tag + "qW", slab // blk)
        p.dma("sync", f"{tag}wstg{bi}", lambda e, c=c, s0=s0, bi=bi: e.dma_start(
            out=stg_d.ap()[c * 128:(c + 1) * 128, s0:s0 + slab],
            in_=wqb[bi][:, 0:slab]),
            ins=[f"{tag}wprep_qb{bi}"], outs=[f"stg_{stg_d.name}:{tag}{i}"])
        yield


def alloc_prep_bufs(p, slab=1408, nm="pw_"):
    scr = {k: p.sb(nm + k, [128, slab], F32) for k in ("y", "u", "a", "b", "s")}
    scr["bm"] = p.sb(nm + "bm", [128, 3 * 88], F32)
    wch = [p.sb(f"{nm}wch{i}", [128, slab], F32) for i in range(2)]
    wqb = [p.sb(f"{nm}wqb{i}", [128, slab], BF16) for i in range(2)]
    return scr, wch, wqb


_RR_END = object()


def rr_merge(*gens):
    gens = list(gens)
    while gens:
        for g in list(gens):
            v = next(g, _RR_END)
            if v is _RR_END:
                gens.remove(g)
            else:
                yield v


def f1_prep_gens(p, dram, scrW, wchW, wqbW, slab_gu=1024, slab_d=1408):
    return (gen_wprep(p, dram["wg"], dram["stg_g"], scrW, wchW, wqbW, 4, NI,
                      D, slab_gu),
            gen_wprep(p, dram["wu"], dram["stg_u"], scrW, wchW, wqbW, 4, NI,
                      D, slab_gu),
            gen_wprep(p, dram["wd"], dram["stg_dn"], scrW, wchW, wqbW, 4, NC,
                      I_DIM, slab_d))


def f1_prep_split(p, dram, bufsD, bufsP):
    scrD, wchD, wqbD = bufsD
    scrP, wchP, wqbP = bufsP
    pool = itertools.chain(
        gen_wprep(p, dram["wg"], dram["stg_g"], scrP, wchP, wqbP, 4, NI, D,
                  512, tag="fp_", engine="pool", unit_sel=lambda i: i % 3 != 2),
        gen_wprep(p, dram["wu"], dram["stg_u"], scrP, wchP, wqbP, 4, NI, D,
                  512, tag="fp_", engine="pool", unit_sel=lambda i: i % 3 != 2))
    dve = itertools.chain(
        gen_wprep(p, dram["wg"], dram["stg_g"], scrD, wchD, wqbD, 4, NI, D,
                  512, tag="fd_", engine="dve", unit_sel=lambda i: i % 3 == 2),
        gen_wprep(p, dram["wu"], dram["stg_u"], scrD, wchD, wqbD, 4, NI, D,
                  512, tag="fd_", engine="dve", unit_sel=lambda i: i % 3 == 2),
        gen_wprep(p, dram["wd"], dram["stg_dn"], scrD, wchD, wqbD, 4, NC,
                  I_DIM, 352, tag="fd_", engine="dve"))
    return pool, dve


def load_wT_staged(p, stg_d, dst, dstkey, ncols_chunks, chunk_rows,
                   eng="scalar"):
    W = chunk_rows
    for ec in range(ncols_chunks):
        p.dma(eng, f"wl{ec % 2}", lambda e, ec=ec: e.dma_start(
            out=dst[:, ec * W:(ec + 1) * W],
            in_=stg_d.ap()[:, ec * 128:(ec + 1) * 128], transpose=True),
            ins=[f"stg_{stg_d.name}"], outs=[f"{dstkey}:{ec}"])



def declare_attn_dram(nc, layer, pre, x_d, y_d, eye_d):
    d = {"x": x_d, "y": y_d, "eye": eye_d}
    for nm, shape in (("wqT", [D, D]), ("wkT", [D, D]), ("wvT", [D, D]),
                      ("woT", [D, D]), ("bq_pc", [128, NC]), ("bk_pc", [128, NC]),
                      ("bv_row", [1, D]), ("bo_row", [1, D])):
        d[nm] = nc.dram_tensor(pre + nm, shape, F32, kind="ExternalInput")
    if layer == 0:
        d["dpmix"] = nc.dram_tensor(pre + "dpmix", [1, 64], F32, kind="ExternalInput")
        d["mdiff"] = nc.dram_tensor(pre + "mdiff", [128, WM], F32, kind="ExternalInput")
    else:
        d["gb"] = nc.dram_tensor(pre + "gb", [1, 2 * D], F32, kind="ExternalInput")
    return d


def emit_attn(p, dram, layer, prep=None, prep2=None):
    def prep_step(n=1, site=3):
        g = prep2 if (site == 2 and prep2 is not None) else prep
        if g is not None:
            for _ in range(n):
                next(g, None)

    nc = p.nc
    x_d, y_d, eye_d = dram["x"], dram["y"], dram["eye"]
    wqT_d, wkT_d, wvT_d, woT_d = dram["wqT"], dram["wkT"], dram["wvT"], dram["woT"]
    p.mh = p.sb("mh", [128, 1], F32)
    p.op("gpsimd", lambda e: e.memset(p.mh[:], -0.5), outs=["mh"])
    bq_d, bk_d, bv_d, bo_d = dram["bq_pc"], dram["bk_pc"], dram["bv_row"], dram["bo_row"]
    if layer == 0:
        dpmix_d, mdiff_d = dram["dpmix"], dram["mdiff"]
    else:
        gb_d = dram["gb"]
    wqkv = [p.sb(f"wqkv{i}", [128, 2 * D], BF16) for i in range(2)]
    bq = p.sb("bq", [128, NC], F32)
    bk = p.sb("bk", [128, NC], F32)
    bvr = p.sb("bvr", [1, D], BF16)
    bor = p.sb("bor", [1, D], BF16)
    eye = p.sb("eye", [128, 128], BF16)
    ones = p.sb("ones", [128, 128], BF16)
    ones32 = p.sb("ones32", [1, 128], F32)
    u_fm = p.sb("u_fm", [128, NC * T], BF16)
    attn = p.sb("attn", [128, NHP * T], BF16)
    stats = p.sb("stats", [128, 4 * NT], F32)
    stats2 = p.sb("stats2", [128, 4 * 2], F32)
    stats3 = p.sb("stats3", [128, 4 * 2], F32)
    qp = p.sb("qp", [128, T], BF16)
    kp = p.sb("kp", [128, T], BF16)
    v_all = p.sb("v_all", [128, NT * 1040], BF16)
    ebf = p.sb("ebf", [128, 2 * 1024], BF16)
    msk = p.sb("msk", [128, 2 * 1024 if layer == 0 else 1024], F32)
    sq_scr = msk[:, 0:D]
    dsc = p.sb("dsc", [128, 512], F32)
    dbf = p.sb("dbf", [128, 512], BF16)
    asb = [p.sb(f"asb{i}", [128, 512], F32) for i in range(2)]
    dnb = [p.sb(f"dnb{i}", [128, 512], F32) for i in range(2)]
    sc_mask = ExitStack()
    if layer == 0:
        _oact = p.actx
        p.actx = sc_mask
        mdiff = p.sb("mdiff_s", [128, WM], F32)
        emask = p.sb("emask", [128, 2 * WM], F16)
        etmp = p.sb("etmp", [128, 1024], F32)
        ball = p.sb("ball", [128, 64], F32)
        brow = p.sb("brow", [1, 64], F32)
        mrow = p.sb("mrow", [1, 64], F32)
        p.actx = _oact
    pq = p.ps("pq", [128, 512], F32)
    po0 = p.ps("po0", [128, 512], F32)
    po1 = p.ps("po1", [128, 512], F32)

    p.dma("gpsimd", "c_bv", lambda e: e.dma_start(out=bvr[:], in_=bv_d.ap()),
          outs=["bvr"])
    p.dma("gpsimd", "c_bo", lambda e: e.dma_start(out=bor[:], in_=bo_d.ap()),
          outs=["bor"])
    p.dma("sync", "c_bq", lambda e: e.dma_start(out=bq[:], in_=bq_d.ap()),
          outs=["bq"])
    p.dma("sync", "c_bk", lambda e: e.dma_start(out=bk[:], in_=bk_d.ap()),
          outs=["bk"])
    p.dma("sync", "c_eye", lambda e: e.dma_start(out=eye[:], in_=eye_d.ap()),
          outs=["eye"])
    p.op("vector", lambda e: e.memset(ones[:], 1.0), outs=["ones"])
    p.op("vector", lambda e: e.memset(ones32[:], 1.0), outs=["ones32"])
    if layer == 0:
        p.dma("sync", "c_md", lambda e: e.dma_start(out=mdiff[:], in_=mdiff_d.ap()),
              outs=["mdiff"])
        p.dma("sync", "c_dm", lambda e: e.dma_start(out=mrow[0:1, :],
                                                    in_=dpmix_d.ap()),
              outs=["mrow"])

    if layer == 0:
        p.op("vector", lambda e: e.tensor_scalar(brow[0:1, 0:32], mrow[0:1, 0:32],
                                                 0.01, 0.9999, ALU.max, ALU.min),
             ins=["mrow"], outs=["dpc"])
        p.op("scalar", lambda e: e.activation(out=brow[0:1, 0:32],
                                              in_=brow[0:1, 0:32], func=AF.Ln),
             ins=["dpc"], outs=["lndp"])
        p.op("scalar", lambda e: e.activation(out=mrow[0:1, 32:64],
                                              in_=mrow[0:1, 32:64], func=AF.Exp),
             ins=["mrow"], outs=["mexp"])
        mv = mrow[0:1, 32:64].rearrange("p (h n) -> p h n", n=2)
        p.op("vector", lambda e: e.tensor_reduce(
            mrow[0:1, 16:32].unsqueeze(2), mv, AX.X, ALU.add),
             ins=["mexp"], outs=["msum"])
        p.op("vector", lambda e: e.reciprocal(mrow[0:1, 16:32], mrow[0:1, 16:32]),
             ins=["msum"], outs=["mrec"])
        mrb = mrow[0:1, 16:32].unsqueeze(2).broadcast_to([1, 16, 2])
        p.op("vector", lambda e: e.tensor_tensor(mv, mv, mrb, ALU.mult),
             ins=["mexp", "mrec"], outs=["mnorm"])
        p.op("scalar", lambda e: e.activation(out=brow[0:1, 32:64],
                                              in_=mrow[0:1, 32:64], func=AF.Ln),
             ins=["mnorm"], outs=["lnm"])
        p.op("vector", lambda e: e.tensor_scalar_add(
            brow[0:1, 32:64], brow[0:1, 32:64], float(-np.log(8.0))),
             ins=["lnm"], outs=["lnm"])
        p.op("tensor", lambda e: e.matmul(pq[:, 0:64], ones32[0:1, :],
                                          brow[0:1, :], start=True, stop=True),
             ins=["ones32", "lndp", "lnm"], outs=["pq"])
        p.op("scalar", lambda e: e.activation(out=ball[:], in_=pq[:, 0:64],
                                              func=AF.Identity, bias=np.float32(0.0)),
             ins=["pq"], outs=["ball"])

    outer_actx = p.actx
    with ExitStack() as sc1:
        p.actx = sc1
        ptr = [p.ps(f"ptr{i}", [128, 128], BF16) for i in range(2)]
        xt = [p.sb(f"xt{i}", [128, D], F32) for i in range(2)]
        ubf = [p.sb(f"ubf{i}", [128, D], BF16) for i in range(2)]
        p.actx = outer_actx
        for t in range(NT):
            bi = t % 2
            p.dma("sync", f"x{bi}", lambda e, t=t, bi=bi: e.dma_start(
                out=xt[bi][:], in_=x_d.ap()[t * 128:(t + 1) * 128, :]),
                outs=[f"xt{bi}"])
            ln_stats_and_u(p, xt[bi][:], f"xt{bi}", sq_scr[:], "sq_scr",
                           stats, "st", t, ubf[bi][:], f"ubf{bi}")
            transpose_tile_to_fm(p, lambda c, bi=bi: ubf[bi][:, c * 128:(c + 1) * 128],
                                 f"ubf{bi}", t, u_fm, "ufm", ptr, eye)
        p.barrier()
    psW = [p.ps("psW0", [128, 1024], F32), p.ps("psW1", [128, 1024], F32)]
    pq2 = p.ps("pq2", [128, 512], F32)
    ufm_all = fm_keys("ufm")

    with ExitStack() as scv:
        _oact = p.actx
        p.actx = scv
        wv_all = p.sb("wv_all", [128, NC * D], BF16)
        p.actx = _oact
        p.dma("gpsimd", "wvl", lambda e: e.dma_start(
            out=wv_all[:].rearrange("p (c f) -> p c f", c=NC),
            in_=wvT_d.ap().rearrange("(c p) f -> p c f", p=128)),
            outs=["wv_all"])
        v3o = v_all[:].rearrange("p (g c) -> p g c", c=130)
        p.op("vector", lambda e: e.memset(v3o[:, :, 64:65], 1.0), outs=["vo"])
        p.op("vector", lambda e: e.memset(v3o[:, :, 129:130], 1.0), outs=["vo"])
        for t in range(NT):
            for half in range(2):
                psel, kpsel = (pq, "pq") if half == 0 else (pq2, "pq2")

                def mmvq(e, t=t, half=half, psel=psel):
                    for c in range(NC):
                        e.matmul(psel[:],
                                 u_fm[:, c * T + t * 128: c * T + (t + 1) * 128],
                                 wv_all[:, c * D + half * 512: c * D + half * 512 + 512],
                                 start=(c == 0), stop=False)
                    return e.matmul(psel[:], ones[0:1, :],
                                    bvr[0:1, half * 512: (half + 1) * 512],
                                    start=False, stop=True)
                p.op("tensor", mmvq, ins=["wv_all", "bvr", "ones"] + ufm_all,
                     outs=[kpsel])
                v3 = v_all[:, t * 1040 + half * 520: t * 1040 + (half + 1) * 520] \
                    .rearrange("p (g c) -> p g c", c=130)
                p3 = psel[:].rearrange("p (g hc) -> p g hc", g=4)
                p.op("vector", lambda e, v3=v3, p3=p3: e.tensor_copy(
                    v3[:, :, 0:64], p3[:, :, 0:64]),
                    ins=[kpsel], outs=[f"va{t}:{half}"])
                p.op("vector", lambda e, v3=v3, p3=p3: e.tensor_copy(
                    v3[:, :, 65:129], p3[:, :, 64:128]),
                    ins=[kpsel], outs=[f"va{t}:{half}"])

    for hp in range(NHP):
        wb = hp % 2
        wkey = f"wqkv{wb}"
        for j, src in enumerate((wqT_d, wkT_d)):
            p.dma("gpsimd", f"wpair{wb}", lambda e, j=j, src=src, wb=wb, hp=hp:
                  e.dma_start(
                      out=wqkv[wb][:, j * D:(j + 1) * D],
                      in_=src.ap().rearrange("(c p) f -> p c f", p=128)
                      [:, :, hp * 128:(hp + 1) * 128]),
                  outs=[wkey + str(j)])
        qkeys = [f"qp{i}" for i in range(4)]
        kkeys = [f"kp{i}" for i in range(4)]
        for j, (bsb, bkey, dst, dks) in enumerate(
                ((bq, "bq", qp, "qp"), (bk, "bk", kp, "kp"))):
            for ts4 in range(4):
                sl = slice(ts4 * 512, (ts4 + 1) * 512)
                psel, kpsel = (pq, "pq") if ts4 % 2 == 0 else (pq2, "pq2")

                def mmq(e, j=j, sl=sl, wb=wb, psel=psel):
                    for c in range(NC):
                        inst = e.matmul(
                            psel[:], wqkv[wb][:, j * D + c * 128: j * D + c * 128 + 128],
                            u_fm[:, c * T:(c + 1) * T][:, sl],
                            start=(c == 0), stop=(c == NC - 1))
                    return inst
                p.op("tensor", mmq, ins=[wkey + str(j)] + ufm_all, outs=[kpsel])
                p.op("scalar", lambda e, sl=sl, bsb=bsb, hp=hp, dst=dst,
                     psel=psel: e.activation(
                    out=dst[:, sl], in_=psel[:], func=AF.Identity,
                    bias=bsb[:, hp:hp + 1]),
                    ins=[kpsel, bkey], outs=[dks + str(ts4)])
        prep_step(3, site=2)

        if layer == 0:
            for h2 in range(2):
                h = 2 * hp + h2
                dst = emask[:, h2 * WM:(h2 + 1) * WM]
                p.op("scalar", lambda e, dst=dst, h=h: e.activation(
                    out=dst, in_=mdiff[:], func=AF.Exp,
                    scale=ball[:, 2 * h: 2 * h + 1],
                    bias=ball[:, 32 + 2 * h: 32 + 2 * h + 1]),
                    ins=["mdiff", "ball"], outs=[f"emA{h2}"])
                for cs in range(0, WM, 1024):
                    cw = min(1024, WM - cs)
                    p.op("scalar", lambda e, h=h, cs=cs, cw=cw: e.activation(
                        out=etmp[:, 0:cw], in_=mdiff[:, cs:cs + cw], func=AF.Exp,
                        scale=ball[:, 2 * h + 1: 2 * h + 2],
                        bias=ball[:, 32 + 2 * h + 1: 32 + 2 * h + 2]),
                        ins=["mdiff", "ball", f"emsk{h2}:{cs}"], outs=["etmp"])
                    p.op("vector", lambda e, dst=dst, cs=cs, cw=cw: e.tensor_add(
                        dst[:, cs:cs + cw], dst[:, cs:cs + cw], etmp[:, 0:cw]),
                        ins=[f"emA{h2}", "etmp"], outs=[f"emask{h2}",
                                                        f"emsk{h2}:{cs}"])

        for qb in range(4):
            qsl = slice(qb * 512, (qb + 1) * 512)

            def scores(kc):
                par = kc % 2
                psx = psW[par]
                kA, kB = f"psW{par}h0", f"psW{par}h1"
                p.op("tensor", lambda e, kc=kc, psx=psx, qsl=qsl: e.matmul(
                    psx[:, 0:512], kp[0:64, kc * 128:(kc + 1) * 128], qp[0:64, qsl],
                    start=True, stop=True, tile_position=(0, 0)),
                    ins=qkeys + kkeys, outs=[kA])
                p.op("tensor", lambda e, kc=kc, psx=psx, qsl=qsl: e.matmul(
                    psx[:, 512:1024], kp[64:128, kc * 128:(kc + 1) * 128],
                    qp[64:128, qsl],
                    start=True, stop=True, tile_position=(64, 0)),
                    ins=qkeys + kkeys, outs=[kB])
                esl = slice(par * 1024, (par + 1) * 1024)
                ek = f"ebfp{par}"
                if layer == 0:
                    c0 = (T - 1) + qb * 512 - kc * 128
                    m3 = msk[:, esl].rearrange("p (h w) -> p h w", h=2)
                    p3 = psx[:].rearrange("p (h w) -> p h w", h=2)
                    e3 = emask[:].rearrange("p (h w) -> p h w", h=2)[:, :, c0:c0 + 512]
                    p.op("vector", lambda e, m3=m3, p3=p3, e3=e3: e.tensor_tensor(
                        m3, p3, e3, ALU.mult),
                        ins=[kA, kB, "emask0", "emask1"], outs=[f"mskp{par}"])
                    p.op("scalar", lambda e, esl=esl: e.activation(
                        out=ebf[:, esl], in_=msk[:, esl], func=AF.Exp),
                        ins=[f"mskp{par}"], outs=[ek])
                else:
                    p.op("scalar", lambda e, psx=psx, esl=esl: e.activation(
                        out=ebf[:, esl], in_=psx[:], func=AF.Exp,
                        scale=np.float32(0.125)),
                        ins=[kA, kB], outs=[ek])

            def pv(kc):
                par = kc % 2
                eo = par * 1024
                vb = kc * 1040 + hp * 130
                vkey = f"va{kc}:{hp // 4}"
                p.op("tensor", lambda e, kc=kc, eo=eo, vb=vb: e.matmul(
                    po0[0:65, :], v_all[:, vb: vb + 65],
                    ebf[:, eo: eo + 512],
                    start=(kc == 0), stop=(kc == NT - 1)),
                    ins=[f"ebfp{par}", vkey, "vo"], outs=["po0"])
                p.op("tensor", lambda e, kc=kc, eo=eo, vb=vb: e.matmul(
                    po1[0:65, :], v_all[:, vb + 65: vb + 130],
                    ebf[:, eo + 512: eo + 1024],
                    start=(kc == 0), stop=(kc == NT - 1)),
                    ins=[f"ebfp{par}", vkey, "vo"], outs=["po1"])

            for kc in range(NT + 1):
                if kc < NT:
                    scores(kc)
                if kc > 0:
                    pv(kc - 1)
            p.op("scalar", lambda e: e.activation(out=dsc[64:65, 0:512],
                                                  in_=po0[64:65, :], func=AF.Ln),
                 ins=["po0"], outs=["dscA"])
            p.op("scalar", lambda e: e.activation(out=dbf[64:65, 0:512],
                                                  in_=dsc[64:65, 0:512], func=AF.Exp,
                                                  scale=np.float32(-1.0)),
                 ins=["dscA"], outs=["dbfA"])
            p.op("scalar", lambda e, qb=qb: e.activation(out=asb[qb % 2][0:64, :],
                                                  in_=po0[0:64, :],
                                                  func=AF.Identity, bias=np.float32(0.0)),
                 ins=["po0"], outs=[f"asbA{qb % 2}"])
            p.op("tensor", lambda e: e.matmul(pq[0:64, :], ones[64:65, 0:64],
                                              dbf[64:65, 0:512],
                                              start=True, stop=True),
                 ins=["dbfA", "ones"], outs=["pq"])
            p.op("scalar", lambda e, qb=qb: e.activation(
                out=dnb[qb % 2][0:64, :], in_=pq[0:64, :],
                func=AF.Identity, bias=np.float32(0.0)),
                ins=["pq"], outs=[f"dnbA{qb % 2}"])
            p.op("vector", lambda e, hp=hp, qsl=qsl, qb=qb: e.tensor_mul(
                attn[0:64, hp * T: (hp + 1) * T][:, qsl], dnb[qb % 2][0:64, :],
                asb[qb % 2][0:64, :]),
                ins=[f"dnbA{qb % 2}", f"asbA{qb % 2}"], outs=[f"attn:{hp}:0:{qb}"])
            p.op("scalar", lambda e: e.activation(out=dsc[0:1, 0:512],
                                                  in_=po1[64:65, :], func=AF.Ln),
                 ins=["po1"], outs=["dscB"])
            p.op("scalar", lambda e: e.activation(out=dbf[0:1, 0:512],
                                                  in_=dsc[0:1, 0:512],
                                                  func=AF.Exp,
                                                  scale=np.float32(-1.0)),
                 ins=["dscB"], outs=["dbfB"])
            p.op("scalar", lambda e, qb=qb: e.activation(out=asb[qb % 2][64:128, :],
                                                  in_=po1[0:64, :],
                                                  func=AF.Identity, bias=np.float32(0.0)),
                 ins=["po1"], outs=[f"asbB{qb % 2}"])
            p.op("tensor", lambda e: e.matmul(pq2[64:128, :], ones[0:1, 0:64],
                                              dbf[0:1, 0:512],
                                              start=True, stop=True),
                 ins=["dbfB", "ones"], outs=["pq2"])
            p.op("scalar", lambda e, qb=qb: e.activation(
                out=dnb[qb % 2][64:128, :], in_=pq2[64:128, :],
                func=AF.Identity, bias=np.float32(0.0)),
                ins=["pq2"], outs=[f"dnbB{qb % 2}"])
            p.op("vector", lambda e, hp=hp, qsl=qsl, qb=qb: e.tensor_mul(
                attn[64:128, hp * T: (hp + 1) * T][:, qsl],
                dnb[qb % 2][64:128, :], asb[qb % 2][64:128, :]),
                ins=[f"dnbB{qb % 2}", f"asbB{qb % 2}"], outs=[f"attn:{hp}:1:{qb}"])
            prep_step(4)

    attn_all = [f"attn:{g}:{h2}:{qb}" for g in range(NHP) for h2 in range(2)
                for qb in range(4)]
    p.barrier()
    sc_mask.close()
    wo = p.sb("wo", [128, NC * D], BF16)
    xtw = [p.sb(f"xtw{i}", [128, D], F32) for i in range(2)]
    p.dma("gpsimd", "c_wo", lambda e: e.dma_start(
        out=wo[:].rearrange("p (c f) -> p c f", c=NC),
        in_=woT_d.ap().rearrange("(c p) f -> p c f", p=128)),
        outs=["wo"])
    if layer == 1:
        gb_b = p.sb("gb_b", [128, 2 * D], F32)
        u32 = [p.sb("u32_s", [128, D], F32)] * 2
        ysb = p.sb("ysb", [128, D], F32)
        p.dma("sync", "c_gb", lambda e: e.dma_start(out=gb_b[0:1, :],
                                                    in_=gb_d.ap()),
              outs=["gbrow"])
        for half in range(2):
            for fs in range(0, D, 512):
                o = half * D + fs
                p.op("tensor", lambda e, o=o: e.matmul(
                    pq[:, 0:512], ones32[0:1, :], gb_b[0:1, o:o + 512],
                    start=True, stop=True),
                    ins=["gbrow", "ones32"], outs=["pq"])
                p.op("scalar", lambda e, o=o: e.activation(
                    out=gb_b[:, o:o + 512], in_=pq[:, 0:512],
                    func=AF.Identity, bias=np.float32(0.0)),
                    ins=["pq"], outs=["gb_b"])
    if "dbg_attn" in dram:
        p.dma("sync", "dbg1", lambda e: e.dma_start(out=dram["dbg_attn"].ap(),
                                                    in_=attn[:]), ins=attn_all,
              outs=["dbgA"])
        p.dma("sync", "dbg2", lambda e: e.dma_start(out=dram["dbg_qp"].ap(),
                                                    in_=qp[:]),
              ins=[f"qp{i}" for i in range(4)], outs=["dbgQ"])
        p.dma("sync", "dbg3", lambda e: e.dma_start(out=dram["dbg_kp"].ap(),
                                                    in_=kp[:]),
              ins=[f"kp{i}" for i in range(4)], outs=["dbgK"])
        p.dma("sync", "dbg4", lambda e: e.dma_start(out=dram["dbg_vaug"].ap(),
                                                    in_=v_all[:, 0:NT * 130]),
              outs=["dbgV"])

    for t in range(NT):
        bi = t % 2
        p.dma("sync", f"x{bi}", lambda e, t=t, bi=bi: e.dma_start(
            out=xtw[bi][:], in_=x_d.ap()[t * 128:(t + 1) * 128, :]),
            outs=[f"xt{bi}"])
        for fs in range(2):
            fsl = slice(fs * 512, (fs + 1) * 512)
            psx, kpsx = (psW[0][:, 0:512], "psW0h0") if fs == 0 else (
                psW[0][:, 512:1024], "psW0h1")

            def mwo(e, t=t, fsl=fsl, psx=psx):
                for g in range(NHP):
                    e.matmul(psx,
                             attn[:, g * T + t * 128: g * T + (t + 1) * 128],
                             wo[:, g * D:(g + 1) * D][:, fsl],
                             start=(g == 0), stop=False)
                return e.matmul(psx, ones[0:1, :], bor[0:1, fsl],
                                start=False, stop=True)
            p.op("tensor", mwo, ins=attn_all + ["wo", "bor", "ones"], outs=[kpsx])
        if layer == 0:
            for fs in range(2):
                fsl = slice(fs * 512, (fs + 1) * 512)
                psx, kpsx = (psW[0][:, 0:512], "psW0h0") if fs == 0 else (
                    psW[0][:, 512:1024], "psW0h1")
                p.op("vector", lambda e, bi=bi, fsl=fsl, psx=psx: e.tensor_add(
                    xtw[bi][:, fsl], psx, xtw[bi][:, fsl]),
                    ins=[kpsx, f"xt{bi}"], outs=[f"xt{bi}"])
        else:
            ln_stats_and_u(p, xtw[bi][:], f"xt{bi}", sq_scr[:], "sq_scr",
                           stats2, "wt", bi, u32[bi][:], "u32s")
            for fs in range(2):
                fsl = slice(fs * 512, (fs + 1) * 512)
                psx, kpsx = (psW[0][:, 0:512], "psW0h0") if fs == 0 else (
                    psW[0][:, 512:1024], "psW0h1")
                p.op("vector", lambda e, bi=bi, fsl=fsl, psx=psx: e.tensor_add(
                    ysb[:, fsl], psx, u32[bi][:, fsl]),
                    ins=[kpsx, "u32s"], outs=["ysb"])
            ln_stats_and_u(p, ysb[:], "ysb", sq_scr[:], "sq_scr",
                           stats3, "yt", bi, ysb[:], "ysb")
            p.op("vector", lambda e: e.tensor_mul(ysb[:], ysb[:], gb_b[:, 0:D]),
                 ins=["ysb", "gb_b"], outs=["ysb"])
            p.op("vector", lambda e: e.tensor_add(ysb[:], ysb[:], gb_b[:, D:2 * D]),
                 ins=["ysb", "gb_b"], outs=["ysb"])
            p.op("vector", lambda e, bi=bi: e.tensor_add(xtw[bi][:], ysb[:],
                                                         xtw[bi][:]),
                 ins=["ysb", f"xt{bi}"], outs=[f"xt{bi}"])
        p.dma("sync", "st", lambda e, t=t, bi=bi: e.dma_start(
            out=y_d.ap()[t * 128:(t + 1) * 128, :], in_=xtw[bi][:]),
            ins=[f"xt{bi}"], outs=[f"y{t}"])


def build_attn(layer):
    nc = bass.Bass("TRN2", target_bir_lowering=False)
    ctx = ExitStack()
    x_d = nc.dram_tensor("x", [T, D], F32, kind="ExternalInput")
    y_d = nc.dram_tensor("y", [T, D], F32, kind="ExternalOutput")
    eye_d = nc.inline_tensor(np.eye(128, dtype=BFNP), name="eye_bf")
    dram = declare_attn_dram(nc, layer, "", x_d, y_d, eye_d)
    import os as _os
    if _os.environ.get("DBG_ATTN") == "1":
        dram["dbg_attn"] = nc.dram_tensor("dbg_attn", [128, NHP * T], BF16,
                                          kind="ExternalOutput")
        dram["dbg_qp"] = nc.dram_tensor("dbg_qp", [128, T], BF16,
                                        kind="ExternalOutput")
        dram["dbg_kp"] = nc.dram_tensor("dbg_kp", [128, T], BF16,
                                        kind="ExternalOutput")
        dram["dbg_vaug"] = nc.dram_tensor("dbg_vaug", [128, NT * 130], BF16,
                                          kind="ExternalOutput")
    with ctx:
        p = Prog(nc, ctx)
        emit_attn(p, dram, layer)
        p.barrier()
        p.emit()
    return nc



def declare_ffn0_dram(nc, pre, x_d, y_d, eye_d):
    return {"x": x_d, "y": y_d, "eye": eye_d,
            "w": nc.dram_tensor(pre + "w", [D, D], F32, kind="ExternalInput"),
            "b_row": nc.dram_tensor(pre + "b_row", [1, D], F32, kind="ExternalInput"),
            "stg": nc.dram_tensor(pre + "stg", [D, D], BF16, kind="Internal")}


def emit_ffn0(p, dram, prep_done=False):
    nc = p.nc
    x_d, y_d, eye_d = dram["x"], dram["y"], dram["eye"]
    b_d = dram["b_row"]
    stg_d = dram["stg"]
    p.mh = p.sb("mh", [128, 1], F32)
    p.op("gpsimd", lambda e: e.memset(p.mh[:], -0.5), outs=["mh"])
    if not prep_done:
        _oact = p.actx
        with ExitStack() as sw:
            p.actx = sw
            scrW, wchW, wqbW = alloc_prep_bufs(p)
            p.actx = _oact
            for _ in gen_wprep(p, dram["w"], stg_d, scrW, wchW, wqbW, 6, NC, D,
                               1024):
                pass
            p.barrier()
    wqT = p.sb("wqT", [128, NC * D], BF16)
    br = p.sb("br", [1, D], BF16)
    eye = p.sb("eye", [128, 128], BF16)
    ones = p.sb("ones", [1, 128], BF16)
    stats = p.sb("stats", [128, 4 * 2], F32)
    xt = [p.sb(f"xt{i}", [128, D], F32) for i in range(2)]
    xtw = [p.sb(f"xtw{i}", [128, D], F32) for i in range(2)]
    u32 = [p.sb(f"u32_{i}", [128, D], F32) for i in range(2)]
    xqb = [p.sb(f"xqb{i}", [128, D], BF16) for i in range(2)]
    xqf = [p.sb(f"xqf{i}", [128, D], BF16) for i in range(4)]
    sq_scr = p.sb("sq_scr", [128, D], F32)
    scr = {k: p.sb("qs_" + k, [128, D], F32) for k in ("y", "u", "a", "b", "s")}
    scr["bm"] = p.sb("qs_bm", [128, 3 * 32], F32)
    ptr = [p.ps(f"ptr{i}", [128, 128], BF16) for i in range(2)]
    pd = [p.ps(f"pd{i}", [128, 512], F32) for i in range(4)]

    p.dma("sync", "c_eye", lambda e: e.dma_start(out=eye[:], in_=eye_d.ap()),
          outs=["eye"])
    p.dma("gpsimd", "c_b", lambda e: e.dma_start(out=br[:], in_=b_d.ap()),
          outs=["br"])
    p.op("vector", lambda e: e.memset(ones[:], 1.0), outs=["ones"])

    load_wT_staged(p, stg_d, wqT, "wqT", NC, D)
    wqT_all = [f"wqT:{ec}" for ec in range(NC)]

    def front(t):
        bi, ri = t % 2, t % 4
        p.dma("sync", f"x{bi}", lambda e, t=t, bi=bi: e.dma_start(
            out=xt[bi][:], in_=x_d.ap()[t * 128:(t + 1) * 128, :]),
            outs=[f"xt{bi}"])
        ln_stats_and_u(p, xt[bi][:], f"xt{bi}", sq_scr[:], "sq_scr",
                       stats, "st", bi, u32[bi][:], "u32s", rstd_via="pool")
        emit_quant_fp6(p, u32[bi][:], "u32s", xqb[bi][:], "xqbs",
                       scr, "qs", 32)
        for ec in range(NC):
            pk = "ptr" + str(ec % 2)
            p.op("tensor", lambda e, ec=ec, bi=bi: e.transpose(
                ptr[ec % 2][:], xqb[bi][:, ec * 128:(ec + 1) * 128], eye[:]),
                ins=["xqbs", "eye"], outs=[pk])
            p.op("scalar", lambda e, ec=ec, ri=ri: e.activation(
                out=xqf[ri][:, ec * 128:(ec + 1) * 128],
                in_=ptr[ec % 2][:], func=AF.Identity, bias=np.float32(0.0)),
                ins=[pk], outs=[f"xqf{ri}_{ec}"])

    def back_mm(t):
        bi, ri = t % 2, t % 4
        xqf_all = [f"xqf{ri}_{ec}" for ec in range(NC)]
        p.dma("gpsimd", f"xw{bi}", lambda e, t=t, bi=bi: e.dma_start(
            out=xtw[bi][:], in_=x_d.ap()[t * 128:(t + 1) * 128, :]),
            outs=[f"xtw{bi}"])
        for fs in range(2):
            fsl = slice(fs * 512, (fs + 1) * 512)
            psx, kpsx = pd[(t % 2) * 2 + fs], f"pd{(t % 2) * 2 + fs}"

            def mmd(e, fsl=fsl, psx=psx, ri=ri):
                for ec in range(NC):
                    e.matmul(psx[:], xqf[ri][:, ec * 128:(ec + 1) * 128],
                             wqT[:, ec * D:(ec + 1) * D][:, fsl],
                             start=(ec == 0), stop=False)
                return e.matmul(psx[:], ones[0:1, :], br[0:1, fsl],
                                start=False, stop=True)
            p.op("tensor", mmd, ins=xqf_all + wqT_all + ["br", "ones"], outs=[kpsx])

    def back_fin(t):
        bi = t % 2
        for fs in range(2):
            fsl = slice(fs * 512, (fs + 1) * 512)
            psx, kpsx = pd[(t % 2) * 2 + fs], f"pd{(t % 2) * 2 + fs}"
            p.op("vector", lambda e, bi=bi, fsl=fsl, psx=psx: e.tensor_add(
                xtw[bi][:, fsl], psx[:], xtw[bi][:, fsl]),
                ins=[kpsx, f"xtw{bi}"], outs=[f"xtw{bi}"])
        p.dma("sync", "st", lambda e, t=t, bi=bi: e.dma_start(
            out=y_d.ap()[t * 128:(t + 1) * 128, :], in_=xtw[bi][:]),
            ins=[f"xtw{bi}"], outs=[f"y{t}"])

    front(0)
    front(1)
    for t in range(NT):
        back_mm(t)
        if t + 2 < NT:
            front(t + 2)
        back_fin(t)


def build_ffn0():
    nc = bass.Bass("TRN2", target_bir_lowering=False)
    ctx = ExitStack()
    x_d = nc.dram_tensor("x", [T, D], F32, kind="ExternalInput")
    y_d = nc.dram_tensor("y", [T, D], F32, kind="ExternalOutput")
    eye_d = nc.inline_tensor(np.eye(128, dtype=BFNP), name="eye_bf")
    dram = declare_ffn0_dram(nc, "", x_d, y_d, eye_d)
    with ctx:
        p = Prog(nc, ctx)
        emit_ffn0(p, dram)
        p.barrier()
        p.emit()
    return nc



NI = 22
I_DIM = 2816


def declare_ffn1_dram(nc, pre, x_d, y_d, eye_d):
    return {"x": x_d, "y": y_d, "eye": eye_d,
            "wg": nc.dram_tensor(pre + "wg", [I_DIM, D], F32, kind="ExternalInput"),
            "wu": nc.dram_tensor(pre + "wu", [I_DIM, D], F32, kind="ExternalInput"),
            "wd": nc.dram_tensor(pre + "wd", [D, I_DIM], F32, kind="ExternalInput"),
            "stg_g": nc.dram_tensor(pre + "stg_g", [I_DIM, D], BF16, kind="Internal"),
            "stg_u": nc.dram_tensor(pre + "stg_u", [I_DIM, D], BF16, kind="Internal"),
            "stg_dn": nc.dram_tensor(pre + "stg_dn", [D, I_DIM], BF16, kind="Internal")}


def emit_ffn1(p, dram, prep_done=False):
    nc = p.nc
    x_d, y_d, eye_d = dram["x"], dram["y"], dram["eye"]
    stg_g, stg_u, stg_dn = dram["stg_g"], dram["stg_u"], dram["stg_dn"]
    HF = I_DIM // 2
    p.mh = p.sb("mh", [128, 1], F32)
    p.op("gpsimd", lambda e: e.memset(p.mh[:], -0.5), outs=["mh"])
    if not prep_done:
        _oact = p.actx
        with ExitStack() as sw:
            p.actx = sw
            scrW, wchW, wqbW = alloc_prep_bufs(p)
            p.actx = _oact
            for g in f1_prep_gens(p, dram, scrW, wchW, wqbW):
                for _ in g:
                    pass
            p.barrier()
    wgT = p.sb("wgT", [128, NC * I_DIM], BF16)
    wuT = p.sb("wuT", [128, NC * I_DIM], BF16)
    wdT = p.sb("wdT", [128, NI * D], BF16)
    eye = p.sb("eye", [128, 128], BF16)
    stats = p.sb("stats", [128, 4 * 2], F32)
    xt = [p.sb(f"xt{i}", [128, D], F32) for i in range(3)]
    u32 = [p.sb("u32_s", [128, D], F32)] * 2
    xqb = [p.sb("xqb_s", [128, D], BF16)] * 2
    xqf = [p.sb(f"xqf{i}", [128, D], BF16) for i in range(3)]
    gs = [p.sb(f"gs{i}", [128, I_DIM], F32) for i in range(2)]
    upe = p.sb("upe", [128, 2 * 512], F32)
    hq = [p.sb(f"hq{i}", [128, I_DIM], BF16) for i in range(2)]
    hqT = p.sb("hqT", [128, I_DIM], BF16)
    scr = {k: p.sb("qs_" + k, [128, 512], F32) for k in ("y", "u", "a", "b")}
    scr["bm"] = p.sb("qs_bm", [128, 3 * 64], F32)
    ptr = [p.ps(f"ptr{i}", [128, 128], BF16) for i in range(2)]
    pg = [p.ps("pg0", [128, 512], F32), p.ps("pg1", [128, 512], F32)]
    pu = [p.ps("pu0", [128, 512], F32), p.ps("pu1", [128, 512], F32)]
    pd0 = p.ps("pd0", [128, 512], F32)
    pd1 = p.ps("pd1", [128, 512], F32)

    p.dma("sync", "c_eye", lambda e: e.dma_start(out=eye[:], in_=eye_d.ap()),
          outs=["eye"])
    load_wT_staged(p, stg_g, wgT, "wgT", NC, I_DIM)
    load_wT_staged(p, stg_u, wuT, "wuT", NC, I_DIM)
    load_wT_staged(p, stg_dn, wdT, "wdT", NI, D)
    wgT_all = [f"wgT:{ec}" for ec in range(NC)]
    wuT_all = [f"wuT:{ec}" for ec in range(NC)]
    wdT_all = [f"wdT:{ic}" for ic in range(NI)]

    ISL = [(k * 512, min((k + 1) * 512, I_DIM)) for k in range(6)]

    def front(t):
        bi, ri = t % 2, t % 3
        p.dma("sync", f"x{ri}", lambda e, t=t, ri=ri: e.dma_start(
            out=xt[ri][:], in_=x_d.ap()[t * 128:(t + 1) * 128, :]),
            outs=[f"xt{ri}"])
        ln_stats_and_u(p, xt[ri][:], f"xt{ri}", u32[bi][:], "u32s",
                       stats, "st", bi, u32[bi][:], "u32s", rstd_via="pool")
        for h in range(2):
            emit_quant_fp4(p, u32[bi][:, h * 512:(h + 1) * 512], "u32s",
                           xqb[bi][:, h * 512:(h + 1) * 512], "xqbs",
                           scr, "qs", 32)
        for ec in range(NC):
            pk = "ptr" + str(ec % 2)
            p.op("tensor", lambda e, ec=ec, bi=bi: e.transpose(
                ptr[ec % 2][:], xqb[bi][:, ec * 128:(ec + 1) * 128], eye[:]),
                ins=["xqbs", "eye"], outs=[pk])
            p.op("scalar", lambda e, ec=ec, ri=ri: e.activation(
                out=xqf[ri][:, ec * 128:(ec + 1) * 128],
                in_=ptr[ec % 2][:], func=AF.Identity, bias=np.float32(0.0)),
                ins=[pk], outs=[f"xqf{ri}_{ec}"])

    def back_mid(t):
        bi, ri = t % 2, t % 3
        xqf_all = [f"xqf{ri}_{ec}" for ec in range(NC)]
        for k, (i0, i1) in enumerate(ISL):
            isl = slice(i0, i1)
            pgk, kpg = pg[k % 2], f"pg{k % 2}"
            puk, kpu = pu[k % 2], f"pu{k % 2}"

            def mmg(e, isl=isl, i0=i0, i1=i1, ri=ri, pgk=pgk):
                for ec in range(NC):
                    inst = e.matmul(pgk[:, 0:i1 - i0],
                                    xqf[ri][:, ec * 128:(ec + 1) * 128],
                                    wgT[:, ec * I_DIM:(ec + 1) * I_DIM][:, isl],
                                    start=(ec == 0), stop=(ec == NC - 1))
                return inst
            p.op("tensor", mmg, ins=xqf_all + wgT_all, outs=[kpg])
            p.op("scalar", lambda e, isl=isl, i0=i0, i1=i1, bi=bi,
                 pgk=pgk: e.activation(
                out=gs[bi][:, isl], in_=pgk[:, 0:i1 - i0], func=AF.Silu),
                ins=[kpg], outs=[f"gs{bi}:{k}"])

            def mmu(e, isl=isl, i0=i0, i1=i1, ri=ri, puk=puk):
                for ec in range(NC):
                    inst = e.matmul(puk[:, 0:i1 - i0],
                                    xqf[ri][:, ec * 128:(ec + 1) * 128],
                                    wuT[:, ec * I_DIM:(ec + 1) * I_DIM][:, isl],
                                    start=(ec == 0), stop=(ec == NC - 1))
                return inst
            p.op("tensor", mmu, ins=xqf_all + wuT_all, outs=[kpu])
            usl = slice((k % 2) * 512, (k % 2) * 512 + (i1 - i0))
            p.op("scalar", lambda e, usl=usl, i0=i0, i1=i1, puk=puk: e.activation(
                out=upe[:, usl], in_=puk[:, 0:i1 - i0], func=AF.Identity,
                bias=np.float32(0.0)),
                ins=[kpu], outs=[f"upe{k % 2}"])
            p.op("gpsimd", lambda e, isl=isl, usl=usl, bi=bi: e.tensor_tensor(
                gs[bi][:, isl], upe[:, usl], gs[bi][:, isl], ALU.mult),
                ins=[f"upe{k % 2}", f"gs{bi}:{k}"], outs=[f"gs{bi}:{k}"])
            emit_quant_fp4(p, gs[bi][:, isl], [f"gs{bi}:{k}"],
                           hq[bi][:, isl], f"hq{bi}s_{k}", scr, "qs",
                           (i1 - i0) // 16)

    def back_down(t):
        bi, ri = t % 2, t % 3
        for ic in range(NI):
            pk = "ptr" + str(ic % 2)
            hqk = f"hq{bi}s_{min(ic // 4, 5)}"
            p.op("tensor", lambda e, ic=ic, bi=bi: e.transpose(
                ptr[ic % 2][:], hq[bi][:, ic * 128:(ic + 1) * 128], eye[:]),
                ins=[hqk, "eye"], outs=[pk])
            p.op("scalar", lambda e, ic=ic: e.activation(
                out=hqT[:, ic * 128:(ic + 1) * 128],
                in_=ptr[ic % 2][:], func=AF.Identity, bias=np.float32(0.0)),
                ins=[pk], outs=[f"hqT{ic}"])
        hqT_all = [f"hqT{ic}" for ic in range(NI)]
        for fs in range(2):
            fsl = slice(fs * 512, (fs + 1) * 512)
            psx, kpsx = (pd0, "pd0") if fs == 0 else (pd1, "pd1")

            def mmd(e, fsl=fsl, psx=psx):
                for ic in range(NI):
                    inst = e.matmul(psx[:], hqT[:, ic * 128:(ic + 1) * 128],
                                    wdT[:, ic * D:(ic + 1) * D][:, fsl],
                                    start=(ic == 0), stop=(ic == NI - 1))
                return inst
            p.op("tensor", mmd, ins=hqT_all + wdT_all, outs=[kpsx])
            p.op("vector", lambda e, ri=ri, fsl=fsl, psx=psx: e.tensor_add(
                xt[ri][:, fsl], psx[:], xt[ri][:, fsl]),
                ins=[kpsx, f"xt{ri}"], outs=[f"xt{ri}"])
        p.dma("sync", "st", lambda e, t=t, ri=ri: e.dma_start(
            out=y_d.ap()[t * 128:(t + 1) * 128, :], in_=xt[ri][:]),
            ins=[f"xt{ri}"], outs=[f"y{t}"])

    front(0)
    front(1)
    back_mid(0)
    for t in range(NT):
        if t + 2 < NT:
            front(t + 2)
        if t + 1 < NT:
            back_mid(t + 1)
        back_down(t)


def build_ffn1():
    nc = bass.Bass("TRN2", target_bir_lowering=False)
    ctx = ExitStack()
    x_d = nc.dram_tensor("x", [T, D], F32, kind="ExternalInput")
    y_d = nc.dram_tensor("y", [T, D], F32, kind="ExternalOutput")
    eye_d = nc.inline_tensor(np.eye(128, dtype=BFNP), name="eye_bf")
    dram = declare_ffn1_dram(nc, "", x_d, y_d, eye_d)
    with ctx:
        p = Prog(nc, ctx)
        emit_ffn1(p, dram)
        p.barrier()
        p.emit()
    return nc



def _mdiff_np():
    c = np.arange(WM, dtype=np.float32)[None, :]
    i = np.arange(128, dtype=np.float32)[:, None]
    return np.ascontiguousarray(np.abs(c - (T - 1) - i).astype(np.float32))


def _attn_host_inputs(Wq, bq, Wk, bk, Wv, bv, Wo, bo):
    d = {}
    d["wqT"] = np.ascontiguousarray(Wq.T)
    d["wkT"] = np.ascontiguousarray(Wk.T)
    d["wvT"] = np.ascontiguousarray(Wv.T)
    d["woT"] = np.ascontiguousarray(Wo.T)
    d["bq_pc"] = np.ascontiguousarray(bq.reshape(NC, 128).T)
    d["bk_pc"] = np.ascontiguousarray(bk.reshape(NC, 128).T)
    d["bv_row"] = np.ascontiguousarray(bv.reshape(1, D))
    d["bo_row"] = np.ascontiguousarray(bo.reshape(1, D))
    return d


_progs = {}


def _get_prog(name, builder):
    if name not in _progs:
        _progs[name] = builder()
    return _progs[name]


def build_fused():
    nc = bass.Bass("TRN2", target_bir_lowering=False)
    ctx = ExitStack()
    x0 = nc.dram_tensor("x", [T, D], F32, kind="ExternalInput")
    y = nc.dram_tensor("y", [T, D], F32, kind="ExternalOutput")
    x1 = nc.dram_tensor("xs1", [T, D], F32, kind="Internal")
    x2 = nc.dram_tensor("xs2", [T, D], F32, kind="Internal")
    x3 = nc.dram_tensor("xs3", [T, D], F32, kind="Internal")
    eye_d = nc.inline_tensor(np.eye(128, dtype=BFNP), name="eye_bf")
    dA = declare_attn_dram(nc, 0, "a0_", x0, x1, eye_d)
    dB = declare_ffn0_dram(nc, "f0_", x1, x2, eye_d)
    dC = declare_attn_dram(nc, 1, "a1_", x2, x3, eye_d)
    dD = declare_ffn1_dram(nc, "f1_", x3, y, eye_d)
    with ctx:
        p = Prog(nc, ctx)
        with ExitStack() as prepctx0:
            p.pre, p.actx = "f0p_", prepctx0
            bufs0 = alloc_prep_bufs(p, slab=256, nm="p0_")
            gen0 = gen_wprep(p, dB["w"], dB["stg"], *bufs0, 6, NC, D, 256,
                             tag="f0_")
            with ExitStack() as actx:
                p.pre, p.actx = "a0_", actx
                emit_attn(p, dA, 0, prep2=gen0)
                for _ in gen0:
                    pass
                p.barrier()
        with ExitStack() as actx:
            p.pre, p.actx = "f0_", actx
            emit_ffn0(p, dB, prep_done=True)
            p.barrier()
        with ExitStack() as prepctx:
            p.pre, p.actx = "f1_", prepctx
            scrW, wchW, wqbW = alloc_prep_bufs(p, slab=512)
            gen = itertools.chain(*f1_prep_gens(p, dD, scrW, wchW, wqbW,
                                                slab_gu=512, slab_d=352))
            with ExitStack() as actx:
                p.pre, p.actx = "a1_", actx
                emit_attn(p, dC, 1, prep=gen)
                for _ in gen:
                    pass
                p.barrier()
        with ExitStack() as actx:
            p.pre, p.actx = "f1_", actx
            emit_ffn1(p, dD, prep_done=True)
            p.barrier()
        p.pre, p.actx = "", ctx
        p.emit()
    return nc


def kernel(L0_attn_x, L0_attn_Wq, L0_attn_bq, L0_attn_Wk, L0_attn_bk,
           L0_attn_Wv, L0_attn_bv, L0_attn_Wo, L0_attn_bo,
           L0_attn_decay_params, L0_attn_mix_weights,
           L0_ffn_weight, L0_ffn_bias,
           L1_attn_Wq, L1_attn_bq, L1_attn_Wk, L1_attn_bk, L1_attn_Wv,
           L1_attn_bv, L1_attn_Wo, L1_attn_bo, L1_attn_ln_gamma,
           L1_attn_ln_beta, L1_ffn_w_gate, L1_ffn_w_up, L1_ffn_w_down):
    ncore = 8
    f32 = np.float32
    x = np.ascontiguousarray(np.asarray(L0_attn_x, dtype=f32))
    cores = list(range(ncore))

    ins = {}
    for pre, args in (
            ("a0_", (L0_attn_Wq, L0_attn_bq, L0_attn_Wk, L0_attn_bk,
                     L0_attn_Wv, L0_attn_bv, L0_attn_Wo, L0_attn_bo)),
            ("a1_", (L1_attn_Wq, L1_attn_bq, L1_attn_Wk, L1_attn_bk,
                     L1_attn_Wv, L1_attn_bv, L1_attn_Wo, L1_attn_bo))):
        h = _attn_host_inputs(*[np.asarray(a, f32) for a in args])
        ins.update({pre + k: v for k, v in h.items()})
    ins["a0_dpmix"] = np.ascontiguousarray(np.concatenate(
        [np.asarray(L0_attn_decay_params, f32).reshape(1, 32),
         np.asarray(L0_attn_mix_weights, f32).reshape(1, 32)], axis=1))
    ins["a0_mdiff"] = _mdiff_np()
    ins["a1_gb"] = np.ascontiguousarray(np.concatenate(
        [np.asarray(L1_attn_ln_gamma, f32),
         np.asarray(L1_attn_ln_beta, f32)]).reshape(1, 2 * D))
    ins["f0_w"] = np.ascontiguousarray(np.asarray(L0_ffn_weight, f32))
    ins["f0_b_row"] = np.ascontiguousarray(np.asarray(L0_ffn_bias, f32).reshape(1, D))
    ins["f1_wg"] = np.ascontiguousarray(np.asarray(L1_ffn_w_gate, f32))
    ins["f1_wu"] = np.ascontiguousarray(np.asarray(L1_ffn_w_up, f32))
    ins["f1_wd"] = np.ascontiguousarray(np.asarray(L1_ffn_w_down, f32))

    ncF = _get_prog("fused", build_fused)
    res = run_bass_kernel_spmd(
        ncF, [dict(ins, x=np.ascontiguousarray(x[b])) for b in cores],
        core_ids=cores)
    return np.stack([np.asarray(res.results[b]["y"]) for b in cores],
                    axis=0).astype(np.float32)


# version 25
# speedup vs baseline: 1.0361x; 1.0009x over previous
import itertools
import numpy as np
import ml_dtypes
from contextlib import ExitStack

import concourse.bass as bass
import concourse.mybir as mybir
from concourse.bass_utils import run_bass_kernel_spmd

F32 = mybir.dt.float32
BF16 = mybir.dt.bfloat16
F16 = mybir.dt.float16
AF = mybir.ActivationFunctionType
ALU = mybir.AluOpType
AX = mybir.AxisListType
BFNP = ml_dtypes.bfloat16

T = 2048
D = 1024
NT = T // 128
NC = D // 128
NH = 16
HD = 64
NHP = 8
WM = 2 * T - 1
LNEPS = 1e-5

ENGINES = ("sync", "scalar", "vector", "tensor", "gpsimd")


class Prog:

    def __init__(self, nc, ctx):
        self.nc, self.ctx = nc, ctx
        self.pre = ""
        self.actx = ctx
        self.sem = {e: ctx.enter_context(nc.semaphore("S_" + e)) for e in ENGINES}
        self.cnt = {e: 0 for e in ENGINES}
        self.dsem = {}
        self.dcnt = {}
        self.writer = {}
        self.readers = {}
        self.waited = {e: {} for e in ENGINES}
        self.recs = {e: [] for e in ENGINES}

    def chan(self, name):
        if name not in self.dsem:
            self.dsem[name] = self.ctx.enter_context(self.nc.semaphore("D_" + name))
            self.dcnt[name] = 0
        return name

    def _need(self, eng, ins, outs):
        need = []
        for k in ins:
            w = self.writer.get(k)
            if w:
                need.append(w)
        for k in outs:
            w = self.writer.get(k)
            if w:
                need.append(w)
            need += self.readers.get(k, ())
        best = {}
        for typ, a, c in need:
            if best.get((typ, a), -1) < c:
                best[(typ, a)] = c
        waits = []
        wt = self.waited[eng]
        for (typ, a), c in sorted(best.items()):
            if typ == "d":
                c = self.dcnt[a]
            if wt.get((typ, a), 0) >= c:
                continue
            wt[(typ, a)] = c
            waits.append((typ, a, c))
        return waits

    def _book(self, me, ins, outs):
        for k in outs:
            self.writer[k] = me
            self.readers[k] = []
        for k in ins:
            self.readers.setdefault(k, []).append(me)

    def op(self, eng, fn, ins=(), outs=()):
        waits = self._need(eng, ins, outs)
        self.cnt[eng] += 1
        self._book(("e", eng, self.cnt[eng]), ins, outs)
        self.recs[eng].append((waits, fn, None))

    def dma(self, eng, ch, fn, ins=(), outs=()):
        ch = self.chan(ch)
        waits = self._need(eng, ins, outs)
        self.dcnt[ch] += 1
        self._book(("d", ch, self.dcnt[ch]), ins, outs)
        self.recs[eng].append((waits, fn, ch))

    def barrier(self):
        snap_e = dict(self.cnt)
        snap_d = dict(self.dcnt)
        for e in ENGINES:
            waits = []
            wt = self.waited[e]
            for e2 in ENGINES:
                c = snap_e[e2]
                if e2 != e and c > 0 and wt.get(("e", e2), 0) < c:
                    wt[("e", e2)] = c
                    waits.append(("e", e2, c))
            for ch, c in snap_d.items():
                if c > 0 and wt.get(("d", ch), 0) < c:
                    wt[("d", ch)] = c
                    waits.append(("d", ch, c))
            if waits:
                self.recs[e].append((waits, None, None))
        self.writer = {}
        self.readers = {}

    def emit(self):
        block = self.ctx.enter_context(self.nc.Block())
        for eng in ENGINES:
            recs = self.recs[eng]

            def mk(recs_, eng_):
                def body(e):
                    for waits, fn, ch in recs_:
                        for typ, a, c in waits:
                            if typ == "e":
                                e.wait_ge(self.sem[a], c)
                            else:
                                e.wait_ge(self.dsem[a], 16 * c)
                        if fn is None:
                            continue
                        inst = fn(e)
                        if ch is None:
                            inst.then_inc(self.sem[eng_], 1)
                        else:
                            inst.then_inc(self.dsem[ch], 16)
                return body

            getattr(block, eng)(mk(recs, eng))

    def sb(self, name, shape, dtype):
        return self.actx.enter_context(
            self.nc.sbuf_tensor(self.pre + name, shape, dtype))

    def ps(self, name, shape, dtype=F32):
        return self.actx.enter_context(
            self.nc.psum_tensor(self.pre + name, shape, dtype))



def ln_stats_and_u(p, xt_ap, xkey, sq_scr, sqkey, stats, stkey, col, u_out, ukey,
                   extra_u=None, rstd_via="act"):
    ncol = stats.shape[1] // 4
    s1 = stats[:, col:col + 1]
    rstd = stats[:, ncol + col:ncol + col + 1]
    negmr = stats[:, 2 * ncol + col:2 * ncol + col + 1]
    ssq = stats[:, 3 * ncol + col:3 * ncol + col + 1]
    kst = stkey + str(col) + "_"
    p.op("vector", lambda e: e.reduce_sum(s1, xt_ap, AX.X), ins=[xkey], outs=[kst + "s1"])
    p.op("vector", lambda e: e.tensor_scalar_mul(s1, s1, -1.0 / D),
         ins=[kst + "s1"], outs=[kst + "s1"])
    p.op("scalar", lambda e: e.activation(out=sq_scr, in_=xt_ap, func=AF.Square,
                                          bias=s1, accum_out=ssq),
         ins=[xkey, kst + "s1"], outs=[sqkey, kst + "ssq"])
    if rstd_via == "act":
        p.op("scalar", lambda e: e.activation(out=rstd, in_=ssq, func=AF.Ln,
                                              scale=np.float32(1.0 / D),
                                              bias=np.float32(LNEPS)),
             ins=[kst + "ssq"], outs=[kst + "rstd"])
        p.op("scalar", lambda e: e.activation(out=rstd, in_=rstd, func=AF.Exp,
                                              scale=np.float32(-0.5)),
             ins=[kst + "rstd"], outs=[kst + "rstd"])
    else:
        p.op("vector", lambda e: e.tensor_scalar(rstd, ssq, 1.0 / D, LNEPS,
                                                 ALU.mult, ALU.add),
             ins=[kst + "ssq"], outs=[kst + "rstd"])
        p.op("gpsimd", lambda e: e.tensor_tensor(rstd, rstd, p.mh[:], ALU.pow),
             ins=[kst + "rstd", "mh"], outs=[kst + "rstd"])
    p.op("vector", lambda e: e.tensor_mul(negmr, s1, rstd),
         ins=[kst + "s1", kst + "rstd"], outs=[kst + "negmr"])
    p.op("scalar", lambda e: e.activation(out=u_out, in_=xt_ap, func=AF.Identity,
                                          scale=rstd, bias=negmr),
         ins=[xkey, kst + "rstd", kst + "negmr"], outs=[ukey])
    if extra_u is not None:
        ap2, key2 = extra_u
        p.op("scalar", lambda e: e.activation(out=ap2, in_=xt_ap, func=AF.Identity,
                                              scale=rstd, bias=negmr),
             ins=[xkey, kst + "rstd", kst + "negmr"], outs=[key2])


def transpose_tile_to_fm(p, u_src_fn, ukey, t, fm_sb, fmkey, ptr, eye, tcols=T):
    for c in range(NC):
        pk = "ptr" + str(c % 2)
        p.op("tensor",
             lambda e, c=c: e.transpose(ptr[c % 2][:], u_src_fn(c), eye[:]),
             ins=[ukey, "eye"], outs=[pk])
        p.op("vector",
             lambda e, c=c: e.tensor_copy(
                 fm_sb[:, c * tcols + t * 128: c * tcols + (t + 1) * 128],
                 ptr[c % 2][:]),
             ins=[pk], outs=[f"{fmkey}:{c}:{t}"])


def fm_keys(fmkey, ncs=NC, nts=NT):
    return [f"{fmkey}:{c}:{t}" for c in range(ncs) for t in range(nts)]



def emit_quant_fp4(p, x_ap, xkey, out_ap, outkey, scr, kb, nblk, blk=16):
    if isinstance(xkey, str):
        xkey = [xkey]
    W = nblk * blk
    bm, y, u, a, b = scr["bm"], scr["y"], scr["u"], scr["a"], scr["b"]
    xg = x_ap.rearrange("p (n i) -> p n i", i=blk)
    p.op("vector", lambda e: e.tensor_reduce(bm[:, 0:nblk].unsqueeze(2), xg, AX.X,
                                             ALU.max, apply_absolute_value=True),
         ins=xkey, outs=[kb + "bm"])
    p.op("vector", lambda e: e.tensor_scalar(bm[:, nblk:2 * nblk], bm[:, 0:nblk],
                                             1.0 / 6.0, 1e-12, ALU.mult, ALU.max),
         ins=[kb + "bm"], outs=[kb + "sc"])
    p.op("vector", lambda e: e.reciprocal(bm[:, 2 * nblk:3 * nblk],
                                          bm[:, nblk:2 * nblk]),
         ins=[kb + "sc"], outs=[kb + "rs"])
    rb = bm[:, 2 * nblk:3 * nblk].unsqueeze(2).broadcast_to([128, nblk, blk])
    scb = bm[:, nblk:2 * nblk].unsqueeze(2).broadcast_to([128, nblk, blk])
    yg = y[:, 0:W].rearrange("p (n i) -> p n i", i=blk)
    p.op("vector", lambda e: e.tensor_tensor(yg, xg, rb, ALU.mult),
         ins=xkey + [kb + "rs"], outs=[kb + "y"])
    p.op("vector", lambda e: e.scalar_tensor_tensor(
        out=u[:, 0:W], in0=y[:, 0:W], scalar=-1.0, in1=y[:, 0:W],
        op0=ALU.mult, op1=ALU.max),
         ins=[kb + "y"], outs=[kb + "u"])
    p.op("vector", lambda e: e.tensor_single_scalar(a[:, 0:W], u[:, 0:W], 0.25,
                                                    ALU.is_gt),
         ins=[kb + "u"], outs=[kb + "a"])
    for th in (0.75, 1.25, 1.75):
        p.op("vector", lambda e, th=th: e.scalar_tensor_tensor(
            out=a[:, 0:W], in0=u[:, 0:W], scalar=th, in1=a[:, 0:W],
            op0=ALU.is_gt, op1=ALU.add), ins=[kb + "u", kb + "a"], outs=[kb + "a"])
    p.op("vector", lambda e: e.tensor_single_scalar(b[:, 0:W], u[:, 0:W], 2.5,
                                                    ALU.is_gt),
         ins=[kb + "u"], outs=[kb + "b"])
    p.op("vector", lambda e: e.scalar_tensor_tensor(
        out=b[:, 0:W], in0=u[:, 0:W], scalar=3.5, in1=b[:, 0:W],
        op0=ALU.is_gt, op1=ALU.add), ins=[kb + "u", kb + "b"], outs=[kb + "b"])
    p.op("vector", lambda e: e.tensor_single_scalar(u[:, 0:W], u[:, 0:W], 5.0,
                                                    ALU.is_gt),
         ins=[kb + "u"], outs=[kb + "cc"])
    p.op("vector", lambda e: e.scalar_tensor_tensor(
        out=b[:, 0:W], in0=u[:, 0:W], scalar=2.0, in1=b[:, 0:W],
        op0=ALU.mult, op1=ALU.add), ins=[kb + "cc", kb + "b"], outs=[kb + "b"])
    p.op("vector", lambda e: e.scalar_tensor_tensor(
        out=b[:, 0:W], in0=a[:, 0:W], scalar=0.5, in1=b[:, 0:W],
        op0=ALU.mult, op1=ALU.add), ins=[kb + "a", kb + "b"], outs=[kb + "b"])
    bg = b[:, 0:W].rearrange("p (n i) -> p n i", i=blk)
    p.op("vector", lambda e: e.tensor_tensor(bg, bg, scb, ALU.mult),
         ins=[kb + "b", kb + "sc"], outs=[kb + "b"])
    p.op("vector", lambda e: e.tensor_scalar(u[:, 0:W], y[:, 0:W], 0.0, 2.0,
                                             ALU.is_gt, ALU.mult),
         ins=[kb + "y"], outs=[kb + "ssel"])
    p.op("vector", lambda e: e.tensor_mul(a[:, 0:W], b[:, 0:W], u[:, 0:W]),
         ins=[kb + "b", kb + "ssel"], outs=[kb + "d2"])
    p.op("vector", lambda e: e.tensor_sub(out_ap, a[:, 0:W], b[:, 0:W]),
         ins=[kb + "d2", kb + "b"], outs=[outkey])


def emit_quant_fp6(p, x_ap, xkey, out_ap, outkey, scr, kb, nblk, blk=32):
    if isinstance(xkey, str):
        xkey = [xkey]
    W = nblk * blk
    bm, y, u, a, b, s_t = scr["bm"], scr["y"], scr["u"], scr["a"], scr["b"], scr["s"]
    xg = x_ap.rearrange("p (n i) -> p n i", i=blk)
    p.op("vector", lambda e: e.tensor_reduce(bm[:, 0:nblk].unsqueeze(2), xg, AX.X,
                                             ALU.max, apply_absolute_value=True),
         ins=xkey, outs=[kb + "bm"])
    p.op("vector", lambda e: e.tensor_scalar(bm[:, nblk:2 * nblk], bm[:, 0:nblk],
                                             1.0 / 7.5, 1e-30, ALU.mult, ALU.max),
         ins=[kb + "bm"], outs=[kb + "sc"])
    p.op("vector", lambda e: e.reciprocal(bm[:, 2 * nblk:3 * nblk],
                                          bm[:, nblk:2 * nblk]),
         ins=[kb + "sc"], outs=[kb + "rs"])
    rb = bm[:, 2 * nblk:3 * nblk].unsqueeze(2).broadcast_to([128, nblk, blk])
    scb = bm[:, nblk:2 * nblk].unsqueeze(2).broadcast_to([128, nblk, blk])
    yg = y[:, 0:W].rearrange("p (n i) -> p n i", i=blk)
    p.op("vector", lambda e: e.tensor_tensor(yg, xg, rb, ALU.mult),
         ins=xkey + [kb + "rs"], outs=[kb + "y"])
    p.op("vector", lambda e: e.scalar_tensor_tensor(
        out=u[:, 0:W], in0=y[:, 0:W], scalar=-1.0, in1=y[:, 0:W],
        op0=ALU.mult, op1=ALU.max),
         ins=[kb + "y"], outs=[kb + "u"])
    p.op("vector", lambda e: e.tensor_single_scalar(a[:, 0:W], u[:, 0:W], 2.0,
                                                    ALU.is_gt),
         ins=[kb + "u"], outs=[kb + "c1"])
    p.op("vector", lambda e: e.tensor_single_scalar(b[:, 0:W], u[:, 0:W], 3.875,
                                                    ALU.is_gt),
         ins=[kb + "u"], outs=[kb + "c2"])
    p.op("vector", lambda e: e.tensor_scalar(s_t[:, 0:W], a[:, 0:W], 0.125, 0.125,
                                             ALU.mult, ALU.add),
         ins=[kb + "c1"], outs=[kb + "s"])
    p.op("vector", lambda e: e.scalar_tensor_tensor(
        out=s_t[:, 0:W], in0=b[:, 0:W], scalar=0.25, in1=s_t[:, 0:W],
        op0=ALU.mult, op1=ALU.add), ins=[kb + "c2", kb + "s"], outs=[kb + "s"])
    p.op("vector", lambda e: e.tensor_scalar(a[:, 0:W], a[:, 0:W], -4.0, 8.0,
                                             ALU.mult, ALU.add),
         ins=[kb + "c1"], outs=[kb + "t"])
    p.op("vector", lambda e: e.scalar_tensor_tensor(
        out=a[:, 0:W], in0=b[:, 0:W], scalar=-2.0, in1=a[:, 0:W],
        op0=ALU.mult, op1=ALU.add), ins=[kb + "c2", kb + "t"], outs=[kb + "t"])
    p.op("vector", lambda e: e.tensor_mul(u[:, 0:W], u[:, 0:W], a[:, 0:W]),
         ins=[kb + "u", kb + "t"], outs=[kb + "z"])
    p.op("vector", lambda e: e.tensor_single_scalar(a[:, 0:W], u[:, 0:W], 7.5,
                                                    ALU.is_gt),
         ins=[kb + "z"], outs=[kb + "cs"])
    p.op("vector", lambda e: e.tensor_scalar_mul(b[:, 0:W], a[:, 0:W], 8.0),
         ins=[kb + "cs"], outs=[kb + "r"])
    p.op("vector", lambda e: e.scalar_tensor_tensor(
        out=u[:, 0:W], in0=a[:, 0:W], scalar=-8.0, in1=u[:, 0:W],
        op0=ALU.mult, op1=ALU.add), ins=[kb + "cs", kb + "z"], outs=[kb + "z"])
    for w in (4.0, 2.0, 1.0, 1.0):
        p.op("vector", lambda e, w=w: e.tensor_single_scalar(a[:, 0:W], u[:, 0:W],
                                                             w - 0.5, ALU.is_gt),
             ins=[kb + "z"], outs=[kb + "cs"])
        p.op("vector", lambda e, w=w: e.scalar_tensor_tensor(
            out=b[:, 0:W], in0=a[:, 0:W], scalar=w, in1=b[:, 0:W],
            op0=ALU.mult, op1=ALU.add), ins=[kb + "cs", kb + "r"], outs=[kb + "r"])
        p.op("vector", lambda e, w=w: e.scalar_tensor_tensor(
            out=u[:, 0:W], in0=a[:, 0:W], scalar=-w, in1=u[:, 0:W],
            op0=ALU.mult, op1=ALU.add), ins=[kb + "cs", kb + "z"], outs=[kb + "z"])
    p.op("vector", lambda e: e.tensor_mul(b[:, 0:W], b[:, 0:W], s_t[:, 0:W]),
         ins=[kb + "r", kb + "s"], outs=[kb + "q"])
    bg = b[:, 0:W].rearrange("p (n i) -> p n i", i=blk)
    p.op("vector", lambda e: e.tensor_tensor(bg, bg, scb, ALU.mult),
         ins=[kb + "q", kb + "sc"], outs=[kb + "q"])
    p.op("vector", lambda e: e.tensor_scalar(u[:, 0:W], y[:, 0:W], 0.0, 2.0,
                                             ALU.is_gt, ALU.mult),
         ins=[kb + "y"], outs=[kb + "ssel"])
    p.op("vector", lambda e: e.tensor_mul(a[:, 0:W], b[:, 0:W], u[:, 0:W]),
         ins=[kb + "q", kb + "ssel"], outs=[kb + "d2"])
    p.op("vector", lambda e: e.tensor_sub(out_ap, a[:, 0:W], b[:, 0:W]),
         ins=[kb + "d2", kb + "q"], outs=[outkey])




def _pool_quant_pre(p, x_ap, xkey, scr, kb, nblk, blk, qmax, eps):
    if isinstance(xkey, str):
        xkey = [xkey]
    W = nblk * blk
    bm, y, u = scr["bm"], scr["y"], scr["u"]
    xg = x_ap.rearrange("p (n i) -> p n i", i=blk)
    p.op("vector", lambda e: e.tensor_reduce(bm[:, 0:nblk].unsqueeze(2), xg, AX.X,
                                             ALU.max, apply_absolute_value=True),
         ins=xkey, outs=[kb + "bm"])
    p.op("vector", lambda e: e.tensor_scalar(bm[:, nblk:2 * nblk], bm[:, 0:nblk],
                                             1.0 / qmax, eps, ALU.mult, ALU.max),
         ins=[kb + "bm"], outs=[kb + "sc"])
    p.op("vector", lambda e: e.reciprocal(bm[:, 2 * nblk:3 * nblk],
                                          bm[:, nblk:2 * nblk]),
         ins=[kb + "sc"], outs=[kb + "rs"])
    rb = bm[:, 2 * nblk:3 * nblk].unsqueeze(2).broadcast_to([128, nblk, blk])
    yg = y[:, 0:W].rearrange("p (n i) -> p n i", i=blk)
    p.op("gpsimd", lambda e: e.tensor_tensor(yg, xg, rb, ALU.mult),
         ins=xkey + [kb + "rs"], outs=[kb + "y"])
    a = scr["a"]
    p.op("gpsimd", lambda e: e.tensor_single_scalar(a[:, 0:W], y[:, 0:W], 0.0,
                                                    ALU.is_gt),
         ins=[kb + "y"], outs=[kb + "pm"])
    p.op("gpsimd", lambda e: e.tensor_scalar(a[:, 0:W], a[:, 0:W], 2.0, -1.0,
                                             ALU.mult, ALU.add),
         ins=[kb + "pm"], outs=[kb + "sgn"])
    p.op("gpsimd", lambda e: e.tensor_tensor(u[:, 0:W], y[:, 0:W], a[:, 0:W],
                                             ALU.mult),
         ins=[kb + "y", kb + "sgn"], outs=[kb + "u"])


def emit_quant_fp4_pool(p, x_ap, xkey, out_ap, outkey, scr, kb, nblk, blk=16):
    W = nblk * blk
    bm, u, a, b, q = scr["bm"], scr["u"], scr["a"], scr["b"], scr["y"]
    _pool_quant_pre(p, x_ap, xkey, scr, kb, nblk, blk, 6.0, 1e-12)
    p.op("vector", lambda e: e.tensor_scalar_mul(bm[:, 3 * nblk:4 * nblk],
                                                 bm[:, nblk:2 * nblk], 0.5),
         ins=[kb + "sc"], outs=[kb + "sc2"])
    p.op("gpsimd", lambda e: e.tensor_single_scalar(b[:, 0:W], u[:, 0:W], 0.25,
                                                    ALU.is_gt),
         ins=[kb + "u"], outs=[kb + "q2"])
    for th in (0.75, 1.25, 1.75):
        p.op("gpsimd", lambda e, th=th: e.tensor_single_scalar(
            q[:, 0:W], u[:, 0:W], th, ALU.is_gt),
            ins=[kb + "u"], outs=[kb + "c"])
        p.op("gpsimd", lambda e: e.tensor_tensor(b[:, 0:W], b[:, 0:W], q[:, 0:W],
                                                 ALU.add),
             ins=[kb + "c", kb + "q2"], outs=[kb + "q2"])
    for th, wgt in ((2.5, 2.0), (3.5, 2.0), (5.0, 4.0)):
        p.op("gpsimd", lambda e, th=th, wgt=wgt: e.tensor_scalar(
            q[:, 0:W], u[:, 0:W], th, wgt, ALU.is_gt, ALU.mult),
            ins=[kb + "u"], outs=[kb + "c"])
        p.op("gpsimd", lambda e: e.tensor_tensor(b[:, 0:W], b[:, 0:W], q[:, 0:W],
                                                 ALU.add),
             ins=[kb + "c", kb + "q2"], outs=[kb + "q2"])
    scb = bm[:, 3 * nblk:4 * nblk].unsqueeze(2).broadcast_to([128, nblk, blk])
    bg = b[:, 0:W].rearrange("p (n i) -> p n i", i=blk)
    p.op("gpsimd", lambda e: e.tensor_tensor(bg, bg, scb, ALU.mult),
         ins=[kb + "q2", kb + "sc2"], outs=[kb + "d1"])
    p.op("gpsimd", lambda e: e.tensor_tensor(out_ap, b[:, 0:W], a[:, 0:W],
                                             ALU.mult),
         ins=[kb + "d1", kb + "sgn"], outs=[outkey])


def emit_quant_fp6_pool(p, x_ap, xkey, out_ap, outkey, scr, kb, nblk, blk=32):
    W = nblk * blk
    bm, u, a, b, q, s_t = (scr["bm"], scr["u"], scr["a"], scr["b"], scr["y"],
                           scr["s"])
    _pool_quant_pre(p, x_ap, xkey, scr, kb, nblk, blk, 7.5, 1e-30)
    p.op("gpsimd", lambda e: e.tensor_single_scalar(q[:, 0:W], u[:, 0:W], 2.0,
                                                    ALU.is_gt),
         ins=[kb + "u"], outs=[kb + "c1"])
    p.op("gpsimd", lambda e: e.tensor_scalar(b[:, 0:W], q[:, 0:W], -4.0, 8.0,
                                             ALU.mult, ALU.add),
         ins=[kb + "c1"], outs=[kb + "t"])
    p.op("gpsimd", lambda e: e.tensor_scalar(s_t[:, 0:W], q[:, 0:W], 0.125, 0.125,
                                             ALU.mult, ALU.add),
         ins=[kb + "c1"], outs=[kb + "s"])
    p.op("gpsimd", lambda e: e.tensor_single_scalar(q[:, 0:W], u[:, 0:W], 3.875,
                                                    ALU.is_gt),
         ins=[kb + "u"], outs=[kb + "c2"])
    p.op("gpsimd", lambda e: e.tensor_scalar(q[:, 0:W], q[:, 0:W], 2.0, 0.0,
                                             ALU.mult, ALU.add),
         ins=[kb + "c2"], outs=[kb + "c22"])
    p.op("gpsimd", lambda e: e.tensor_tensor(b[:, 0:W], b[:, 0:W], q[:, 0:W],
                                             ALU.subtract),
         ins=[kb + "c22", kb + "t"], outs=[kb + "t"])
    p.op("gpsimd", lambda e: e.tensor_scalar(q[:, 0:W], q[:, 0:W], 0.125, 0.0,
                                             ALU.mult, ALU.add),
         ins=[kb + "c22"], outs=[kb + "s2"])
    p.op("gpsimd", lambda e: e.tensor_tensor(s_t[:, 0:W], s_t[:, 0:W], q[:, 0:W],
                                             ALU.add),
         ins=[kb + "s2", kb + "s"], outs=[kb + "s"])
    p.op("gpsimd", lambda e: e.tensor_tensor(u[:, 0:W], u[:, 0:W], b[:, 0:W],
                                             ALU.mult),
         ins=[kb + "u", kb + "t"], outs=[kb + "z"])
    p.op("gpsimd", lambda e: e.tensor_single_scalar(q[:, 0:W], u[:, 0:W], 7.5,
                                                    ALU.is_gt),
         ins=[kb + "z"], outs=[kb + "cs"])
    p.op("gpsimd", lambda e: e.tensor_scalar(b[:, 0:W], q[:, 0:W], 8.0, 0.0,
                                             ALU.mult, ALU.add),
         ins=[kb + "cs"], outs=[kb + "r"])
    p.op("gpsimd", lambda e: e.tensor_tensor(u[:, 0:W], u[:, 0:W], b[:, 0:W],
                                             ALU.subtract),
         ins=[kb + "r", kb + "z"], outs=[kb + "z"])
    for w in (4.0, 2.0, 1.0, 1.0):
        p.op("gpsimd", lambda e, w=w: e.tensor_scalar(
            q[:, 0:W], u[:, 0:W], w - 0.5, w, ALU.is_gt, ALU.mult),
            ins=[kb + "z"], outs=[kb + "csw"])
        p.op("gpsimd", lambda e: e.tensor_tensor(b[:, 0:W], b[:, 0:W], q[:, 0:W],
                                                 ALU.add),
             ins=[kb + "csw", kb + "r"], outs=[kb + "r"])
        p.op("gpsimd", lambda e: e.tensor_tensor(u[:, 0:W], u[:, 0:W], q[:, 0:W],
                                                 ALU.subtract),
             ins=[kb + "csw", kb + "z"], outs=[kb + "z"])
    p.op("gpsimd", lambda e: e.tensor_tensor(b[:, 0:W], b[:, 0:W], s_t[:, 0:W],
                                             ALU.mult),
         ins=[kb + "r", kb + "s"], outs=[kb + "qv"])
    scb = bm[:, nblk:2 * nblk].unsqueeze(2).broadcast_to([128, nblk, blk])
    bg = b[:, 0:W].rearrange("p (n i) -> p n i", i=blk)
    p.op("gpsimd", lambda e: e.tensor_tensor(bg, bg, scb, ALU.mult),
         ins=[kb + "qv", kb + "sc"], outs=[kb + "d1"])
    p.op("gpsimd", lambda e: e.tensor_tensor(out_ap, b[:, 0:W], a[:, 0:W],
                                             ALU.mult),
         ins=[kb + "d1", kb + "sgn"], outs=[outkey])


def gen_wprep(p, w_d, stg_d, scr, wch, wqb, qkind, nchunks, width, slab,
              tag="", engine="dve", unit_sel=None):
    emitq = {("dve", 6): emit_quant_fp6, ("dve", 4): emit_quant_fp4,
             ("pool", 6): emit_quant_fp6_pool,
             ("pool", 4): emit_quant_fp4_pool}[(engine, qkind)]
    blk = 32 if qkind == 6 else 16
    units = [(c, s0) for c in range(nchunks) for s0 in range(0, width, slab)]
    if unit_sel is not None:
        units = [u for i, u in enumerate(units) if unit_sel(i)]

    def load(i):
        c, s0 = units[i]
        bi = i % 2
        p.dma("sync", f"{tag}wprep{bi}", lambda e, c=c, s0=s0, bi=bi: e.dma_start(
            out=wch[bi][:, 0:slab],
            in_=w_d.ap()[c * 128:(c + 1) * 128, s0:s0 + slab]),
            outs=[f"{tag}wprep_ch{bi}"])

    load(0)
    for i, (c, s0) in enumerate(units):
        bi = i % 2
        if i + 1 < len(units):
            load(i + 1)
        emitq(p, wch[bi][:, 0:slab], f"{tag}wprep_ch{bi}", wqb[bi][:, 0:slab],
              f"{tag}wprep_qb{bi}", scr, tag + "qW", slab // blk)
        p.dma("sync", f"{tag}wstg{bi}", lambda e, c=c, s0=s0, bi=bi: e.dma_start(
            out=stg_d.ap()[c * 128:(c + 1) * 128, s0:s0 + slab],
            in_=wqb[bi][:, 0:slab]),
            ins=[f"{tag}wprep_qb{bi}"], outs=[f"stg_{stg_d.name}:{tag}{i}"])
        yield


def alloc_prep_bufs(p, slab=1408, nm="pw_"):
    scr = {k: p.sb(nm + k, [128, slab], F32) for k in ("y", "u", "a", "b", "s")}
    scr["bm"] = p.sb(nm + "bm", [128, 3 * 88], F32)
    wch = [p.sb(f"{nm}wch{i}", [128, slab], F32) for i in range(2)]
    wqb = [p.sb(f"{nm}wqb{i}", [128, slab], BF16) for i in range(2)]
    return scr, wch, wqb


_RR_END = object()


def rr_merge(*gens):
    gens = list(gens)
    while gens:
        for g in list(gens):
            v = next(g, _RR_END)
            if v is _RR_END:
                gens.remove(g)
            else:
                yield v


def f1_prep_gens(p, dram, scrW, wchW, wqbW, slab_gu=1024, slab_d=1408):
    return (gen_wprep(p, dram["wg"], dram["stg_g"], scrW, wchW, wqbW, 4, NI,
                      D, slab_gu),
            gen_wprep(p, dram["wu"], dram["stg_u"], scrW, wchW, wqbW, 4, NI,
                      D, slab_gu),
            gen_wprep(p, dram["wd"], dram["stg_dn"], scrW, wchW, wqbW, 4, NC,
                      I_DIM, slab_d))


def f1_prep_split(p, dram, bufsD, bufsP):
    scrD, wchD, wqbD = bufsD
    scrP, wchP, wqbP = bufsP
    pool = itertools.chain(
        gen_wprep(p, dram["wg"], dram["stg_g"], scrP, wchP, wqbP, 4, NI, D,
                  512, tag="fp_", engine="pool", unit_sel=lambda i: i % 3 != 2),
        gen_wprep(p, dram["wu"], dram["stg_u"], scrP, wchP, wqbP, 4, NI, D,
                  512, tag="fp_", engine="pool", unit_sel=lambda i: i % 3 != 2))
    dve = itertools.chain(
        gen_wprep(p, dram["wg"], dram["stg_g"], scrD, wchD, wqbD, 4, NI, D,
                  512, tag="fd_", engine="dve", unit_sel=lambda i: i % 3 == 2),
        gen_wprep(p, dram["wu"], dram["stg_u"], scrD, wchD, wqbD, 4, NI, D,
                  512, tag="fd_", engine="dve", unit_sel=lambda i: i % 3 == 2),
        gen_wprep(p, dram["wd"], dram["stg_dn"], scrD, wchD, wqbD, 4, NC,
                  I_DIM, 352, tag="fd_", engine="dve"))
    return pool, dve


def load_wT_staged(p, stg_d, dst, dstkey, ncols_chunks, chunk_rows,
                   eng="scalar"):
    W = chunk_rows
    for ec in range(ncols_chunks):
        p.dma(eng, f"wl{ec % 2}", lambda e, ec=ec: e.dma_start(
            out=dst[:, ec * W:(ec + 1) * W],
            in_=stg_d.ap()[:, ec * 128:(ec + 1) * 128], transpose=True),
            ins=[f"stg_{stg_d.name}"], outs=[f"{dstkey}:{ec}"])



def declare_attn_dram(nc, layer, pre, x_d, y_d, eye_d):
    d = {"x": x_d, "y": y_d, "eye": eye_d}
    for nm, shape in (("wqT", [D, D]), ("wkT", [D, D]), ("wvT", [D, D]),
                      ("woT", [D, D]), ("bq_pc", [128, NC]), ("bk_pc", [128, NC]),
                      ("bv_row", [1, D]), ("bo_row", [1, D])):
        d[nm] = nc.dram_tensor(pre + nm, shape, F32, kind="ExternalInput")
    if layer == 0:
        d["dpmix"] = nc.dram_tensor(pre + "dpmix", [1, 64], F32, kind="ExternalInput")
        d["mdiff"] = nc.dram_tensor(pre + "mdiff", [128, WM], F32, kind="ExternalInput")
    else:
        d["gb"] = nc.dram_tensor(pre + "gb", [1, 2 * D], F32, kind="ExternalInput")
    return d


def emit_attn(p, dram, layer, prep=None, prep2=None):
    def prep_step(n=1, site=3):
        g = prep2 if (site == 2 and prep2 is not None) else prep
        if g is not None:
            for _ in range(n):
                next(g, None)

    nc = p.nc
    x_d, y_d, eye_d = dram["x"], dram["y"], dram["eye"]
    wqT_d, wkT_d, wvT_d, woT_d = dram["wqT"], dram["wkT"], dram["wvT"], dram["woT"]
    p.mh = p.sb("mh", [128, 1], F32)
    p.op("gpsimd", lambda e: e.memset(p.mh[:], -0.5), outs=["mh"])
    bq_d, bk_d, bv_d, bo_d = dram["bq_pc"], dram["bk_pc"], dram["bv_row"], dram["bo_row"]
    if layer == 0:
        dpmix_d, mdiff_d = dram["dpmix"], dram["mdiff"]
    else:
        gb_d = dram["gb"]
    wqkv = [p.sb(f"wqkv{i}", [128, 2 * D], BF16) for i in range(2)]
    bq = p.sb("bq", [128, NC], F32)
    bk = p.sb("bk", [128, NC], F32)
    bvr = p.sb("bvr", [1, D], BF16)
    bor = p.sb("bor", [1, D], BF16)
    eye = p.sb("eye", [128, 128], BF16)
    ones = p.sb("ones", [128, 128], BF16)
    ones32 = p.sb("ones32", [1, 128], F32)
    u_fm = p.sb("u_fm", [128, NC * T], BF16)
    attn = p.sb("attn", [128, NHP * T], BF16)
    stats = p.sb("stats", [128, 4 * NT], F32)
    stats2 = p.sb("stats2", [128, 4 * 2], F32)
    stats3 = p.sb("stats3", [128, 4 * 2], F32)
    qp = p.sb("qp", [128, T], BF16)
    kp = p.sb("kp", [128, T], BF16)
    v_all = p.sb("v_all", [128, NT * 1040], BF16)
    ebf = p.sb("ebf", [128, 2 * 1024], BF16)
    msk = p.sb("msk", [128, 2 * 1024 if layer == 0 else 1024], F32)
    sq_scr = msk[:, 0:D]
    dsc = p.sb("dsc", [128, 512], F32)
    dbf = p.sb("dbf", [128, 512], BF16)
    asb = [p.sb(f"asb{i}", [128, 512], F32) for i in range(2)]
    dnb = [p.sb(f"dnb{i}", [128, 512], F32) for i in range(2)]
    sc_mask = ExitStack()
    if layer == 0:
        _oact = p.actx
        p.actx = sc_mask
        mdiff = p.sb("mdiff_s", [128, WM], F32)
        emask = p.sb("emask", [128, 2 * WM], F16)
        etmp = p.sb("etmp", [128, 1024], F32)
        ball = p.sb("ball", [128, 64], F32)
        brow = p.sb("brow", [1, 64], F32)
        mrow = p.sb("mrow", [1, 64], F32)
        p.actx = _oact
    pq = p.ps("pq", [128, 512], F32)
    po0 = p.ps("po0", [128, 512], F32)
    po1 = p.ps("po1", [128, 512], F32)

    p.dma("gpsimd", "c_bv", lambda e: e.dma_start(out=bvr[:], in_=bv_d.ap()),
          outs=["bvr"])
    p.dma("gpsimd", "c_bo", lambda e: e.dma_start(out=bor[:], in_=bo_d.ap()),
          outs=["bor"])
    p.dma("sync", "c_bq", lambda e: e.dma_start(out=bq[:], in_=bq_d.ap()),
          outs=["bq"])
    p.dma("sync", "c_bk", lambda e: e.dma_start(out=bk[:], in_=bk_d.ap()),
          outs=["bk"])
    p.dma("sync", "c_eye", lambda e: e.dma_start(out=eye[:], in_=eye_d.ap()),
          outs=["eye"])
    p.op("vector", lambda e: e.memset(ones[:], 1.0), outs=["ones"])
    p.op("vector", lambda e: e.memset(ones32[:], 1.0), outs=["ones32"])
    if layer == 0:
        p.dma("sync", "c_md", lambda e: e.dma_start(out=mdiff[:], in_=mdiff_d.ap()),
              outs=["mdiff"])
        p.dma("sync", "c_dm", lambda e: e.dma_start(out=mrow[0:1, :],
                                                    in_=dpmix_d.ap()),
              outs=["mrow"])

    if layer == 0:
        p.op("vector", lambda e: e.tensor_scalar(brow[0:1, 0:32], mrow[0:1, 0:32],
                                                 0.01, 0.9999, ALU.max, ALU.min),
             ins=["mrow"], outs=["dpc"])
        p.op("scalar", lambda e: e.activation(out=brow[0:1, 0:32],
                                              in_=brow[0:1, 0:32], func=AF.Ln),
             ins=["dpc"], outs=["lndp"])
        p.op("scalar", lambda e: e.activation(out=mrow[0:1, 32:64],
                                              in_=mrow[0:1, 32:64], func=AF.Exp),
             ins=["mrow"], outs=["mexp"])
        mv = mrow[0:1, 32:64].rearrange("p (h n) -> p h n", n=2)
        p.op("vector", lambda e: e.tensor_reduce(
            mrow[0:1, 16:32].unsqueeze(2), mv, AX.X, ALU.add),
             ins=["mexp"], outs=["msum"])
        p.op("vector", lambda e: e.reciprocal(mrow[0:1, 16:32], mrow[0:1, 16:32]),
             ins=["msum"], outs=["mrec"])
        mrb = mrow[0:1, 16:32].unsqueeze(2).broadcast_to([1, 16, 2])
        p.op("vector", lambda e: e.tensor_tensor(mv, mv, mrb, ALU.mult),
             ins=["mexp", "mrec"], outs=["mnorm"])
        p.op("scalar", lambda e: e.activation(out=brow[0:1, 32:64],
                                              in_=mrow[0:1, 32:64], func=AF.Ln),
             ins=["mnorm"], outs=["lnm"])
        p.op("vector", lambda e: e.tensor_scalar_add(
            brow[0:1, 32:64], brow[0:1, 32:64], float(-np.log(8.0))),
             ins=["lnm"], outs=["lnm"])
        p.op("tensor", lambda e: e.matmul(pq[:, 0:64], ones32[0:1, :],
                                          brow[0:1, :], start=True, stop=True),
             ins=["ones32", "lndp", "lnm"], outs=["pq"])
        p.op("scalar", lambda e: e.activation(out=ball[:], in_=pq[:, 0:64],
                                              func=AF.Identity, bias=np.float32(0.0)),
             ins=["pq"], outs=["ball"])

    outer_actx = p.actx
    with ExitStack() as sc1:
        p.actx = sc1
        ptr = [p.ps(f"ptr{i}", [128, 128], BF16) for i in range(2)]
        xt = [p.sb(f"xt{i}", [128, D], F32) for i in range(2)]
        ubf = [p.sb(f"ubf{i}", [128, D], BF16) for i in range(2)]
        p.actx = outer_actx
        for t in range(NT):
            bi = t % 2
            p.dma("sync", f"x{bi}", lambda e, t=t, bi=bi: e.dma_start(
                out=xt[bi][:], in_=x_d.ap()[t * 128:(t + 1) * 128, :]),
                outs=[f"xt{bi}"])
            ln_stats_and_u(p, xt[bi][:], f"xt{bi}", sq_scr[:], "sq_scr",
                           stats, "st", t, ubf[bi][:], f"ubf{bi}")
            transpose_tile_to_fm(p, lambda c, bi=bi: ubf[bi][:, c * 128:(c + 1) * 128],
                                 f"ubf{bi}", t, u_fm, "ufm", ptr, eye)
        p.barrier()
    psW = [p.ps("psW0", [128, 1024], F32), p.ps("psW1", [128, 1024], F32)]
    pq2 = p.ps("pq2", [128, 512], F32)
    ufm_all = fm_keys("ufm")

    with ExitStack() as scv:
        _oact = p.actx
        p.actx = scv
        wv_all = p.sb("wv_all", [128, NC * D], BF16)
        p.actx = _oact
        p.dma("gpsimd", "wvl", lambda e: e.dma_start(
            out=wv_all[:].rearrange("p (c f) -> p c f", c=NC),
            in_=wvT_d.ap().rearrange("(c p) f -> p c f", p=128)),
            outs=["wv_all"])
        v3o = v_all[:].rearrange("p (g c) -> p g c", c=130)
        p.op("vector", lambda e: e.memset(v3o[:, :, 64:65], 1.0), outs=["vo"])
        p.op("vector", lambda e: e.memset(v3o[:, :, 129:130], 1.0), outs=["vo"])
        for t in range(NT):
            for half in range(2):
                psel, kpsel = (pq, "pq") if half == 0 else (pq2, "pq2")

                def mmvq(e, t=t, half=half, psel=psel):
                    for c in range(NC):
                        e.matmul(psel[:],
                                 u_fm[:, c * T + t * 128: c * T + (t + 1) * 128],
                                 wv_all[:, c * D + half * 512: c * D + half * 512 + 512],
                                 start=(c == 0), stop=False)
                    return e.matmul(psel[:], ones[0:1, :],
                                    bvr[0:1, half * 512: (half + 1) * 512],
                                    start=False, stop=True)
                p.op("tensor", mmvq, ins=["wv_all", "bvr", "ones"] + ufm_all,
                     outs=[kpsel])
                v3 = v_all[:, t * 1040 + half * 520: t * 1040 + (half + 1) * 520] \
                    .rearrange("p (g c) -> p g c", c=130)
                p3 = psel[:].rearrange("p (g hc) -> p g hc", g=4)
                p.op("vector", lambda e, v3=v3, p3=p3: e.tensor_copy(
                    v3[:, :, 0:64], p3[:, :, 0:64]),
                    ins=[kpsel], outs=[f"va{t}:{half}"])
                p.op("vector", lambda e, v3=v3, p3=p3: e.tensor_copy(
                    v3[:, :, 65:129], p3[:, :, 64:128]),
                    ins=[kpsel], outs=[f"va{t}:{half}"])

    for hp in range(NHP):
        wb = hp % 2
        wkey = f"wqkv{wb}"
        for j, src in enumerate((wqT_d, wkT_d)):
            p.dma("gpsimd", f"wpair{wb}", lambda e, j=j, src=src, wb=wb, hp=hp:
                  e.dma_start(
                      out=wqkv[wb][:, j * D:(j + 1) * D],
                      in_=src.ap().rearrange("(c p) f -> p c f", p=128)
                      [:, :, hp * 128:(hp + 1) * 128]),
                  outs=[wkey + str(j)])
        qkeys = [f"qp{i}" for i in range(4)]
        kkeys = [f"kp{i}" for i in range(4)]
        for j, (bsb, bkey, dst, dks) in enumerate(
                ((bq, "bq", qp, "qp"), (bk, "bk", kp, "kp"))):
            for ts4 in range(4):
                sl = slice(ts4 * 512, (ts4 + 1) * 512)
                psel, kpsel = (pq, "pq") if ts4 % 2 == 0 else (pq2, "pq2")

                def mmq(e, j=j, sl=sl, wb=wb, psel=psel):
                    for c in range(NC):
                        inst = e.matmul(
                            psel[:], wqkv[wb][:, j * D + c * 128: j * D + c * 128 + 128],
                            u_fm[:, c * T:(c + 1) * T][:, sl],
                            start=(c == 0), stop=(c == NC - 1))
                    return inst
                p.op("tensor", mmq, ins=[wkey + str(j)] + ufm_all, outs=[kpsel])
                p.op("scalar", lambda e, sl=sl, bsb=bsb, hp=hp, dst=dst,
                     psel=psel: e.activation(
                    out=dst[:, sl], in_=psel[:], func=AF.Identity,
                    bias=bsb[:, hp:hp + 1]),
                    ins=[kpsel, bkey], outs=[dks + str(ts4)])
        prep_step(3, site=2)

        if layer == 0:
            for h2 in range(2):
                h = 2 * hp + h2
                dst = emask[:, h2 * WM:(h2 + 1) * WM]
                p.op("scalar", lambda e, dst=dst, h=h: e.activation(
                    out=dst, in_=mdiff[:], func=AF.Exp,
                    scale=ball[:, 2 * h: 2 * h + 1],
                    bias=ball[:, 32 + 2 * h: 32 + 2 * h + 1]),
                    ins=["mdiff", "ball"], outs=[f"emA{h2}"])
                for cs in range(0, WM, 1024):
                    cw = min(1024, WM - cs)
                    p.op("scalar", lambda e, h=h, cs=cs, cw=cw: e.activation(
                        out=etmp[:, 0:cw], in_=mdiff[:, cs:cs + cw], func=AF.Exp,
                        scale=ball[:, 2 * h + 1: 2 * h + 2],
                        bias=ball[:, 32 + 2 * h + 1: 32 + 2 * h + 2]),
                        ins=["mdiff", "ball", f"emsk{h2}:{cs}"], outs=["etmp"])
                    p.op("vector", lambda e, dst=dst, cs=cs, cw=cw: e.tensor_add(
                        dst[:, cs:cs + cw], dst[:, cs:cs + cw], etmp[:, 0:cw]),
                        ins=[f"emA{h2}", "etmp"], outs=[f"emask{h2}",
                                                        f"emsk{h2}:{cs}"])

        for qb in range(4):
            qsl = slice(qb * 512, (qb + 1) * 512)

            def scores(kc):
                par = kc % 2
                psx = psW[par]
                kA, kB = f"psW{par}h0", f"psW{par}h1"
                p.op("tensor", lambda e, kc=kc, psx=psx, qsl=qsl: e.matmul(
                    psx[:, 0:512], kp[0:64, kc * 128:(kc + 1) * 128], qp[0:64, qsl],
                    start=True, stop=True, tile_position=(0, 0)),
                    ins=qkeys + kkeys, outs=[kA])
                p.op("tensor", lambda e, kc=kc, psx=psx, qsl=qsl: e.matmul(
                    psx[:, 512:1024], kp[64:128, kc * 128:(kc + 1) * 128],
                    qp[64:128, qsl],
                    start=True, stop=True, tile_position=(64, 0)),
                    ins=qkeys + kkeys, outs=[kB])
                esl = slice(par * 1024, (par + 1) * 1024)
                ek = f"ebfp{par}"
                if layer == 0:
                    c0 = (T - 1) + qb * 512 - kc * 128
                    m3 = msk[:, esl].rearrange("p (h w) -> p h w", h=2)
                    p3 = psx[:].rearrange("p (h w) -> p h w", h=2)
                    e3 = emask[:].rearrange("p (h w) -> p h w", h=2)[:, :, c0:c0 + 512]
                    p.op("vector", lambda e, m3=m3, p3=p3, e3=e3: e.tensor_tensor(
                        m3, p3, e3, ALU.mult),
                        ins=[kA, kB, "emask0", "emask1"], outs=[f"mskp{par}"])
                    p.op("scalar", lambda e, esl=esl: e.activation(
                        out=ebf[:, esl], in_=msk[:, esl], func=AF.Exp),
                        ins=[f"mskp{par}"], outs=[ek])
                else:
                    p.op("scalar", lambda e, psx=psx, esl=esl: e.activation(
                        out=ebf[:, esl], in_=psx[:], func=AF.Exp,
                        scale=np.float32(0.125)),
                        ins=[kA, kB], outs=[ek])

            def pv(kc):
                par = kc % 2
                eo = par * 1024
                vb = kc * 1040 + hp * 130
                vkey = f"va{kc}:{hp // 4}"
                p.op("tensor", lambda e, kc=kc, eo=eo, vb=vb: e.matmul(
                    po0[0:65, :], v_all[:, vb: vb + 65],
                    ebf[:, eo: eo + 512],
                    start=(kc == 0), stop=(kc == NT - 1)),
                    ins=[f"ebfp{par}", vkey, "vo"], outs=["po0"])
                p.op("tensor", lambda e, kc=kc, eo=eo, vb=vb: e.matmul(
                    po1[0:65, :], v_all[:, vb + 65: vb + 130],
                    ebf[:, eo + 512: eo + 1024],
                    start=(kc == 0), stop=(kc == NT - 1)),
                    ins=[f"ebfp{par}", vkey, "vo"], outs=["po1"])

            for kc in range(NT + 1):
                if kc < NT:
                    scores(kc)
                if kc > 0:
                    pv(kc - 1)
            p.op("scalar", lambda e: e.activation(out=dsc[64:65, 0:512],
                                                  in_=po0[64:65, :], func=AF.Ln),
                 ins=["po0"], outs=["dscA"])
            p.op("scalar", lambda e: e.activation(out=dbf[64:65, 0:512],
                                                  in_=dsc[64:65, 0:512], func=AF.Exp,
                                                  scale=np.float32(-1.0)),
                 ins=["dscA"], outs=["dbfA"])
            p.op("scalar", lambda e, qb=qb: e.activation(out=asb[qb % 2][0:64, :],
                                                  in_=po0[0:64, :],
                                                  func=AF.Identity, bias=np.float32(0.0)),
                 ins=["po0"], outs=[f"asbA{qb % 2}"])
            p.op("tensor", lambda e: e.matmul(pq[0:64, :], ones[64:65, 0:64],
                                              dbf[64:65, 0:512],
                                              start=True, stop=True),
                 ins=["dbfA", "ones"], outs=["pq"])
            p.op("scalar", lambda e, qb=qb: e.activation(
                out=dnb[qb % 2][0:64, :], in_=pq[0:64, :],
                func=AF.Identity, bias=np.float32(0.0)),
                ins=["pq"], outs=[f"dnbA{qb % 2}"])
            p.op("vector", lambda e, hp=hp, qsl=qsl, qb=qb: e.tensor_mul(
                attn[0:64, hp * T: (hp + 1) * T][:, qsl], dnb[qb % 2][0:64, :],
                asb[qb % 2][0:64, :]),
                ins=[f"dnbA{qb % 2}", f"asbA{qb % 2}"], outs=[f"attn:{hp}:0:{qb}"])
            p.op("scalar", lambda e: e.activation(out=dsc[0:1, 0:512],
                                                  in_=po1[64:65, :], func=AF.Ln),
                 ins=["po1"], outs=["dscB"])
            p.op("scalar", lambda e: e.activation(out=dbf[0:1, 0:512],
                                                  in_=dsc[0:1, 0:512],
                                                  func=AF.Exp,
                                                  scale=np.float32(-1.0)),
                 ins=["dscB"], outs=["dbfB"])
            p.op("scalar", lambda e, qb=qb: e.activation(out=asb[qb % 2][64:128, :],
                                                  in_=po1[0:64, :],
                                                  func=AF.Identity, bias=np.float32(0.0)),
                 ins=["po1"], outs=[f"asbB{qb % 2}"])
            p.op("tensor", lambda e: e.matmul(pq2[64:128, :], ones[0:1, 0:64],
                                              dbf[0:1, 0:512],
                                              start=True, stop=True),
                 ins=["dbfB", "ones"], outs=["pq2"])
            p.op("scalar", lambda e, qb=qb: e.activation(
                out=dnb[qb % 2][64:128, :], in_=pq2[64:128, :],
                func=AF.Identity, bias=np.float32(0.0)),
                ins=["pq2"], outs=[f"dnbB{qb % 2}"])
            p.op("vector", lambda e, hp=hp, qsl=qsl, qb=qb: e.tensor_mul(
                attn[64:128, hp * T: (hp + 1) * T][:, qsl],
                dnb[qb % 2][64:128, :], asb[qb % 2][64:128, :]),
                ins=[f"dnbB{qb % 2}", f"asbB{qb % 2}"], outs=[f"attn:{hp}:1:{qb}"])
            prep_step(4)

    attn_all = [f"attn:{g}:{h2}:{qb}" for g in range(NHP) for h2 in range(2)
                for qb in range(4)]
    p.barrier()
    sc_mask.close()
    wo = p.sb("wo", [128, NC * D], BF16)
    xtw = [p.sb(f"xtw{i}", [128, D], F32) for i in range(2)]
    p.dma("gpsimd", "c_wo", lambda e: e.dma_start(
        out=wo[:].rearrange("p (c f) -> p c f", c=NC),
        in_=woT_d.ap().rearrange("(c p) f -> p c f", p=128)),
        outs=["wo"])
    if layer == 1:
        gb_b = p.sb("gb_b", [128, 2 * D], F32)
        u32 = [p.sb("u32_s", [128, D], F32)] * 2
        ysb = p.sb("ysb", [128, D], F32)
        p.dma("sync", "c_gb", lambda e: e.dma_start(out=gb_b[0:1, :],
                                                    in_=gb_d.ap()),
              outs=["gbrow"])
        for half in range(2):
            for fs in range(0, D, 512):
                o = half * D + fs
                p.op("tensor", lambda e, o=o: e.matmul(
                    pq[:, 0:512], ones32[0:1, :], gb_b[0:1, o:o + 512],
                    start=True, stop=True),
                    ins=["gbrow", "ones32"], outs=["pq"])
                p.op("scalar", lambda e, o=o: e.activation(
                    out=gb_b[:, o:o + 512], in_=pq[:, 0:512],
                    func=AF.Identity, bias=np.float32(0.0)),
                    ins=["pq"], outs=["gb_b"])
    if "dbg_attn" in dram:
        p.dma("sync", "dbg1", lambda e: e.dma_start(out=dram["dbg_attn"].ap(),
                                                    in_=attn[:]), ins=attn_all,
              outs=["dbgA"])
        p.dma("sync", "dbg2", lambda e: e.dma_start(out=dram["dbg_qp"].ap(),
                                                    in_=qp[:]),
              ins=[f"qp{i}" for i in range(4)], outs=["dbgQ"])
        p.dma("sync", "dbg3", lambda e: e.dma_start(out=dram["dbg_kp"].ap(),
                                                    in_=kp[:]),
              ins=[f"kp{i}" for i in range(4)], outs=["dbgK"])
        p.dma("sync", "dbg4", lambda e: e.dma_start(out=dram["dbg_vaug"].ap(),
                                                    in_=v_all[:, 0:NT * 130]),
              outs=["dbgV"])

    for t in range(NT):
        bi = t % 2
        p.dma("sync", f"x{bi}", lambda e, t=t, bi=bi: e.dma_start(
            out=xtw[bi][:], in_=x_d.ap()[t * 128:(t + 1) * 128, :]),
            outs=[f"xt{bi}"])
        for fs in range(2):
            fsl = slice(fs * 512, (fs + 1) * 512)
            psx, kpsx = (psW[0][:, 0:512], "psW0h0") if fs == 0 else (
                psW[0][:, 512:1024], "psW0h1")

            def mwo(e, t=t, fsl=fsl, psx=psx):
                for g in range(NHP):
                    e.matmul(psx,
                             attn[:, g * T + t * 128: g * T + (t + 1) * 128],
                             wo[:, g * D:(g + 1) * D][:, fsl],
                             start=(g == 0), stop=False)
                return e.matmul(psx, ones[0:1, :], bor[0:1, fsl],
                                start=False, stop=True)
            p.op("tensor", mwo, ins=attn_all + ["wo", "bor", "ones"], outs=[kpsx])
        if layer == 0:
            for fs in range(2):
                fsl = slice(fs * 512, (fs + 1) * 512)
                psx, kpsx = (psW[0][:, 0:512], "psW0h0") if fs == 0 else (
                    psW[0][:, 512:1024], "psW0h1")
                p.op("vector", lambda e, bi=bi, fsl=fsl, psx=psx: e.tensor_add(
                    xtw[bi][:, fsl], psx, xtw[bi][:, fsl]),
                    ins=[kpsx, f"xt{bi}"], outs=[f"xt{bi}"])
        else:
            ln_stats_and_u(p, xtw[bi][:], f"xt{bi}", sq_scr[:], "sq_scr",
                           stats2, "wt", bi, u32[bi][:], "u32s")
            for fs in range(2):
                fsl = slice(fs * 512, (fs + 1) * 512)
                psx, kpsx = (psW[0][:, 0:512], "psW0h0") if fs == 0 else (
                    psW[0][:, 512:1024], "psW0h1")
                p.op("vector", lambda e, bi=bi, fsl=fsl, psx=psx: e.tensor_add(
                    ysb[:, fsl], psx, u32[bi][:, fsl]),
                    ins=[kpsx, "u32s"], outs=["ysb"])
            ln_stats_and_u(p, ysb[:], "ysb", sq_scr[:], "sq_scr",
                           stats3, "yt", bi, ysb[:], "ysb")
            p.op("vector", lambda e: e.tensor_mul(ysb[:], ysb[:], gb_b[:, 0:D]),
                 ins=["ysb", "gb_b"], outs=["ysb"])
            p.op("vector", lambda e: e.tensor_add(ysb[:], ysb[:], gb_b[:, D:2 * D]),
                 ins=["ysb", "gb_b"], outs=["ysb"])
            p.op("vector", lambda e, bi=bi: e.tensor_add(xtw[bi][:], ysb[:],
                                                         xtw[bi][:]),
                 ins=["ysb", f"xt{bi}"], outs=[f"xt{bi}"])
        p.dma("sync", "st", lambda e, t=t, bi=bi: e.dma_start(
            out=y_d.ap()[t * 128:(t + 1) * 128, :], in_=xtw[bi][:]),
            ins=[f"xt{bi}"], outs=[f"y{t}"])


def build_attn(layer):
    nc = bass.Bass("TRN2", target_bir_lowering=False)
    ctx = ExitStack()
    x_d = nc.dram_tensor("x", [T, D], F32, kind="ExternalInput")
    y_d = nc.dram_tensor("y", [T, D], F32, kind="ExternalOutput")
    eye_d = nc.inline_tensor(np.eye(128, dtype=BFNP), name="eye_bf")
    dram = declare_attn_dram(nc, layer, "", x_d, y_d, eye_d)
    import os as _os
    if _os.environ.get("DBG_ATTN") == "1":
        dram["dbg_attn"] = nc.dram_tensor("dbg_attn", [128, NHP * T], BF16,
                                          kind="ExternalOutput")
        dram["dbg_qp"] = nc.dram_tensor("dbg_qp", [128, T], BF16,
                                        kind="ExternalOutput")
        dram["dbg_kp"] = nc.dram_tensor("dbg_kp", [128, T], BF16,
                                        kind="ExternalOutput")
        dram["dbg_vaug"] = nc.dram_tensor("dbg_vaug", [128, NT * 130], BF16,
                                          kind="ExternalOutput")
    with ctx:
        p = Prog(nc, ctx)
        emit_attn(p, dram, layer)
        p.barrier()
        p.emit()
    return nc



def declare_ffn0_dram(nc, pre, x_d, y_d, eye_d):
    return {"x": x_d, "y": y_d, "eye": eye_d,
            "w": nc.dram_tensor(pre + "w", [D, D], F32, kind="ExternalInput"),
            "b_row": nc.dram_tensor(pre + "b_row", [1, D], F32, kind="ExternalInput"),
            "stg": nc.dram_tensor(pre + "stg", [D, D], BF16, kind="Internal")}


def emit_ffn0(p, dram, prep_done=False):
    nc = p.nc
    x_d, y_d, eye_d = dram["x"], dram["y"], dram["eye"]
    b_d = dram["b_row"]
    stg_d = dram["stg"]
    p.mh = p.sb("mh", [128, 1], F32)
    p.op("gpsimd", lambda e: e.memset(p.mh[:], -0.5), outs=["mh"])
    if not prep_done:
        _oact = p.actx
        with ExitStack() as sw:
            p.actx = sw
            scrW, wchW, wqbW = alloc_prep_bufs(p)
            p.actx = _oact
            for _ in gen_wprep(p, dram["w"], stg_d, scrW, wchW, wqbW, 6, NC, D,
                               1024):
                pass
            p.barrier()
    wqT = p.sb("wqT", [128, NC * D], BF16)
    br = p.sb("br", [1, D], BF16)
    eye = p.sb("eye", [128, 128], BF16)
    ones = p.sb("ones", [1, 128], BF16)
    stats = p.sb("stats", [128, 4 * 2], F32)
    xt = [p.sb(f"xt{i}", [128, D], F32) for i in range(2)]
    xtw = [p.sb(f"xtw{i}", [128, D], F32) for i in range(2)]
    u32 = [p.sb(f"u32_{i}", [128, D], F32) for i in range(2)]
    xqb = [p.sb(f"xqb{i}", [128, D], BF16) for i in range(2)]
    xqf = [p.sb(f"xqf{i}", [128, D], BF16) for i in range(4)]
    sq_scr = p.sb("sq_scr", [128, D], F32)
    scr = {k: p.sb("qs_" + k, [128, D], F32) for k in ("y", "u", "a", "b", "s")}
    scr["bm"] = p.sb("qs_bm", [128, 3 * 32], F32)
    ptr = [p.ps(f"ptr{i}", [128, 128], BF16) for i in range(2)]
    pd = [p.ps(f"pd{i}", [128, 512], F32) for i in range(4)]

    p.dma("sync", "c_eye", lambda e: e.dma_start(out=eye[:], in_=eye_d.ap()),
          outs=["eye"])
    p.dma("gpsimd", "c_b", lambda e: e.dma_start(out=br[:], in_=b_d.ap()),
          outs=["br"])
    p.op("vector", lambda e: e.memset(ones[:], 1.0), outs=["ones"])

    load_wT_staged(p, stg_d, wqT, "wqT", NC, D)
    wqT_all = [f"wqT:{ec}" for ec in range(NC)]

    def front(t):
        bi, ri = t % 2, t % 4
        p.dma("sync", f"x{bi}", lambda e, t=t, bi=bi: e.dma_start(
            out=xt[bi][:], in_=x_d.ap()[t * 128:(t + 1) * 128, :]),
            outs=[f"xt{bi}"])
        ln_stats_and_u(p, xt[bi][:], f"xt{bi}", sq_scr[:], "sq_scr",
                       stats, "st", bi, u32[bi][:], "u32s", rstd_via="pool")
        emit_quant_fp6(p, u32[bi][:], "u32s", xqb[bi][:], "xqbs",
                       scr, "qs", 32)
        for ec in range(NC):
            pk = "ptr" + str(ec % 2)
            p.op("tensor", lambda e, ec=ec, bi=bi: e.transpose(
                ptr[ec % 2][:], xqb[bi][:, ec * 128:(ec + 1) * 128], eye[:]),
                ins=["xqbs", "eye"], outs=[pk])
            p.op("scalar", lambda e, ec=ec, ri=ri: e.activation(
                out=xqf[ri][:, ec * 128:(ec + 1) * 128],
                in_=ptr[ec % 2][:], func=AF.Identity, bias=np.float32(0.0)),
                ins=[pk], outs=[f"xqf{ri}_{ec}"])

    def back_mm(t):
        bi, ri = t % 2, t % 4
        xqf_all = [f"xqf{ri}_{ec}" for ec in range(NC)]
        p.dma("gpsimd", f"xw{bi}", lambda e, t=t, bi=bi: e.dma_start(
            out=xtw[bi][:], in_=x_d.ap()[t * 128:(t + 1) * 128, :]),
            outs=[f"xtw{bi}"])
        for fs in range(2):
            fsl = slice(fs * 512, (fs + 1) * 512)
            psx, kpsx = pd[(t % 2) * 2 + fs], f"pd{(t % 2) * 2 + fs}"

            def mmd(e, fsl=fsl, psx=psx, ri=ri):
                for ec in range(NC):
                    e.matmul(psx[:], xqf[ri][:, ec * 128:(ec + 1) * 128],
                             wqT[:, ec * D:(ec + 1) * D][:, fsl],
                             start=(ec == 0), stop=False)
                return e.matmul(psx[:], ones[0:1, :], br[0:1, fsl],
                                start=False, stop=True)
            p.op("tensor", mmd, ins=xqf_all + wqT_all + ["br", "ones"], outs=[kpsx])

    def back_fin(t):
        bi = t % 2
        for fs in range(2):
            fsl = slice(fs * 512, (fs + 1) * 512)
            psx, kpsx = pd[(t % 2) * 2 + fs], f"pd{(t % 2) * 2 + fs}"
            p.op("vector", lambda e, bi=bi, fsl=fsl, psx=psx: e.tensor_add(
                xtw[bi][:, fsl], psx[:], xtw[bi][:, fsl]),
                ins=[kpsx, f"xtw{bi}"], outs=[f"xtw{bi}"])
        p.dma("sync", "st", lambda e, t=t, bi=bi: e.dma_start(
            out=y_d.ap()[t * 128:(t + 1) * 128, :], in_=xtw[bi][:]),
            ins=[f"xtw{bi}"], outs=[f"y{t}"])

    front(0)
    front(1)
    for t in range(NT):
        back_mm(t)
        if t + 2 < NT:
            front(t + 2)
        back_fin(t)


def build_ffn0():
    nc = bass.Bass("TRN2", target_bir_lowering=False)
    ctx = ExitStack()
    x_d = nc.dram_tensor("x", [T, D], F32, kind="ExternalInput")
    y_d = nc.dram_tensor("y", [T, D], F32, kind="ExternalOutput")
    eye_d = nc.inline_tensor(np.eye(128, dtype=BFNP), name="eye_bf")
    dram = declare_ffn0_dram(nc, "", x_d, y_d, eye_d)
    with ctx:
        p = Prog(nc, ctx)
        emit_ffn0(p, dram)
        p.barrier()
        p.emit()
    return nc



NI = 22
I_DIM = 2816


def declare_ffn1_dram(nc, pre, x_d, y_d, eye_d):
    return {"x": x_d, "y": y_d, "eye": eye_d,
            "wg": nc.dram_tensor(pre + "wg", [I_DIM, D], F32, kind="ExternalInput"),
            "wu": nc.dram_tensor(pre + "wu", [I_DIM, D], F32, kind="ExternalInput"),
            "wd": nc.dram_tensor(pre + "wd", [D, I_DIM], F32, kind="ExternalInput"),
            "stg_g": nc.dram_tensor(pre + "stg_g", [I_DIM, D], BF16, kind="Internal"),
            "stg_u": nc.dram_tensor(pre + "stg_u", [I_DIM, D], BF16, kind="Internal"),
            "stg_dn": nc.dram_tensor(pre + "stg_dn", [D, I_DIM], BF16, kind="Internal")}


def emit_ffn1(p, dram, prep_done=False):
    nc = p.nc
    x_d, y_d, eye_d = dram["x"], dram["y"], dram["eye"]
    stg_g, stg_u, stg_dn = dram["stg_g"], dram["stg_u"], dram["stg_dn"]
    HF = I_DIM // 2
    p.mh = p.sb("mh", [128, 1], F32)
    p.op("gpsimd", lambda e: e.memset(p.mh[:], -0.5), outs=["mh"])
    if not prep_done:
        _oact = p.actx
        with ExitStack() as sw:
            p.actx = sw
            scrW, wchW, wqbW = alloc_prep_bufs(p)
            p.actx = _oact
            for g in f1_prep_gens(p, dram, scrW, wchW, wqbW):
                for _ in g:
                    pass
            p.barrier()
    wgT = p.sb("wgT", [128, NC * I_DIM], BF16)
    wuT = p.sb("wuT", [128, NC * I_DIM], BF16)
    wdT = p.sb("wdT", [128, NI * D], BF16)
    eye = p.sb("eye", [128, 128], BF16)
    stats = p.sb("stats", [128, 4 * 2], F32)
    xt = [p.sb(f"xt{i}", [128, D], F32) for i in range(3)]
    u32 = [p.sb("u32_s", [128, D], F32)] * 2
    xqb = [p.sb(f"xqb{i}", [128, D], BF16) for i in range(2)]
    xqf = [p.sb(f"xqf{i}", [128, D], BF16) for i in range(3)]
    gs = [p.sb(f"gs{i}", [128, 512], F32) for i in range(2)]
    upe = p.sb("upe", [128, 2 * 512], F32)
    hq = [p.sb(f"hq{i}", [128, I_DIM], BF16) for i in range(2)]
    hqT = p.sb("hqT", [128, I_DIM], BF16)
    scr = {k: p.sb("qs_" + k, [128, 512], F32) for k in ("y", "u", "a", "b")}
    scr["bm"] = p.sb("qs_bm", [128, 3 * 32], F32)
    ptr = [p.ps(f"ptr{i}", [128, 128], BF16) for i in range(2)]
    pg = [p.ps("pg0", [128, 512], F32), p.ps("pg1", [128, 512], F32)]
    pu = [p.ps("pu0", [128, 512], F32), p.ps("pu1", [128, 512], F32)]
    pd0 = p.ps("pd0", [128, 512], F32)
    pd1 = p.ps("pd1", [128, 512], F32)

    p.dma("sync", "c_eye", lambda e: e.dma_start(out=eye[:], in_=eye_d.ap()),
          outs=["eye"])
    load_wT_staged(p, stg_g, wgT, "wgT", NC, I_DIM)
    load_wT_staged(p, stg_u, wuT, "wuT", NC, I_DIM)
    load_wT_staged(p, stg_dn, wdT, "wdT", NI, D)
    wgT_all = [f"wgT:{ec}" for ec in range(NC)]
    wuT_all = [f"wuT:{ec}" for ec in range(NC)]
    wdT_all = [f"wdT:{ic}" for ic in range(NI)]

    ISL = [(k * 512, min((k + 1) * 512, I_DIM)) for k in range(6)]

    def front(t):
        bi, ri = t % 2, t % 3
        p.dma("sync", f"x{ri}", lambda e, t=t, ri=ri: e.dma_start(
            out=xt[ri][:], in_=x_d.ap()[t * 128:(t + 1) * 128, :]),
            outs=[f"xt{ri}"])
        ln_stats_and_u(p, xt[ri][:], f"xt{ri}", u32[bi][:], "u32s",
                       stats, "st", bi, u32[bi][:], "u32s", rstd_via="pool")
        for h in range(2):
            emit_quant_fp4(p, u32[bi][:, h * 512:(h + 1) * 512], "u32s",
                           xqb[bi][:, h * 512:(h + 1) * 512], f"xqb{bi}",
                           scr, "qs", 32)
        for ec in range(NC):
            pk = "ptr" + str(ec % 2)
            p.op("tensor", lambda e, ec=ec, bi=bi: e.transpose(
                ptr[ec % 2][:], xqb[bi][:, ec * 128:(ec + 1) * 128], eye[:]),
                ins=[f"xqb{bi}", "eye"], outs=[pk])
            p.op("scalar", lambda e, ec=ec, ri=ri: e.activation(
                out=xqf[ri][:, ec * 128:(ec + 1) * 128],
                in_=ptr[ec % 2][:], func=AF.Identity, bias=np.float32(0.0)),
                ins=[pk], outs=[f"xqf{ri}_{ec}"])

    def back_mid(t):
        bi, ri = t % 2, t % 3
        xqf_all = [f"xqf{ri}_{ec}" for ec in range(NC)]
        for k, (i0, i1) in enumerate(ISL):
            isl = slice(i0, i1)
            pgk, kpg = pg[k % 2], f"pg{k % 2}"
            puk, kpu = pu[k % 2], f"pu{k % 2}"

            def mmg(e, isl=isl, i0=i0, i1=i1, ri=ri, pgk=pgk):
                for ec in range(NC):
                    inst = e.matmul(pgk[:, 0:i1 - i0],
                                    xqf[ri][:, ec * 128:(ec + 1) * 128],
                                    wgT[:, ec * I_DIM:(ec + 1) * I_DIM][:, isl],
                                    start=(ec == 0), stop=(ec == NC - 1))
                return inst
            p.op("tensor", mmg, ins=xqf_all + wgT_all, outs=[kpg])
            p.op("scalar", lambda e, i0=i0, i1=i1, k=k,
                 pgk=pgk: e.activation(
                out=gs[k % 2][:, 0:i1 - i0], in_=pgk[:, 0:i1 - i0], func=AF.Silu),
                ins=[kpg], outs=[f"gsb{k % 2}"])

            def mmu(e, isl=isl, i0=i0, i1=i1, ri=ri, puk=puk):
                for ec in range(NC):
                    inst = e.matmul(puk[:, 0:i1 - i0],
                                    xqf[ri][:, ec * 128:(ec + 1) * 128],
                                    wuT[:, ec * I_DIM:(ec + 1) * I_DIM][:, isl],
                                    start=(ec == 0), stop=(ec == NC - 1))
                return inst
            p.op("tensor", mmu, ins=xqf_all + wuT_all, outs=[kpu])
            usl = slice((k % 2) * 512, (k % 2) * 512 + (i1 - i0))
            p.op("scalar", lambda e, usl=usl, i0=i0, i1=i1, puk=puk: e.activation(
                out=upe[:, usl], in_=puk[:, 0:i1 - i0], func=AF.Identity,
                bias=np.float32(0.0)),
                ins=[kpu], outs=[f"upe{k % 2}"])
            p.op("gpsimd", lambda e, usl=usl, k=k, i0=i0, i1=i1: e.tensor_tensor(
                gs[k % 2][:, 0:i1 - i0], upe[:, usl], gs[k % 2][:, 0:i1 - i0],
                ALU.mult),
                ins=[f"upe{k % 2}", f"gsb{k % 2}"], outs=[f"gsb{k % 2}"])
            emit_quant_fp4(p, gs[k % 2][:, 0:i1 - i0], [f"gsb{k % 2}"],
                           hq[bi][:, isl], f"hq{bi}s_{k}", scr, "qs",
                           (i1 - i0) // 16)

    def back_down(t):
        bi, ri = t % 2, t % 3
        for ic in range(NI):
            pk = "ptr" + str(ic % 2)
            hqk = f"hq{bi}s_{min(ic // 4, 5)}"
            p.op("tensor", lambda e, ic=ic, bi=bi: e.transpose(
                ptr[ic % 2][:], hq[bi][:, ic * 128:(ic + 1) * 128], eye[:]),
                ins=[hqk, "eye"], outs=[pk])
            p.op("scalar", lambda e, ic=ic: e.activation(
                out=hqT[:, ic * 128:(ic + 1) * 128],
                in_=ptr[ic % 2][:], func=AF.Identity, bias=np.float32(0.0)),
                ins=[pk], outs=[f"hqT{ic}"])
        hqT_all = [f"hqT{ic}" for ic in range(NI)]
        for fs in range(2):
            fsl = slice(fs * 512, (fs + 1) * 512)
            psx, kpsx = (pd0, "pd0") if fs == 0 else (pd1, "pd1")

            def mmd(e, fsl=fsl, psx=psx):
                for ic in range(NI):
                    inst = e.matmul(psx[:], hqT[:, ic * 128:(ic + 1) * 128],
                                    wdT[:, ic * D:(ic + 1) * D][:, fsl],
                                    start=(ic == 0), stop=(ic == NI - 1))
                return inst
            p.op("tensor", mmd, ins=hqT_all + wdT_all, outs=[kpsx])
            p.op("vector", lambda e, ri=ri, fsl=fsl, psx=psx: e.tensor_add(
                xt[ri][:, fsl], psx[:], xt[ri][:, fsl]),
                ins=[kpsx, f"xt{ri}"], outs=[f"xt{ri}"])
        p.dma("sync", "st", lambda e, t=t, ri=ri: e.dma_start(
            out=y_d.ap()[t * 128:(t + 1) * 128, :], in_=xt[ri][:]),
            ins=[f"xt{ri}"], outs=[f"y{t}"])

    front(0)
    front(1)
    back_mid(0)
    for t in range(NT):
        if t + 2 < NT:
            front(t + 2)
        if t + 1 < NT:
            back_mid(t + 1)
        back_down(t)


def build_ffn1():
    nc = bass.Bass("TRN2", target_bir_lowering=False)
    ctx = ExitStack()
    x_d = nc.dram_tensor("x", [T, D], F32, kind="ExternalInput")
    y_d = nc.dram_tensor("y", [T, D], F32, kind="ExternalOutput")
    eye_d = nc.inline_tensor(np.eye(128, dtype=BFNP), name="eye_bf")
    dram = declare_ffn1_dram(nc, "", x_d, y_d, eye_d)
    with ctx:
        p = Prog(nc, ctx)
        emit_ffn1(p, dram)
        p.barrier()
        p.emit()
    return nc



def _mdiff_np():
    c = np.arange(WM, dtype=np.float32)[None, :]
    i = np.arange(128, dtype=np.float32)[:, None]
    return np.ascontiguousarray(np.abs(c - (T - 1) - i).astype(np.float32))


def _attn_host_inputs(Wq, bq, Wk, bk, Wv, bv, Wo, bo):
    d = {}
    d["wqT"] = np.ascontiguousarray(Wq.T)
    d["wkT"] = np.ascontiguousarray(Wk.T)
    d["wvT"] = np.ascontiguousarray(Wv.T)
    d["woT"] = np.ascontiguousarray(Wo.T)
    d["bq_pc"] = np.ascontiguousarray(bq.reshape(NC, 128).T)
    d["bk_pc"] = np.ascontiguousarray(bk.reshape(NC, 128).T)
    d["bv_row"] = np.ascontiguousarray(bv.reshape(1, D))
    d["bo_row"] = np.ascontiguousarray(bo.reshape(1, D))
    return d


_progs = {}


def _get_prog(name, builder):
    if name not in _progs:
        _progs[name] = builder()
    return _progs[name]


def build_fused():
    nc = bass.Bass("TRN2", target_bir_lowering=False)
    ctx = ExitStack()
    x0 = nc.dram_tensor("x", [T, D], F32, kind="ExternalInput")
    y = nc.dram_tensor("y", [T, D], F32, kind="ExternalOutput")
    x1 = nc.dram_tensor("xs1", [T, D], F32, kind="Internal")
    x2 = nc.dram_tensor("xs2", [T, D], F32, kind="Internal")
    x3 = nc.dram_tensor("xs3", [T, D], F32, kind="Internal")
    eye_d = nc.inline_tensor(np.eye(128, dtype=BFNP), name="eye_bf")
    dA = declare_attn_dram(nc, 0, "a0_", x0, x1, eye_d)
    dB = declare_ffn0_dram(nc, "f0_", x1, x2, eye_d)
    dC = declare_attn_dram(nc, 1, "a1_", x2, x3, eye_d)
    dD = declare_ffn1_dram(nc, "f1_", x3, y, eye_d)
    with ctx:
        p = Prog(nc, ctx)
        with ExitStack() as prepctx0:
            p.pre, p.actx = "f0p_", prepctx0
            bufs0 = alloc_prep_bufs(p, slab=256, nm="p0_")
            gen0 = gen_wprep(p, dB["w"], dB["stg"], *bufs0, 6, NC, D, 256,
                             tag="f0_")
            with ExitStack() as actx:
                p.pre, p.actx = "a0_", actx
                emit_attn(p, dA, 0, prep2=gen0)
                for _ in gen0:
                    pass
                p.barrier()
        with ExitStack() as actx:
            p.pre, p.actx = "f0_", actx
            emit_ffn0(p, dB, prep_done=True)
            p.barrier()
        with ExitStack() as prepctx:
            p.pre, p.actx = "f1_", prepctx
            scrW, wchW, wqbW = alloc_prep_bufs(p, slab=512)
            gen = itertools.chain(*f1_prep_gens(p, dD, scrW, wchW, wqbW,
                                                slab_gu=512, slab_d=352))
            with ExitStack() as actx:
                p.pre, p.actx = "a1_", actx
                emit_attn(p, dC, 1, prep=gen)
                for _ in gen:
                    pass
                p.barrier()
        with ExitStack() as actx:
            p.pre, p.actx = "f1_", actx
            emit_ffn1(p, dD, prep_done=True)
            p.barrier()
        p.pre, p.actx = "", ctx
        p.emit()
    return nc


def kernel(L0_attn_x, L0_attn_Wq, L0_attn_bq, L0_attn_Wk, L0_attn_bk,
           L0_attn_Wv, L0_attn_bv, L0_attn_Wo, L0_attn_bo,
           L0_attn_decay_params, L0_attn_mix_weights,
           L0_ffn_weight, L0_ffn_bias,
           L1_attn_Wq, L1_attn_bq, L1_attn_Wk, L1_attn_bk, L1_attn_Wv,
           L1_attn_bv, L1_attn_Wo, L1_attn_bo, L1_attn_ln_gamma,
           L1_attn_ln_beta, L1_ffn_w_gate, L1_ffn_w_up, L1_ffn_w_down):
    ncore = 8
    f32 = np.float32
    x = np.ascontiguousarray(np.asarray(L0_attn_x, dtype=f32))
    cores = list(range(ncore))

    ins = {}
    for pre, args in (
            ("a0_", (L0_attn_Wq, L0_attn_bq, L0_attn_Wk, L0_attn_bk,
                     L0_attn_Wv, L0_attn_bv, L0_attn_Wo, L0_attn_bo)),
            ("a1_", (L1_attn_Wq, L1_attn_bq, L1_attn_Wk, L1_attn_bk,
                     L1_attn_Wv, L1_attn_bv, L1_attn_Wo, L1_attn_bo))):
        h = _attn_host_inputs(*[np.asarray(a, f32) for a in args])
        ins.update({pre + k: v for k, v in h.items()})
    ins["a0_dpmix"] = np.ascontiguousarray(np.concatenate(
        [np.asarray(L0_attn_decay_params, f32).reshape(1, 32),
         np.asarray(L0_attn_mix_weights, f32).reshape(1, 32)], axis=1))
    ins["a0_mdiff"] = _mdiff_np()
    ins["a1_gb"] = np.ascontiguousarray(np.concatenate(
        [np.asarray(L1_attn_ln_gamma, f32),
         np.asarray(L1_attn_ln_beta, f32)]).reshape(1, 2 * D))
    ins["f0_w"] = np.ascontiguousarray(np.asarray(L0_ffn_weight, f32))
    ins["f0_b_row"] = np.ascontiguousarray(np.asarray(L0_ffn_bias, f32).reshape(1, D))
    ins["f1_wg"] = np.ascontiguousarray(np.asarray(L1_ffn_w_gate, f32))
    ins["f1_wu"] = np.ascontiguousarray(np.asarray(L1_ffn_w_up, f32))
    ins["f1_wd"] = np.ascontiguousarray(np.asarray(L1_ffn_w_down, f32))

    ncF = _get_prog("fused", build_fused)
    res = run_bass_kernel_spmd(
        ncF, [dict(ins, x=np.ascontiguousarray(x[b])) for b in cores],
        core_ids=cores)
    return np.stack([np.asarray(res.results[b]["y"]) for b in cores],
                    axis=0).astype(np.float32)
